# version 02
# baseline (speedup 1.0000x reference)
import numpy as np
import ml_dtypes
from contextlib import ExitStack

import concourse.bass as bass
import concourse.mybir as mybir
from concourse.bass_utils import run_bass_kernel_spmd

F32 = mybir.dt.float32
BF16 = mybir.dt.bfloat16
AF = mybir.ActivationFunctionType
ALU = mybir.AluOpType
AX = mybir.AxisListType

BF = ml_dtypes.bfloat16

S = 8192
DM = 2048
NC = 8
SC = S // NC
H2 = 8
DH = 128
LAMBDA_INIT = 0.8 - 0.6 * np.exp(-0.3 * 1.0)
SCALE = DH ** -0.5
INTER0 = 5632
MH = 32
MP = 64
MN = 128
MK = 4
M_INTER = 2048
CONV_DIM = M_INTER + 2 * MN
PROJ = M_INTER + CONV_DIM + MH
CHUNK = 128
NCHUNK = S // CHUNK
FF1 = 8192
EPS = 1e-5


class Ser:

    def __init__(self):
        self.ops = []

    def op(self, eng, fn, inc=1):
        self.ops.append((eng, fn, inc))

    def t(self, fn):
        self.op("tensor", fn, 1)

    def v(self, fn):
        self.op("vector", fn, 1)

    def s(self, fn):
        self.op("scalar", fn, 1)

    def g(self, fn):
        self.op("gpsimd", fn, 1)

    def dma(self, fn, eng="sync"):
        self.op(eng, fn, 16)

    def flush(self, nc, tok, final=False):
        if not self.ops and not final:
            return getattr(self, "base", 0)
        base = getattr(self, "base", 0)
        pre = []
        c = base
        for (_, _, inc) in self.ops:
            pre.append(c)
            c += inc
        total = c
        with nc.Block() as block:
            for ename in ["sync", "tensor", "vector", "scalar", "gpsimd"]:
                ops = [(i, fn, inc) for i, (e, fn, inc) in enumerate(self.ops)
                       if e == ename]
                if not ops and not (final and ename == "gpsimd"):
                    continue

                def make(ops, ename):
                    def body(eng):
                        for i, fn, inc in ops:
                            if pre[i] > 0:
                                eng.wait_ge(tok, pre[i])
                            inst = fn(eng)
                            inst.then_inc(tok, inc)
                        if final and ename == "gpsimd":
                            eng.wait_ge(tok, total)
                    return body

                getattr(block, ename)(make(ops, ename))
        self.base = total
        self.ops = []
        return total

    def build(self, nc, tok):
        return self.flush(nc, tok, final=True)


def dap(h, off, dims):
    return bass.AP(h, off, [list(d) for d in dims])


def bf(a):
    return np.ascontiguousarray(np.asarray(a).astype(BF))


def f32(a):
    return np.ascontiguousarray(np.asarray(a).astype(np.float32))



def emit_layernorm(ser, x_f32, xn_bf, scr_f32, m_col, v_col):
    W = 2048
    ser.v(lambda e, a=x_f32, o=m_col: e.reduce_sum(o[:, :], a[:, :], AX.X))
    ser.v(lambda e, o=m_col: e.tensor_scalar_mul(o[:, :], o[:, :], -1.0 / W))
    ser.v(lambda e, a=x_f32, o=scr_f32, m=m_col:
          e.tensor_scalar_add(o[:, :], a[:, :], m[:, :]))
    ser.s(lambda e, a=scr_f32, o=x_f32, acc=v_col:
          e.activation(o[:, :], a[:, :], AF.Square, accum_out=acc[:, :]))
    ser.v(lambda e, o=v_col: e.tensor_scalar(o[:, :], o[:, :], 1.0 / W, EPS,
                                             ALU.mult, ALU.add))
    ser.v(lambda e, o=v_col: e.reciprocal(o[:, :], o[:, :]))
    ser.s(lambda e, o=v_col: e.activation(o[:, :], o[:, :], AF.Sqrt))
    ser.v(lambda e, a=scr_f32, o=xn_bf, r=v_col:
          e.tensor_scalar_mul(o[:, :], a[:, :], r[:, :]))



def build_A():
    nc = bass.Bass("TRN2", target_bir_lowering=False)
    x0 = nc.dram_tensor("x0", [SC, DM], F32, kind="ExternalInput")
    wqkT = nc.dram_tensor("wqkT", [DM, 3072], BF16, kind="ExternalInput")
    wvT = nc.dram_tensor("wvT", [DM, 1024], BF16, kind="ExternalInput")
    bqk = nc.dram_tensor("bqk", [3072], F32, kind="ExternalInput")
    bv = nc.dram_tensor("bv", [1024], F32, kind="ExternalInput")
    identb = nc.dram_tensor("identb", [128, 128], BF16, kind="ExternalInput")
    o_xnT = nc.dram_tensor("xnT", [DM, SC], BF16, kind="ExternalOutput")
    o_qkT = nc.dram_tensor("qkT", [3072, SC], BF16, kind="ExternalOutput")
    o_v = nc.dram_tensor("v", [SC, 1024], BF16, kind="ExternalOutput")

    ser = Ser()
    NT = SC // 128
    with ExitStack() as st:
        def sb(name, shape, dtype):
            return st.enter_context(nc.sbuf_tensor(name, shape, dtype))

        def ps(name, shape, dtype):
            return st.enter_context(nc.psum_tensor(name, shape, dtype))

        xt = sb("xt", [128, DM], F32)
        scr = sb("scr", [128, DM], F32)
        xn_bf = sb("xn_bf", [128, NT * DM], BF16)
        xnT = sb("xnT_sb", [128, 16 * SC], BF16)
        wv_sb = sb("wv_sb", [128, 16 * 1024], BF16)
        wch = sb("wch", [128, 2048], BF16)
        bcol = sb("bcol", [128, 1], F32)
        mcol = sb("mcol", [128, 1], F32)
        vcol = sb("vcol", [128, 1], F32)
        ident = sb("ident", [128, 128], BF16)
        ones_row = sb("ones_row", [1, 128], F32)
        bvrow = sb("bvrow", [1, 1024], F32)
        bvb = sb("bvb", [128, 1024], F32)
        qk_out = sb("qk_out", [128, 1024], BF16)
        v_out = sb("v_out", [128, 1024], BF16)
        pmm = ps("pmm", [128, 512], F32)
        ptr = ps("ptr", [128, 128], BF16)
        pbc = ps("pbc", [128, 512], F32)
        tok = st.enter_context(nc.semaphore("tok"))

        ser.dma(lambda e: e.dma_start(ident[:, :], identb.ap()))
        ser.v(lambda e: e.memset(ones_row[:, :], 1.0))
        ser.dma(lambda e: e.dma_start(bvrow[:, :], dap(bv, 0, [[1024, 1], [1, 1024]])))
        for half in range(2):
            ser.t(lambda e, h=half: e.matmul(pbc[:, :], ones_row[:, :],
                                             bvrow[:, h * 512:(h + 1) * 512],
                                             start=True, stop=True))
            ser.v(lambda e, h=half: e.tensor_copy(bvb[:, h * 512:(h + 1) * 512], pbc[:, :]))

        for i in range(NT):
            ser.dma(lambda e, i=i: e.dma_start(
                xt[:, :], dap(x0, i * 128 * DM, [[DM, 128], [1, DM]])))
            xnb_i = (lambda i: lambda: xn_bf[:, i * DM:(i + 1) * DM])(i)
            emit_layernorm(ser, xt, _View(xn_bf, i * DM, DM), scr, mcol, vcol)
            for f in range(16):
                ser.t(lambda e, i=i, f=f: e.transpose(
                    ptr[:, :], xn_bf[:, i * DM + f * 128: i * DM + (f + 1) * 128],
                    ident[:, :]))
                ser.v(lambda e, i=i, f=f: e.tensor_copy(
                    xnT[:, f * SC + i * 128: f * SC + i * 128 + 128], ptr[:, :]))

        for f in range(16):
            ser.dma(lambda e, f=f: e.dma_start(
                dap(o_xnT, f * 128 * SC, [[SC, 128], [1, SC]]),
                xnT[:, f * SC:(f + 1) * SC]), eng="gpsimd")

        for fo in range(24):
            ser.dma(lambda e, fo=fo: e.dma_start(
                wch[:, :], dap(wqkT, fo * 128,
                               [[3072, 128], [128 * 3072, 16], [1, 128]])))
            ser.dma(lambda e, fo=fo: e.dma_start(
                bcol[:, :], dap(bqk, fo * 128, [[1, 128], [1, 1]])))
            for sh in range(2):
                for fi in range(16):
                    ser.t(lambda e, fi=fi, sh=sh: e.matmul(
                        pmm[:, :], wch[:, fi * 128:(fi + 1) * 128],
                        xnT[:, fi * SC + sh * 512: fi * SC + (sh + 1) * 512],
                        start=(fi == 0), stop=(fi == 15)))
                if fo < 16:
                    ser.v(lambda e, sh=sh: e.tensor_scalar(
                        qk_out[:, sh * 512:(sh + 1) * 512], pmm[:, :],
                        bcol[:, :], SCALE, ALU.add, ALU.mult))
                else:
                    ser.v(lambda e, sh=sh: e.tensor_scalar_add(
                        qk_out[:, sh * 512:(sh + 1) * 512], pmm[:, :], bcol[:, :]))
            ser.dma(lambda e, fo=fo: e.dma_start(
                dap(o_qkT, fo * 128 * SC, [[SC, 128], [1, 1024]]),
                qk_out[:, :]), eng="gpsimd")

        ser.dma(lambda e: e.dma_start(
            wv_sb[:, :], dap(wvT, 0, [[1024, 128], [128 * 1024, 16], [1, 1024]])))
        for i in range(NT):
            for vh in range(2):
                for fi in range(16):
                    ser.t(lambda e, i=i, vh=vh, fi=fi: e.matmul(
                        pmm[:, :], xnT[:, fi * SC + i * 128: fi * SC + (i + 1) * 128],
                        wv_sb[:, fi * 1024 + vh * 512: fi * 1024 + (vh + 1) * 512],
                        start=(fi == 0), stop=(fi == 15)))
                ser.v(lambda e, vh=vh: e.tensor_add(
                    v_out[:, vh * 512:(vh + 1) * 512], pmm[:, :],
                    bvb[:, vh * 512:(vh + 1) * 512]))
            ser.dma(lambda e, i=i: e.dma_start(
                dap(o_v, i * 128 * 1024, [[1024, 128], [1, 1024]]),
                v_out[:, :]), eng="gpsimd")

        ser.build(nc, tok)
    return nc


class _View:

    def __init__(self, t, off, width):
        self.t = t
        self.off = off
        self.width = width

    def __getitem__(self, key):
        p, f = key
        if isinstance(f, slice):
            start = f.start or 0
            stop = f.stop if f.stop is not None else self.width
        else:
            raise TypeError(f)
        return self.t[p, self.off + start: self.off + stop]



def build_B():
    nc = bass.Bass("TRN2", target_bir_lowering=False)
    q1T = nc.dram_tensor("q1T", [128, S], BF16, kind="ExternalInput")
    q2T = nc.dram_tensor("q2T", [128, S], BF16, kind="ExternalInput")
    k1T = nc.dram_tensor("k1T", [128, S], BF16, kind="ExternalInput")
    k2T = nc.dram_tensor("k2T", [128, S], BF16, kind="ExternalInput")
    vg = nc.dram_tensor("vg", [S, 256], BF16, kind="ExternalInput")
    lq1 = nc.dram_tensor("lq1", [128], F32, kind="ExternalInput")
    lk1 = nc.dram_tensor("lk1", [128], F32, kind="ExternalInput")
    lq2 = nc.dram_tensor("lq2", [128], F32, kind="ExternalInput")
    lk2 = nc.dram_tensor("lk2", [128], F32, kind="ExternalInput")
    subln = nc.dram_tensor("subln", [2, 128], F32, kind="ExternalInput")
    masksB = nc.dram_tensor("masksB", [128, 4 * 512], BF16, kind="ExternalInput")
    o_oT = nc.dram_tensor("oT", [256, S], BF16, kind="ExternalOutput")

    ser = Ser()
    with ExitStack() as st:
        def sb(name, shape, dtype):
            return st.enter_context(nc.sbuf_tensor(name, shape, dtype))

        def ps(name, shape, dtype):
            return st.enter_context(nc.psum_tensor(name, shape, dtype))

        qT = [sb("q1T_sb", [128, S], BF16), sb("q2T_sb", [128, S], BF16)]
        kT = [sb("k1T_sb", [128, S], BF16), sb("k2T_sb", [128, S], BF16)]
        v_sb = sb("v_sb", [128, 64 * 256], BF16)
        masks = sb("masks", [128, 4 * 512], BF16)
        ones_colb = sb("ones_colb", [128, 1], BF16)
        ones_colf = sb("ones_colf", [128, 1], F32)
        ones_rowf = sb("ones_rowf", [1, 128], F32)
        lrow = [sb(f"lrow{i}", [1, 128], F32) for i in range(2)]
        ee = [sb(f"ee{i}", [1, 1], F32) for i in range(2)]
        lamneg = sb("lamneg", [1, 1], F32)
        sw = [sb(f"sw{i}", [128, 1], F32) for i in range(2)]
        pT = sb("pT", [128, 512], BF16)
        pTm = sb("pTm", [128, 512], BF16)
        linv = [sb(f"linv{i}", [1, 512], F32) for i in range(2)]
        bsc = [sb(f"bsc{i}", [128, 512], F32) for i in range(2)]
        oc = [sb(f"oc{i}", [128, 512], F32) for i in range(2)]
        tmp = sb("tmpB", [128, 512], F32)
        sq = [sb(f"sq{i}", [128, 512], F32) for i in range(2)]
        rrow = sb("rrow", [1, 512], F32)
        rb = sb("rb", [128, 512], F32)
        ost = sb("ost", [128, 512], BF16)
        pS = ps("pS", [128, 512], F32)
        pO = [[ps(f"pO{m}{h}", [128, 512], F32) for h in range(2)] for m in range(2)]
        pL = [ps(f"pL{m}", [1, 512], F32) for m in range(2)]
        pB = ps("pB", [128, 512], F32)
        tok = st.enter_context(nc.semaphore("tok"))

        qkin = [q1T, q2T, k1T, k2T]
        for i, dst in enumerate(qT + kT):
            ser.dma(lambda e, d=dst, src=qkin[i]: e.dma_start(d[:, :], src.ap()))
        ser.dma(lambda e: e.dma_start(
            v_sb[:, :], dap(vg, 0, [[256, 128], [128 * 256, 64], [1, 256]])))
        ser.dma(lambda e: e.dma_start(masks[:, :], masksB.ap()))
        ser.v(lambda e: e.memset(ones_colb[:, :], 1.0))
        ser.v(lambda e: e.memset(ones_colf[:, :], 1.0))
        ser.v(lambda e: e.memset(ones_rowf[:, :], 1.0))
        lsrc = [(lq1, lk1), (lq2, lk2)]
        for m in range(2):
            a, b = lsrc[m]
            ser.dma(lambda e, a=a, m=m: e.dma_start(
                lrow[m][:, :], dap(a, 0, [[128, 1], [1, 128]])))
            ser.dma(lambda e, b=b, m=m: e.dma_start(
                pT[:1, :128] if False else tmp[:1, :128], dap(b, 0, [[128, 1], [1, 128]])))
            ser.v(lambda e, m=m: e.tensor_mul(lrow[m][:, :], lrow[m][:, :], tmp[:1, :128]))
            ser.v(lambda e, m=m: e.reduce_sum(ee[m][:, :], lrow[m][:, :], AX.X))
            ser.s(lambda e, m=m: e.activation(ee[m][:, :], ee[m][:, :], AF.Exp))
        ser.v(lambda e: e.tensor_sub(lamneg[:, :], ee[1][:, :], ee[0][:, :]))
        ser.v(lambda e: e.tensor_scalar_add(lamneg[:, :], lamneg[:, :],
                                            float(-LAMBDA_INIT)))
        for h in range(2):
            ser.dma(lambda e, h=h: e.dma_start(
                sw[h][:, :], dap(subln, h * 128, [[1, 128], [1, 1]])))
            ser.v(lambda e, h=h: e.tensor_scalar_mul(sw[h][:, :], sw[h][:, :],
                                                     float(1.0 - LAMBDA_INIT)))

        for qb in range(16):
            q0 = qb * 512
            nk = (q0 + 512) // 128
            for m in range(2):
                for kt in range(nk):
                    last = (kt == nk - 1)
                    ser.t(lambda e, m=m, kt=kt, q0=q0: e.matmul(
                        pS[:, :], kT[m][:, kt * 128:(kt + 1) * 128],
                        qT[m][:, q0:q0 + 512], start=True, stop=True))
                    straddle = kt * 128 >= q0
                    if straddle:
                        off = (kt * 128 - q0) // 128
                        ser.s(lambda e: e.activation(pTm[:, :], pS[:, :], AF.Exp))
                        ser.v(lambda e, off=off: e.tensor_mul(
                            pT[:, :], pTm[:, :], masks[:, off * 512:(off + 1) * 512]))
                    else:
                        ser.s(lambda e: e.activation(pT[:, :], pS[:, :], AF.Exp))
                    ser.t(lambda e, m=m, kt=kt, last=last: e.matmul(
                        pL[m][:, :], ones_colb[:, :], pT[:, :],
                        start=(kt == 0), stop=last))
                    for h in range(2):
                        ser.t(lambda e, m=m, h=h, kt=kt, last=last: e.matmul(
                            pO[m][h][:, :],
                            v_sb[:, kt * 256 + h * 128: kt * 256 + (h + 1) * 128],
                            pT[:, :], start=(kt == 0), stop=last))
            for m in range(2):
                ser.v(lambda e, m=m: e.reciprocal(linv[m][:, :], pL[m][:, :]))
            ser.v(lambda e: e.tensor_scalar_mul(linv[1][:, :], linv[1][:, :],
                                                lamneg[:1, :1]))
            for m in range(2):
                ser.t(lambda e, m=m: e.matmul(pB[:, :], ones_rowf[:, :],
                                              linv[m][:, :], start=True, stop=True))
                ser.v(lambda e, m=m: e.tensor_copy(bsc[m][:, :], pB[:, :]))
            for h in range(2):
                ser.v(lambda e, h=h: e.tensor_mul(oc[h][:, :], pO[0][h][:, :],
                                                  bsc[0][:, :]))
                ser.v(lambda e, h=h: e.tensor_mul(tmp[:, :], pO[1][h][:, :],
                                                  bsc[1][:, :]))
                ser.v(lambda e, h=h: e.tensor_add(oc[h][:, :], oc[h][:, :],
                                                  tmp[:, :]))
                ser.s(lambda e, h=h: e.activation(sq[h][:, :], oc[h][:, :],
                                                  AF.Square))
                ser.t(lambda e, h=h: e.matmul(pL[0][:, :], ones_colf[:, :],
                                              sq[h][:, :], start=(h == 0),
                                              stop=(h == 1)))
            ser.v(lambda e: e.tensor_scalar(rrow[:, :], pL[0][:, :], 1.0 / 256,
                                            EPS, ALU.mult, ALU.add))
            ser.v(lambda e: e.reciprocal(rrow[:, :], rrow[:, :]))
            ser.s(lambda e: e.activation(rrow[:, :], rrow[:, :], AF.Sqrt))
            ser.t(lambda e: e.matmul(pB[:, :], ones_rowf[:, :], rrow[:, :],
                                     start=True, stop=True))
            ser.v(lambda e: e.tensor_copy(rb[:, :], pB[:, :]))
            for h in range(2):
                ser.v(lambda e, h=h: e.scalar_tensor_tensor(
                    ost[:, :], oc[h][:, :], sw[h][:, :], rb[:, :],
                    ALU.mult, ALU.mult))
                ser.dma(lambda e, h=h, q0=q0: e.dma_start(
                    dap(o_oT, h * 128 * S + q0, [[S, 128], [1, 512]]),
                    ost[:, :]), eng="gpsimd")

        ser.build(nc, tok)
    return nc



def build_CD():
    nc = bass.Bass("TRN2", target_bir_lowering=False)
    oTs = nc.dram_tensor("oTs", [DM, SC], BF16, kind="ExternalInput")
    woT = nc.dram_tensor("woT", [DM, DM], BF16, kind="ExternalInput")
    outb = nc.dram_tensor("outb", [DM], F32, kind="ExternalInput")
    xn0Ts = nc.dram_tensor("xn0Ts", [DM, SC], BF16, kind="ExternalInput")
    gwT = nc.dram_tensor("gwT", [DM, INTER0], BF16, kind="ExternalInput")
    uwT = nc.dram_tensor("uwT", [DM, INTER0], BF16, kind="ExternalInput")
    gs = nc.dram_tensor("gs", [INTER0], F32, kind="ExternalInput")
    us = nc.dram_tensor("us", [INTER0], F32, kind="ExternalInput")
    dwT = nc.dram_tensor("dwT", [INTER0, DM], BF16, kind="ExternalInput")
    ds = nc.dram_tensor("ds", [DM], F32, kind="ExternalInput")
    x0 = nc.dram_tensor("x0", [SC, DM], F32, kind="ExternalInput")
    winT = nc.dram_tensor("winT", [DM, PROJ], BF16, kind="ExternalInput")
    identb = nc.dram_tensor("identb", [128, 128], BF16, kind="ExternalInput")
    attnT_h = nc.dram_tensor("attnT_h", [DM, SC], BF16)
    o_x1 = nc.dram_tensor("x1", [SC, DM], F32, kind="ExternalOutput")
    o_xn1T = nc.dram_tensor("xn1T", [DM, SC], BF16, kind="ExternalOutput")
    o_gate = nc.dram_tensor("gate", [SC, M_INTER], BF16, kind="ExternalOutput")
    o_hbct = nc.dram_tensor("hbct", [CONV_DIM, SC], BF16, kind="ExternalOutput")
    o_dtt = nc.dram_tensor("dtt", [MH, SC], F32, kind="ExternalOutput")

    ser = Ser()
    NT = SC // 128
    with ExitStack() as st:
        def sb(name, shape, dtype):
            return st.enter_context(nc.sbuf_tensor(name, shape, dtype))

        def ps(name, shape, dtype):
            return st.enter_context(nc.psum_tensor(name, shape, dtype))

        big_a = sb("big_a", [128, 16 * SC], BF16)
        big_b = sb("big_b", [128, 16 * SC], BF16)
        hbig = sb("hbig", [128, 44 * SC], BF16)
        wch = sb("wch", [128, 2048], BF16)
        wch2 = sb("wch2", [128, 2048], BF16)
        wch44 = sb("wch44", [128, 44 * 128], BF16)
        scol = sb("scol", [128, 1], F32)
        scol2 = sb("scol2", [128, 1], F32)
        mcol = sb("mcol", [128, 1], F32)
        vcol = sb("vcol", [128, 1], F32)
        ident = sb("ident", [128, 128], BF16)
        tmpf = sb("tmpf", [128, 512], F32)
        tmpf2 = sb("tmpf2", [128, 512], F32)
        tmpf3 = sb("tmpf3", [128, 512], F32)
        st_ch = sb("st_ch", [128, 1024], BF16)
        dt_st = sb("dt_st", [32, 1024], F32)
        piece = sb("piece", [128, 128], BF16)
        x1t = sb("x1t", [128, DM], F32)
        scr = sb("scr", [128, DM], F32)
        xnbf = sb("xnbf", [128, DM], BF16)
        pg = ps("pg", [128, 512], F32)
        pu = ps("pu", [128, 512], F32)
        ptr = ps("ptr", [128, 128], BF16)
        tok = st.enter_context(nc.semaphore("tok"))

        ser.dma(lambda e: e.dma_start(ident[:, :], identb.ap()))

        ser.dma(lambda e: e.dma_start(
            big_a[:, :], dap(oTs, 0, [[SC, 128], [128 * SC, 16], [1, SC]])))
        for fo in range(16):
            ser.dma(lambda e, fo=fo: e.dma_start(
                wch[:, :], dap(woT, fo * 128, [[DM, 128], [128 * DM, 16], [1, 128]])))
            ser.dma(lambda e, fo=fo: e.dma_start(
                scol[:, :], dap(outb, fo * 128, [[1, 128], [1, 1]])))
            for sh in range(2):
                for fi in range(16):
                    ser.t(lambda e, fi=fi, sh=sh: e.matmul(
                        pg[:, :], wch[:, fi * 128:(fi + 1) * 128],
                        big_a[:, fi * SC + sh * 512: fi * SC + (sh + 1) * 512],
                        start=(fi == 0), stop=(fi == 15)))
                ser.v(lambda e, sh=sh: e.tensor_scalar_add(
                    st_ch[:, sh * 512:(sh + 1) * 512], pg[:, :], scol[:, :]))
            ser.dma(lambda e, fo=fo: e.dma_start(
                dap(attnT_h, fo * 128 * SC, [[SC, 128], [1, SC]]),
                st_ch[:, :]), eng="gpsimd")

        ser.dma(lambda e: e.dma_start(
            big_a[:, :], dap(xn0Ts, 0, [[SC, 128], [128 * SC, 16], [1, SC]])))
        for fo in range(44):
            ser.dma(lambda e, fo=fo: e.dma_start(
                wch[:, :], dap(gwT, fo * 128,
                               [[INTER0, 128], [128 * INTER0, 16], [1, 128]])))
            ser.dma(lambda e, fo=fo: e.dma_start(
                wch2[:, :], dap(uwT, fo * 128,
                                [[INTER0, 128], [128 * INTER0, 16], [1, 128]])))
            ser.dma(lambda e, fo=fo: e.dma_start(
                scol[:, :], dap(gs, fo * 128, [[1, 128], [1, 1]])))
            ser.dma(lambda e, fo=fo: e.dma_start(
                scol2[:, :], dap(us, fo * 128, [[1, 128], [1, 1]])))
            for sh in range(2):
                for fi in range(16):
                    ser.t(lambda e, fi=fi, sh=sh: e.matmul(
                        pg[:, :], wch[:, fi * 128:(fi + 1) * 128],
                        big_a[:, fi * SC + sh * 512: fi * SC + (sh + 1) * 512],
                        start=(fi == 0), stop=(fi == 15)))
                for fi in range(16):
                    ser.t(lambda e, fi=fi, sh=sh: e.matmul(
                        pu[:, :], wch2[:, fi * 128:(fi + 1) * 128],
                        big_a[:, fi * SC + sh * 512: fi * SC + (sh + 1) * 512],
                        start=(fi == 0), stop=(fi == 15)))
                ser.v(lambda e: e.tensor_scalar_mul(tmpf[:, :], pg[:, :], scol[:, :]))
                ser.s(lambda e: e.activation(tmpf2[:, :], tmpf[:, :], AF.Silu))
                ser.v(lambda e: e.tensor_scalar_mul(tmpf3[:, :], pu[:, :], scol2[:, :]))
                ser.v(lambda e, fo=fo, sh=sh: e.tensor_mul(
                    hbig[:, fo * SC + sh * 512: fo * SC + (sh + 1) * 512],
                    tmpf2[:, :], tmpf3[:, :]))
        for do in range(16):
            ser.dma(lambda e, do=do: e.dma_start(
                wch44[:, :], dap(dwT, do * 128, [[DM, 128], [128 * DM, 44], [1, 128]])))
            ser.dma(lambda e, do=do: e.dma_start(
                scol[:, :], dap(ds, do * 128, [[1, 128], [1, 1]])))
            for sh in range(2):
                for fi in range(44):
                    ser.t(lambda e, fi=fi, sh=sh: e.matmul(
                        pg[:, :], wch44[:, fi * 128:(fi + 1) * 128],
                        hbig[:, fi * SC + sh * 512: fi * SC + (sh + 1) * 512],
                        start=(fi == 0), stop=(fi == 43)))
                ser.v(lambda e, do=do, sh=sh: e.tensor_scalar_mul(
                    big_b[:, do * SC + sh * 512: do * SC + (sh + 1) * 512],
                    pg[:, :], scol[:, :]))

        for i in range(NT):
            ser.dma(lambda e, i=i: e.dma_start(
                x1t[:, :], dap(x0, i * 128 * DM, [[DM, 128], [1, DM]])))
            for f in range(16):
                ser.dma(lambda e, i=i, f=f: e.dma_start(
                    piece[:, :], dap(attnT_h, f * 128 * SC + i * 128,
                                     [[SC, 128], [1, 128]])))
                ser.t(lambda e: e.transpose(ptr[:, :], piece[:, :], ident[:, :]))
                ser.v(lambda e, f=f: e.tensor_add(
                    x1t[:, f * 128:(f + 1) * 128], x1t[:, f * 128:(f + 1) * 128],
                    ptr[:, :]))
                ser.t(lambda e, i=i, f=f: e.transpose(
                    ptr[:, :], big_b[:, f * SC + i * 128: f * SC + (i + 1) * 128],
                    ident[:, :]))
                ser.v(lambda e, f=f: e.tensor_add(
                    x1t[:, f * 128:(f + 1) * 128], x1t[:, f * 128:(f + 1) * 128],
                    ptr[:, :]))
            ser.dma(lambda e, i=i: e.dma_start(
                dap(o_x1, i * 128 * DM, [[DM, 128], [1, DM]]), x1t[:, :]),
                eng="gpsimd")
            emit_layernorm(ser, x1t, xnbf, scr, mcol, vcol)
            for f in range(16):
                ser.t(lambda e, f=f: e.transpose(
                    ptr[:, :], xnbf[:, f * 128:(f + 1) * 128], ident[:, :]))
                ser.v(lambda e: e.tensor_copy(piece[:, :], ptr[:, :]))
                ser.dma(lambda e, i=i, f=f: e.dma_start(
                    dap(o_xn1T, f * 128 * SC + i * 128, [[SC, 128], [1, 128]]),
                    piece[:, :]), eng="gpsimd")

        ser.dma(lambda e: e.dma_start(
            big_a[:, :], dap(o_xn1T, 0, [[SC, 128], [128 * SC, 16], [1, SC]])))
        for fo in range(35):
            w = 128 if fo < 34 else 32
            ser.dma(lambda e, fo=fo, w=w: e.dma_start(
                wch[:, :16 * w], dap(winT, fo * 128,
                                     [[PROJ, 128], [128 * PROJ, 16], [1, w]])))
            for sh in range(2):
                for fi in range(16):
                    ser.t(lambda e, fi=fi, sh=sh, w=w: e.matmul(
                        pg[:w, :], wch[:, fi * w:(fi + 1) * w],
                        big_a[:, fi * SC + sh * 512: fi * SC + (sh + 1) * 512],
                        start=(fi == 0), stop=(fi == 15)))
                if fo < 16:
                    ser.v(lambda e, sh=sh: e.tensor_copy(
                        st_ch[:, sh * 512:(sh + 1) * 512], pg[:, :]))
                elif fo < 34:
                    ser.v(lambda e, sh=sh: e.tensor_copy(
                        st_ch[:, sh * 512:(sh + 1) * 512], pg[:, :]))
                else:
                    ser.v(lambda e, sh=sh: e.tensor_copy(
                        dt_st[:, sh * 512:(sh + 1) * 512], pg[:32, :]))
            if fo < 16:
                for i in range(NT):
                    ser.t(lambda e, i=i: e.transpose(
                        ptr[:, :], st_ch[:, i * 128:(i + 1) * 128], ident[:, :]))
                    ser.v(lambda e: e.tensor_copy(piece[:, :], ptr[:, :]))
                    ser.dma(lambda e, i=i, fo=fo: e.dma_start(
                        dap(o_gate, i * 128 * M_INTER + fo * 128,
                            [[M_INTER, 128], [1, 128]]), piece[:, :]),
                        eng="gpsimd")
            elif fo < 34:
                ser.dma(lambda e, fo=fo: e.dma_start(
                    dap(o_hbct, (fo - 16) * 128 * SC, [[SC, 128], [1, SC]]),
                    st_ch[:, :]), eng="gpsimd")
            else:
                ser.dma(lambda e: e.dma_start(
                    dap(o_dtt, 0, [[SC, 32], [1, SC]]), dt_st[:, :]),
                    eng="gpsimd")

        ser.build(nc, tok)
    return nc



def build_E():
    nc = bass.Bass("TRN2", target_bir_lowering=False)
    xs_in = [nc.dram_tensor(f"xs{j}", [128, S], BF16, kind="ExternalInput")
             for j in range(2)]
    bt_in = nc.dram_tensor("bt_in", [128, S], BF16, kind="ExternalInput")
    ct_in = nc.dram_tensor("ct_in", [128, S], BF16, kind="ExternalInput")
    cw = [nc.dram_tensor(f"cw{j}", [128, MK], F32, kind="ExternalInput")
          for j in range(4)]
    cb = [nc.dram_tensor(f"cb{j}", [128], F32, kind="ExternalInput")
          for j in range(4)]
    dtr = nc.dram_tensor("dtr", [4, S], F32, kind="ExternalInput")
    dtb = nc.dram_tensor("dtb", [4], F32, kind="ExternalInput")
    alog = nc.dram_tensor("alog", [4], F32, kind="ExternalInput")
    dpar = nc.dram_tensor("dpar", [4], F32, kind="ExternalInput")
    ut01 = nc.dram_tensor("ut01", [128, 128], F32, kind="ExternalInput")
    eye4d = nc.dram_tensor("eye4d", [4, 4], F32, kind="ExternalInput")
    identb = nc.dram_tensor("identb", [128, 128], BF16, kind="ExternalInput")
    idb2d = nc.dram_tensor("idb2d", [128, 64], BF16, kind="ExternalInput")
    o_y = nc.dram_tensor("ynat", [S, 256], BF16, kind="ExternalOutput")

    ser = Ser()
    with ExitStack() as st:
        def sb(name, shape, dtype):
            return st.enter_context(nc.sbuf_tensor(name, shape, dtype))

        def ps(name, shape, dtype):
            return st.enter_context(nc.psum_tensor(name, shape, dtype))

        xsc = [sb(f"xsc{j}", [128, S], BF16) for j in range(2)]
        btc = sb("btc", [128, S], BF16)
        ctc = sb("ctc", [128, S], BF16)
        u_t = sb("u_t", [128, S + 3], BF16)
        acc = sb("acc", [128, 2048], F32)
        dt4 = sb("dt4", [4, S], F32)
        scr4 = sb("scr4", [4, S], F32)
        ag4 = sb("ag4", [4, S], F32)
        wcol = sb("wcol", [128, MK], F32)
        bcolE = sb("bcolE", [128, 1], F32)
        identB = sb("identB", [128, 128], BF16)
        idb2 = sb("idb2", [128, 64], BF16)
        ut = sb("ut", [128, 128], F32)
        eye4 = sb("eye4", [4, 4], F32)
        ones_rowf = sb("ones_rowfE", [1, 128], F32)
        prow = sb("prow", [1, 4], F32)
        arow = sb("arow", [1, 4], F32)
        drow = sb("drow", [1, 4], F32)
        acol = sb("acol", [4, 1], F32)
        bcol4 = sb("bcol4", [4, 1], F32)
        dcol = [sb(f"dcol{h}", [128, 1], F32) for h in range(4)]
        H = [[sb(f"H{h}_{pp}", [128, MP], F32) for pp in range(2)] for h in range(4)]
        Hbf = sb("Hbf", [128, MP], BF16)
        al = sb("al", [1, 128], F32)
        prevs = sb("prevs", [1, 1], F32)
        negalT = sb("negalT", [128, 1], F32)
        eac = sb("eac", [128, 1], F32)
        dtn = sb("dtn", [128, 1], F32)
        eeh = sb("eeh", [1, 1], F32)
        ebc = sb("ebc", [128, 1], F32)
        BCm = sb("BCm", [128, 128], F32)
        Bnat = sb("Bnat", [128, 128], BF16)
        dec_sb = sb("dec_sb", [128, 128], F32)
        dec2 = sb("dec2", [128, 128], F32)
        Mt = sb("Mt", [128, 128], BF16)
        xnat = sb("xnat", [128, MP], BF16)
        yv = sb("yv", [128, MP], F32)
        y2 = sb("y2", [128, MP], F32)
        dxt = sb("dxt", [128, MP], F32)
        Ssb = sb("Ssb", [128, MP], F32)
        yst = sb("yst", [128, 256], BF16)
        psA = ps("psA", [128, 128], F32)
        psB = ps("psB", [128, 128], F32)
        ptrb = ps("ptrb", [128, 128], BF16)
        ptf = ps("ptf", [128, 128], F32)
        psY = ps("psY", [128, MP], F32)
        psYi = ps("psYi", [128, MP], F32)
        psS = ps("psS", [128, MP], F32)
        psE = ps("psE", [128, 1], F32)
        tok = st.enter_context(nc.semaphore("tok"))

        ser.dma(lambda e: e.dma_start(identB[:, :], identb.ap()))
        ser.dma(lambda e: e.dma_start(idb2[:, :], idb2d.ap()))
        ser.dma(lambda e: e.dma_start(ut[:, :], ut01.ap()))
        ser.dma(lambda e: e.dma_start(eye4[:, :], eye4d.ap()))
        ser.v(lambda e: e.memset(ones_rowf[:, :], 1.0))

        conv_src = [xs_in[0], xs_in[1], bt_in, ct_in]
        conv_dst = [xsc[0], xsc[1], btc, ctc]
        for j in range(4):
            ser.v(lambda e: e.memset(u_t[:, :3], 0.0))
            ser.dma(lambda e, j=j: e.dma_start(u_t[:, 3:], conv_src[j].ap()))
            ser.dma(lambda e, j=j: e.dma_start(wcol[:, :], cw[j].ap()))
            ser.dma(lambda e, j=j: e.dma_start(
                bcolE[:, :], dap(cb[j], 0, [[1, 128], [1, 1]])))
            for q in range(4):
                q0 = q * 2048
                ser.v(lambda e, q0=q0: e.tensor_scalar_mul(
                    acc[:, :], u_t[:, q0:q0 + 2048], wcol[:, 0:1]))
                for kk in range(1, MK):
                    ser.v(lambda e, q0=q0, kk=kk: e.scalar_tensor_tensor(
                        acc[:, :], u_t[:, q0 + kk:q0 + kk + 2048],
                        wcol[:, kk:kk + 1], acc[:, :], ALU.mult, ALU.add))
                ser.s(lambda e, j=j, q0=q0: e.activation(
                    conv_dst[j][:, q0:q0 + 2048], acc[:, :], AF.Silu,
                    bias=bcolE[:, :]))

        ser.dma(lambda e: e.dma_start(prow[:, :], dap(dtb, 0, [[4, 1], [1, 4]])))
        ser.dma(lambda e: e.dma_start(arow[:, :], dap(alog, 0, [[4, 1], [1, 4]])))
        ser.dma(lambda e: e.dma_start(drow[:, :], dap(dpar, 0, [[4, 1], [1, 4]])))
        ser.dma(lambda e: e.dma_start(dt4[:, :], dtr.ap()))
        ser.s(lambda e: e.activation(arow[:, :], arow[:, :], AF.Exp))
        ser.v(lambda e: e.tensor_scalar_mul(arow[:, :], arow[:, :], -1.0))
        ser.t(lambda e: e.matmul(psE[:4, :1], arow[:, :], ones_rowf[:1, :1],
                                 start=True, stop=True))
        ser.v(lambda e: e.tensor_copy(acol[:, :], psE[:4, :1]))
        ser.t(lambda e: e.matmul(psE[:4, :1], prow[:, :], ones_rowf[:1, :1],
                                 start=True, stop=True))
        ser.v(lambda e: e.tensor_copy(bcol4[:, :], psE[:4, :1]))
        ser.v(lambda e: e.tensor_scalar_add(dt4[:, :], dt4[:, :], bcol4[:, :]))
        ser.s(lambda e: e.activation(scr4[:, :], dt4[:, :], AF.Abs))
        ser.s(lambda e: e.activation(scr4[:, :], scr4[:, :], AF.Exp, scale=-1.0))
        ser.s(lambda e: e.activation(scr4[:, :], scr4[:, :], AF.Ln, bias=1.0))
        ser.v(lambda e: e.tensor_scalar_max(dt4[:, :], dt4[:, :], 0.0))
        ser.v(lambda e: e.tensor_add(dt4[:, :], dt4[:, :], scr4[:, :]))
        ser.v(lambda e: e.tensor_scalar_mul(scr4[:, :], dt4[:, :], acol[:, :]))
        ser.v(lambda e: e.tensor_tensor_scan(ag4[:, :], scr4[:, :], scr4[:, :],
                                             0.0, ALU.add, ALU.bypass))
        for h in range(4):
            ser.t(lambda e, h=h: e.matmul(psE[:, :], ones_rowf[:, :],
                                          drow[:1, h:h + 1], start=True, stop=True))
            ser.v(lambda e, h=h: e.tensor_copy(dcol[h][:, :], psE[:, :]))
            ser.v(lambda e, h=h: e.memset(H[h][0][:, :], 0.0))

        for t in range(NCHUNK):
            t0 = t * 128
            cur, nxt = t % 2, (t + 1) % 2
            ser.t(lambda e, t0=t0: e.matmul(psA[:, :], btc[:, t0:t0 + 128],
                                            ctc[:, t0:t0 + 128], start=True, stop=True))
            ser.v(lambda e: e.tensor_mul(BCm[:, :], psA[:, :], ut[:, :]))
            ser.t(lambda e, t0=t0: e.transpose(ptrb[:, :], btc[:, t0:t0 + 128],
                                               identB[:, :]))
            ser.v(lambda e: e.tensor_copy(Bnat[:, :], ptrb[:, :]))
            for h in range(4):
                c0 = h // 2
                p0 = (h % 2) * 64
                if t > 0:
                    ser.t(lambda e, h=h, t0=t0: e.matmul(
                        ptf[:1, :1], eye4[:, h:h + 1], ag4[:4, t0 - 1:t0],
                        start=True, stop=True))
                    ser.v(lambda e: e.tensor_copy(prevs[:, :], ptf[:1, :1]))
                ser.t(lambda e, h=h, t0=t0: e.matmul(
                    ptf[:1, :128], eye4[:, h:h + 1], ag4[:4, t0:t0 + 128],
                    start=True, stop=True))
                if t > 0:
                    ser.v(lambda e: e.tensor_scalar_sub(al[:, :], ptf[:1, :128],
                                                        prevs[:1, :1]))
                else:
                    ser.v(lambda e: e.tensor_copy(al[:, :], ptf[:1, :128]))
                ser.t(lambda e: e.matmul(ptf[:, :1], al[:, :], ones_rowf[:1, :1],
                                         start=True, stop=True))
                ser.s(lambda e: e.activation(eac[:, :], ptf[:, :1], AF.Exp))
                ser.v(lambda e: e.tensor_scalar_mul(negalT[:, :], ptf[:, :1], -1.0))
                ser.t(lambda e, h=h, t0=t0: e.matmul(
                    ptf[:, :1], dt4[:4, t0:t0 + 128], eye4[:, h:h + 1],
                    start=True, stop=True))
                ser.v(lambda e: e.tensor_copy(dtn[:, :], ptf[:, :1]))
                ser.t(lambda e: e.matmul(psB[:, :], ones_rowf[:, :], al[:, :],
                                         start=True, stop=True))
                ser.v(lambda e: e.tensor_scalar(dec_sb[:, :], psB[:, :],
                                                negalT[:, :], 0.0,
                                                ALU.add, ALU.min))
                ser.s(lambda e: e.activation(dec_sb[:, :], dec_sb[:, :], AF.Exp))
                ser.v(lambda e: e.tensor_scalar_mul(dec2[:, :], dec_sb[:, :],
                                                    dtn[:, :]))
                ser.v(lambda e: e.tensor_mul(Mt[:, :], dec2[:, :], BCm[:, :]))
                idap = (lambda e: identB[:64, :64]) if p0 == 0 else (
                    lambda e: idb2[64:128, :64])
                ser.t(lambda e, c0=c0, p0=p0, t0=t0, idap=idap: e.transpose(
                    ptrb[:, :MP], xsc[c0][p0:p0 + 64, t0:t0 + 128], idap(e)))
                ser.v(lambda e: e.tensor_copy(xnat[:, :], ptrb[:, :MP]))
                ser.t(lambda e: e.matmul(psY[:, :], Mt[:, :], xnat[:, :],
                                         start=True, stop=True))
                ser.v(lambda e, h=h, cur=cur: e.tensor_copy(Hbf[:, :],
                                                            H[h][cur][:, :]))
                ser.t(lambda e, t0=t0: e.matmul(
                    psYi[:, :], ctc[:, t0:t0 + 128], Hbf[:, :],
                    start=True, stop=True))
                ser.v(lambda e: e.tensor_scalar_mul(yv[:, :], psYi[:, :], eac[:, :]))
                ser.v(lambda e: e.tensor_add(y2[:, :], psY[:, :], yv[:, :]))
                ser.v(lambda e, h=h: e.tensor_scalar_mul(dxt[:, :], xnat[:, :],
                                                         dcol[h][:, :]))
                ser.v(lambda e, h=h: e.tensor_add(
                    yst[:, h * MP:(h + 1) * MP], dxt[:, :], y2[:, :]))
                ser.v(lambda e: e.tensor_scalar_mul(Mt[:, :], Bnat[:, :],
                                                    dec2[:, 127:128]))
                ser.t(lambda e: e.matmul(psS[:, :], Mt[:, :], xnat[:, :],
                                         start=True, stop=True))
                ser.v(lambda e: e.tensor_copy(Ssb[:, :], psS[:, :]))
                ser.s(lambda e: e.activation(eeh[:, :], al[:1, 127:128], AF.Exp))
                ser.t(lambda e: e.matmul(psE[:, :], ones_rowf[:, :], eeh[:, :],
                                         start=True, stop=True))
                ser.v(lambda e: e.tensor_copy(ebc[:, :], psE[:, :]))
                ser.v(lambda e, h=h, cur=cur, nxt=nxt: e.scalar_tensor_tensor(
                    H[h][nxt][:, :], H[h][cur][:, :], ebc[:, :], Ssb[:, :],
                    ALU.mult, ALU.add))
            ser.dma(lambda e, t0=t0: e.dma_start(
                dap(o_y, t0 * 256, [[256, 128], [1, 256]]), yst[:, :]),
                eng="gpsimd")

        ser.build(nc, tok)
    return nc



def build_F():
    nc = bass.Bass("TRN2", target_bir_lowering=False)
    ynat = nc.dram_tensor("ynat", [SC, M_INTER], BF16, kind="ExternalInput")
    gate = nc.dram_tensor("gate", [SC, M_INTER], BF16, kind="ExternalInput")
    nrm = nc.dram_tensor("nrm", [DM], F32, kind="ExternalInput")
    woutT = nc.dram_tensor("woutT", [DM, DM], BF16, kind="ExternalInput")
    xn1Ts = nc.dram_tensor("xn1Ts", [DM, SC], BF16, kind="ExternalInput")
    wg0 = nc.dram_tensor("wg0", [DM, FF1], BF16, kind="ExternalInput")
    wg1 = nc.dram_tensor("wg1", [DM, FF1], BF16, kind="ExternalInput")
    wlin = nc.dram_tensor("wlin", [FF1, DM], BF16, kind="ExternalInput")
    x1 = nc.dram_tensor("x1", [SC, DM], F32, kind="ExternalInput")
    identb = nc.dram_tensor("identb", [128, 128], BF16, kind="ExternalInput")
    moT_h = nc.dram_tensor("moT_h", [DM, SC], BF16)
    o_out = nc.dram_tensor("out", [SC, DM], F32, kind="ExternalOutput")

    ser = Ser()
    NT = SC // 128
    GC = float(np.sqrt(2.0 / np.pi))
    with ExitStack() as st:
        def sb(name, shape, dtype):
            return st.enter_context(nc.sbuf_tensor(name, shape, dtype))

        def ps(name, shape, dtype):
            return st.enter_context(nc.psum_tensor(name, shape, dtype))

        gnT = sb("gnT", [128, 16 * SC], BF16)
        xn1T = sb("xn1T_sbF", [128, 16 * SC], BF16)
        hid = sb("hid", [128, 64 * 256], BF16)
        fT = sb("fT", [128, 16 * 256], BF16)
        wch = sb("wchF", [128, 2048], BF16)
        wch2 = sb("wch2F", [128, 2048], BF16)
        wch64 = sb("wch64", [128, 64 * 128], BF16)
        nwb = sb("nwb", [128, 2048], F32)
        nrow = sb("nrow", [1, 2048], F32)
        ones_rowf = sb("ones_rowfF", [1, 128], F32)
        ybf = sb("ybf", [128, 2048], BF16)
        gbf = sb("gbf", [128, 2048], BF16)
        sgate = sb("sgate", [128, 2048], BF16)
        g_f = sb("g_f", [128, 2048], F32)
        scr = sb("scrF", [128, 2048], F32)
        gnw = sb("gnw", [128, 2048], BF16)
        ident = sb("identF_", [128, 128], BF16)
        sscol = sb("sscol", [128, 1], F32)
        x2s = sb("x2s", [128, 256], F32)
        x3s = sb("x3s", [128, 256], F32)
        inn = sb("inn", [128, 256], F32)
        ths = sb("ths", [128, 256], F32)
        t3s = sb("t3s", [128, 256], F32)
        ot = sb("ot", [128, 2048], F32)
        piece = sb("pieceF", [128, 128], BF16)
        pg0 = ps("pg0", [128, 512], F32)
        pg1 = ps("pg1", [128, 512], F32)
        ptr = ps("ptrF", [128, 128], BF16)
        pbc = ps("pbcF", [128, 512], F32)
        tok = st.enter_context(nc.semaphore("tok"))

        ser.dma(lambda e: e.dma_start(ident[:, :], identb.ap()))
        ser.v(lambda e: e.memset(ones_rowf[:, :], 1.0))
        ser.dma(lambda e: e.dma_start(nrow[:, :], dap(nrm, 0, [[DM, 1], [1, DM]])))
        for q in range(4):
            ser.t(lambda e, q=q: e.matmul(pbc[:, :], ones_rowf[:, :],
                                          nrow[:1, q * 512:(q + 1) * 512],
                                          start=True, stop=True))
            ser.v(lambda e, q=q: e.tensor_copy(nwb[:, q * 512:(q + 1) * 512],
                                               pbc[:, :]))
        ser.dma(lambda e: e.dma_start(
            xn1T[:, :], dap(xn1Ts, 0, [[SC, 128], [128 * SC, 16], [1, SC]])))

        for i in range(NT):
            ser.dma(lambda e, i=i: e.dma_start(
                ybf[:, :], dap(ynat, i * 128 * M_INTER, [[M_INTER, 128], [1, M_INTER]])))
            ser.dma(lambda e, i=i: e.dma_start(
                gbf[:, :], dap(gate, i * 128 * M_INTER, [[M_INTER, 128], [1, M_INTER]])))
            ser.s(lambda e: e.activation(sgate[:, :], gbf[:, :], AF.Silu))
            ser.v(lambda e: e.tensor_mul(g_f[:, :], ybf[:, :], sgate[:, :]))
            ser.s(lambda e: e.activation(scr[:, :], g_f[:, :], AF.Square,
                                         accum_out=sscol[:, :]))
            ser.v(lambda e: e.tensor_scalar(sscol[:, :], sscol[:, :], 1.0 / DM,
                                            EPS, ALU.mult, ALU.add))
            ser.v(lambda e: e.reciprocal(sscol[:, :], sscol[:, :]))
            ser.s(lambda e: e.activation(sscol[:, :], sscol[:, :], AF.Sqrt))
            ser.v(lambda e: e.tensor_scalar_mul(g_f[:, :], g_f[:, :], sscol[:, :]))
            ser.v(lambda e: e.tensor_mul(gnw[:, :], g_f[:, :], nwb[:, :]))
            for f in range(16):
                ser.t(lambda e, f=f: e.transpose(
                    ptr[:, :], gnw[:, f * 128:(f + 1) * 128], ident[:, :]))
                ser.v(lambda e, f=f, i=i: e.tensor_copy(
                    gnT[:, f * SC + i * 128: f * SC + (i + 1) * 128], ptr[:, :]))
        for fo in range(16):
            ser.dma(lambda e, fo=fo: e.dma_start(
                wch[:, :], dap(woutT, fo * 128, [[DM, 128], [128 * DM, 16], [1, 128]])))
            for sh in range(2):
                for fi in range(16):
                    ser.t(lambda e, fi=fi, sh=sh: e.matmul(
                        pg0[:, :], wch[:, fi * 128:(fi + 1) * 128],
                        gnT[:, fi * SC + sh * 512: fi * SC + (sh + 1) * 512],
                        start=(fi == 0), stop=(fi == 15)))
                ser.v(lambda e, sh=sh: e.tensor_copy(
                    ybf[:, sh * 512:(sh + 1) * 512], pg0[:, :]))
            ser.dma(lambda e, fo=fo: e.dma_start(
                dap(moT_h, fo * 128 * SC, [[SC, 128], [1, SC]]),
                ybf[:, :1024]), eng="gpsimd")

        for qf in range(4):
            s0 = qf * 256
            for fo in range(64):
                ser.dma(lambda e, fo=fo: e.dma_start(
                    wch[:, :], dap(wg0, fo * 128,
                                   [[FF1, 128], [128 * FF1, 16], [1, 128]])))
                ser.dma(lambda e, fo=fo: e.dma_start(
                    wch2[:, :], dap(wg1, fo * 128,
                                    [[FF1, 128], [128 * FF1, 16], [1, 128]])))
                for fi in range(16):
                    ser.t(lambda e, fi=fi, s0=s0: e.matmul(
                        pg0[:, :256], wch[:, fi * 128:(fi + 1) * 128],
                        xn1T[:, fi * SC + s0: fi * SC + s0 + 256],
                        start=(fi == 0), stop=(fi == 15)))
                for fi in range(16):
                    ser.t(lambda e, fi=fi, s0=s0: e.matmul(
                        pg1[:, :256], wch2[:, fi * 128:(fi + 1) * 128],
                        xn1T[:, fi * SC + s0: fi * SC + s0 + 256],
                        start=(fi == 0), stop=(fi == 15)))
                ser.s(lambda e: e.activation(x2s[:, :], pg0[:, :256], AF.Square))
                ser.v(lambda e: e.tensor_mul(x3s[:, :], x2s[:, :], pg0[:, :256]))
                ser.v(lambda e: e.scalar_tensor_tensor(
                    inn[:, :], x3s[:, :], 0.044715, pg0[:, :256], ALU.mult, ALU.add))
                ser.s(lambda e: e.activation(ths[:, :], inn[:, :], AF.Tanh, scale=GC))
                ser.v(lambda e: e.tensor_scalar(ths[:, :], ths[:, :], 1.0, 0.5,
                                                ALU.add, ALU.mult))
                ser.v(lambda e: e.tensor_mul(t3s[:, :], ths[:, :], pg0[:, :256]))
                ser.v(lambda e, fo=fo: e.tensor_mul(
                    hid[:, fo * 256:(fo + 1) * 256], t3s[:, :], pg1[:, :256]))
            for do in range(16):
                ser.dma(lambda e, do=do: e.dma_start(
                    wch64[:, :], dap(wlin, do * 128,
                                     [[DM, 128], [128 * DM, 64], [1, 128]])))
                for fi in range(64):
                    ser.t(lambda e, fi=fi: e.matmul(
                        pg0[:, :256], wch64[:, fi * 128:(fi + 1) * 128],
                        hid[:, fi * 256:(fi + 1) * 256],
                        start=(fi == 0), stop=(fi == 63)))
                ser.v(lambda e, do=do: e.tensor_copy(
                    fT[:, do * 256:(do + 1) * 256], pg0[:, :256]))
            for ii in range(2):
                ig = qf * 2 + ii
                ser.dma(lambda e, ig=ig: e.dma_start(
                    ot[:, :], dap(x1, ig * 128 * DM, [[DM, 128], [1, DM]])))
                for f in range(16):
                    ser.dma(lambda e, ig=ig, f=f: e.dma_start(
                        piece[:, :], dap(moT_h, f * 128 * SC + ig * 128,
                                         [[SC, 128], [1, 128]])))
                    ser.t(lambda e: e.transpose(ptr[:, :], piece[:, :], ident[:, :]))
                    ser.v(lambda e, f=f: e.tensor_add(
                        ot[:, f * 128:(f + 1) * 128], ot[:, f * 128:(f + 1) * 128],
                        ptr[:, :]))
                    ser.t(lambda e, f=f, ii=ii: e.transpose(
                        ptr[:, :], fT[:, f * 256 + ii * 128: f * 256 + (ii + 1) * 128],
                        ident[:, :]))
                    ser.v(lambda e, f=f: e.tensor_add(
                        ot[:, f * 128:(f + 1) * 128], ot[:, f * 128:(f + 1) * 128],
                        ptr[:, :]))
                ser.dma(lambda e, ig=ig: e.dma_start(
                    dap(o_out, ig * 128 * DM, [[DM, 128], [1, DM]]), ot[:, :]),
                    eng="gpsimd")

        ser.build(nc, tok)
    return nc



def kernel_5launch(**inp):
    g = {k: np.asarray(v) for k, v in inp.items()}
    x0 = f32(g["L0_attn_hidden_states"].reshape(S, DM))
    wqkv = g["L0_attn_Wqkv_w"]
    b_all = f32(g["L0_attn_Wqkv_b"])
    identb = bf(np.eye(128))
    cores = list(range(NC))

    ncA = build_A()
    im = [{
        "x0": x0[c * SC:(c + 1) * SC],
        "wqkT": bf(wqkv[:3072].T), "wvT": bf(wqkv[3072:].T),
        "bqk": f32(b_all[:3072]), "bv": f32(b_all[3072:]),
        "identb": identb,
    } for c in range(NC)]
    rA = run_bass_kernel_spmd(ncA, im, core_ids=cores).results
    xnT = np.concatenate([rA[c]["xnT"] for c in range(NC)], axis=1)
    qkT = np.concatenate([rA[c]["qkT"] for c in range(NC)], axis=1)
    v = np.concatenate([rA[c]["v"] for c in range(NC)], axis=0)

    masksB = np.zeros((128, 4 * 512), np.float32)
    for off in range(4):
        for kj in range(128):
            masksB[kj, off * 512 + off * 128 + kj:(off + 1) * 512] = 1.0
    ncB = build_B()
    im = [{
        "q1T": np.ascontiguousarray(qkT[h * 256:h * 256 + 128]),
        "q2T": np.ascontiguousarray(qkT[h * 256 + 128:h * 256 + 256]),
        "k1T": np.ascontiguousarray(qkT[2048 + (h // 2) * 256:2048 + (h // 2) * 256 + 128]),
        "k2T": np.ascontiguousarray(qkT[2048 + (h // 2) * 256 + 128:2048 + (h // 2) * 256 + 256]),
        "vg": np.ascontiguousarray(v[:, (h // 2) * 256:(h // 2 + 1) * 256]),
        "lq1": f32(g["L0_attn_lambda_q1"]), "lk1": f32(g["L0_attn_lambda_k1"]),
        "lq2": f32(g["L0_attn_lambda_q2"]), "lk2": f32(g["L0_attn_lambda_k2"]),
        "subln": f32(g["L0_attn_subln_w"]).reshape(2, 128),
        "masksB": bf(masksB),
    } for h in range(NC)]
    rB = run_bass_kernel_spmd(ncB, im, core_ids=cores).results
    oT = np.concatenate([rB[h]["oT"] for h in range(NC)], axis=0)

    ncCD = build_CD()
    im = [{
        "oTs": np.ascontiguousarray(oT[:, c * SC:(c + 1) * SC]),
        "woT": bf(g["L0_attn_out_w"].T),
        "outb": f32(g["L0_attn_out_b"]),
        "xn0Ts": np.ascontiguousarray(xnT[:, c * SC:(c + 1) * SC]),
        "gwT": bf(g["L0_ffn_gate_w"].T),
        "uwT": bf(g["L0_ffn_up_w"].T),
        "gs": f32(g["L0_ffn_gate_s"]), "us": f32(g["L0_ffn_up_s"]),
        "dwT": bf(g["L0_ffn_down_w"].T),
        "ds": f32(g["L0_ffn_down_s"]),
        "x0": x0[c * SC:(c + 1) * SC],
        "winT": bf(g["L1_attn_W_in"].T),
        "identb": identb,
    } for c in range(NC)]
    rCD = run_bass_kernel_spmd(ncCD, im, core_ids=cores).results
    x1 = np.concatenate([rCD[c]["x1"] for c in range(NC)], axis=0)
    xn1T = np.concatenate([rCD[c]["xn1T"] for c in range(NC)], axis=1)
    gate = np.concatenate([rCD[c]["gate"] for c in range(NC)], axis=0)
    hbct = np.concatenate([rCD[c]["hbct"] for c in range(NC)], axis=1)
    dtt = np.concatenate([rCD[c]["dtt"] for c in range(NC)], axis=1)

    cwr = f32(g["L1_attn_conv_w"]).reshape(18, 128, MK)
    cbr = f32(g["L1_attn_conv_b"]).reshape(18, 128)
    ut01 = np.ascontiguousarray(np.triu(np.ones((128, 128), np.float32)))
    idb2 = np.zeros((128, 64), np.float32)
    for i in range(64):
        idb2[64 + i, i] = 1.0
    ncE = build_E()
    im = [{
        "xs0": np.ascontiguousarray(hbct[c * 256:c * 256 + 128]),
        "xs1": np.ascontiguousarray(hbct[c * 256 + 128:c * 256 + 256]),
        "bt_in": np.ascontiguousarray(hbct[2048:2176]),
        "ct_in": np.ascontiguousarray(hbct[2176:2304]),
        "cw0": cwr[2 * c], "cw1": cwr[2 * c + 1], "cw2": cwr[16], "cw3": cwr[17],
        "cb0": cbr[2 * c], "cb1": cbr[2 * c + 1], "cb2": cbr[16], "cb3": cbr[17],
        "dtr": np.ascontiguousarray(dtt[4 * c:4 * c + 4]),
        "dtb": f32(g["L1_attn_dt_bias"][4 * c:4 * c + 4]),
        "alog": f32(g["L1_attn_A_log"][4 * c:4 * c + 4]),
        "dpar": f32(g["L1_attn_D_param"][4 * c:4 * c + 4]),
        "ut01": ut01, "eye4d": np.eye(4, dtype=np.float32),
        "identb": identb, "idb2d": bf(idb2),
    } for c in range(NC)]
    rE = run_bass_kernel_spmd(ncE, im, core_ids=cores).results
    y = np.concatenate([rE[c]["ynat"] for c in range(NC)], axis=1)

    ncF = build_F()
    im = [{
        "ynat": np.ascontiguousarray(y[c * SC:(c + 1) * SC]),
        "gate": np.ascontiguousarray(gate[c * SC:(c + 1) * SC]),
        "nrm": f32(g["L1_attn_norm_w"]),
        "woutT": bf(g["L1_attn_W_out"].T),
        "xn1Ts": np.ascontiguousarray(xn1T[:, c * SC:(c + 1) * SC]),
        "wg0": bf(g["L1_ffn_w_gating"][0]),
        "wg1": bf(g["L1_ffn_w_gating"][1]),
        "wlin": bf(g["L1_ffn_w_linear"]),
        "x1": x1[c * SC:(c + 1) * SC],
        "identb": identb,
    } for c in range(NC)]
    rF = run_bass_kernel_spmd(ncF, im, core_ids=cores).results
    out = np.concatenate([rF[c]["out"] for c in range(NC)], axis=0)
    return out.reshape(1, S, DM).astype(np.float32)



def build_FUSED():
    nc = bass.Bass("TRN2", target_bir_lowering=False)
    x0f = nc.dram_tensor("x0f", [S, DM], F32, kind="ExternalInput")
    x0s = nc.dram_tensor("x0s", [SC, DM], F32, kind="ExternalInput")
    offs = nc.dram_tensor("offs", [1, 4], mybir.dt.int32, kind="ExternalInput")
    wqk_sl = nc.dram_tensor("wqk_sl", [DM, 512], BF16, kind="ExternalInput")
    bqk_sl = nc.dram_tensor("bqk_sl", [512], F32, kind="ExternalInput")
    wv_sl = nc.dram_tensor("wv_sl", [DM, 256], BF16, kind="ExternalInput")
    bv_sl = nc.dram_tensor("bv_sl", [256], F32, kind="ExternalInput")
    lq1 = nc.dram_tensor("lq1", [128], F32, kind="ExternalInput")
    lk1 = nc.dram_tensor("lk1", [128], F32, kind="ExternalInput")
    lq2 = nc.dram_tensor("lq2", [128], F32, kind="ExternalInput")
    lk2 = nc.dram_tensor("lk2", [128], F32, kind="ExternalInput")
    subln = nc.dram_tensor("subln", [2, 128], F32, kind="ExternalInput")
    masksB = nc.dram_tensor("masksB", [128, 4 * 512], BF16, kind="ExternalInput")
    woT = nc.dram_tensor("woT", [DM, DM], BF16, kind="ExternalInput")
    outb = nc.dram_tensor("outb", [DM], F32, kind="ExternalInput")
    gwT = nc.dram_tensor("gwT", [DM, INTER0], BF16, kind="ExternalInput")
    uwT = nc.dram_tensor("uwT", [DM, INTER0], BF16, kind="ExternalInput")
    gs = nc.dram_tensor("gs", [INTER0], F32, kind="ExternalInput")
    us = nc.dram_tensor("us", [INTER0], F32, kind="ExternalInput")
    dwT = nc.dram_tensor("dwT", [INTER0, DM], BF16, kind="ExternalInput")
    ds = nc.dram_tensor("ds", [DM], F32, kind="ExternalInput")
    winT = nc.dram_tensor("winT", [DM, PROJ], BF16, kind="ExternalInput")
    win_sl = nc.dram_tensor("win_sl", [DM, 516], BF16, kind="ExternalInput")
    cw = [nc.dram_tensor(f"cw{j}", [128, MK], F32, kind="ExternalInput")
          for j in range(4)]
    cb = [nc.dram_tensor(f"cb{j}", [128], F32, kind="ExternalInput")
          for j in range(4)]
    dtb = nc.dram_tensor("dtb", [4], F32, kind="ExternalInput")
    alog = nc.dram_tensor("alog", [4], F32, kind="ExternalInput")
    dpar = nc.dram_tensor("dpar", [4], F32, kind="ExternalInput")
    ut01 = nc.dram_tensor("ut01", [128, 128], F32, kind="ExternalInput")
    eye4d = nc.dram_tensor("eye4d", [4, 4], F32, kind="ExternalInput")
    identb = nc.dram_tensor("identb", [128, 128], BF16, kind="ExternalInput")
    idb2d = nc.dram_tensor("idb2d", [128, 64], BF16, kind="ExternalInput")
    nrm = nc.dram_tensor("nrm", [DM], F32, kind="ExternalInput")
    woutT = nc.dram_tensor("woutT", [DM, DM], BF16, kind="ExternalInput")
    wg0 = nc.dram_tensor("wg0", [DM, FF1], BF16, kind="ExternalInput")
    wg1 = nc.dram_tensor("wg1", [DM, FF1], BF16, kind="ExternalInput")
    wlin = nc.dram_tensor("wlin", [FF1, DM], BF16, kind="ExternalInput")
    xnT_d = nc.dram_tensor("xnT_d", [DM, S], BF16)
    cin_oT = nc.dram_tensor("cin_oT", [256, S], BF16)
    oT_all = nc.dram_tensor("oT_all", [DM, S], BF16)
    attnT_h = nc.dram_tensor("attnT_h", [DM, SC], BF16)
    x1_d = nc.dram_tensor("x1_d", [SC, DM], F32)
    gate_d = nc.dram_tensor("gate_d", [SC, M_INTER], BF16)
    cin_xn1T = nc.dram_tensor("cin_xn1T", [DM, SC], BF16)
    xn1T_all = nc.dram_tensor("xn1T_all", [NC * DM, SC], BF16)
    xsB_d = [nc.dram_tensor(f"xsB_d{j}", [128, S], BF16) for j in range(4)]
    cin_y = nc.dram_tensor("cin_y", [S, 256], BF16)
    y_all = nc.dram_tensor("y_all", [NC * S, 256], BF16)
    y_own = nc.dram_tensor("y_own", [SC, M_INTER], BF16)
    moT_h = nc.dram_tensor("moT_h", [DM, SC], BF16)
    o_out = nc.dram_tensor("out", [SC, DM], F32, kind="ExternalOutput")
    import os as _os
    dbg = _os.environ.get("FUSED_DEBUG") == "1"
    if dbg:
        d_xnT = nc.dram_tensor("d_xnT", [DM, S], BF16, kind="ExternalOutput")
        d_biga = nc.dram_tensor("d_biga", [DM, SC], BF16, kind="ExternalOutput")
        d_oT = nc.dram_tensor("d_oT", [256, S], BF16, kind="ExternalOutput")
        d_oTall = nc.dram_tensor("d_oTall", [DM, 512], BF16, kind="ExternalOutput")
        d_x1 = nc.dram_tensor("d_x1", [SC, DM], F32, kind="ExternalOutput")
        d_xn1 = nc.dram_tensor("d_xn1", [DM, SC], BF16, kind="ExternalOutput")
        d_y = nc.dram_tensor("d_y", [S, 256], BF16, kind="ExternalOutput")
        d_yown = nc.dram_tensor("d_yown", [SC, M_INTER], BF16, kind="ExternalOutput")
        d_gate = nc.dram_tensor("d_gate", [SC, M_INTER], BF16, kind="ExternalOutput")
        d_xsB = nc.dram_tensor("d_xsB", [128, S], BF16, kind="ExternalOutput")
        d_attnT = nc.dram_tensor("d_attnT", [DM, SC], BF16, kind="ExternalOutput")
        d_h = nc.dram_tensor("d_h", [INTER0, SC], BF16, kind="ExternalOutput")
        d_f0 = nc.dram_tensor("d_f0", [DM, SC], BF16, kind="ExternalOutput")

    ser = Ser()
    NT = SC // 128
    GC = float(np.sqrt(2.0 / np.pi))
    with ExitStack() as st:
        tok = st.enter_context(nc.semaphore("tok"))
        grb = st.enter_context(nc.gpsimd.register("grb"))
        gry = st.enter_context(nc.gpsimd.register("gry"))
        grt = st.enter_context(nc.gpsimd.register("grt"))

        def sbT(name, shape, dtype):
            return nc.sbuf_tensor(name, shape, dtype)

        ident = st.enter_context(sbT("identP", [128, 128], BF16))
        ones_rowf = st.enter_context(sbT("ones_rowfP", [1, 128], F32))
        offs_sb = st.enter_context(sbT("offs_sb", [1, 4], mybir.dt.int32))
        ser.dma(lambda e: e.dma_start(ident[:, :], identb.ap()))
        ser.dma(lambda e: e.dma_start(offs_sb[:, :], offs.ap()))
        ser.v(lambda e: e.memset(ones_rowf[:, :], 1.0))
        ser.op("gpsimd", lambda e: e.reg_load(grb, offs_sb[:1, :1]), 0)
        ser.op("gpsimd", lambda e: e.reg_load(gry, offs_sb[:1, 1:2]), 0)

        with sbT("xtP", [128, DM], F32) as xt, \
             sbT("scrP", [128, DM], F32) as scr, \
             sbT("xnbfP", [128, DM], BF16) as xnbf, \
             sbT("slabP", [128, 16 * 128], BF16) as slab, \
             sbT("mcolP", [128, 1], F32) as mcol, \
             sbT("vcolP", [128, 1], F32) as vcol, \
             nc.psum_tensor("ptrP", [128, 128], BF16) as ptr:
            for i in range(S // 128):
                ser.dma(lambda e, i=i: e.dma_start(
                    xt[:, :], dap(x0f, i * 128 * DM, [[DM, 128], [1, DM]])))
                emit_layernorm(ser, xt, xnbf, scr, mcol, vcol)
                for f in range(16):
                    ser.t(lambda e, f=f: e.transpose(
                        ptr[:, :], xnbf[:, f * 128:(f + 1) * 128], ident[:, :]))
                    ser.v(lambda e, f=f: e.tensor_copy(
                        slab[:, f * 128:(f + 1) * 128], ptr[:, :]))
                ser.dma(lambda e, i=i: e.dma_start(
                    dap(xnT_d, i * 128, [[S, 128], [128 * S, 16], [1, 128]]),
                    slab[:, :]), eng="gpsimd")
            ser.flush(nc, tok)

        st23 = ExitStack()
        qT = [st23.enter_context(sbT(f"qTf{m}", [128, S], BF16)) for m in range(2)]
        kT = [st23.enter_context(sbT(f"kTf{m}", [128, S], BF16)) for m in range(2)]
        v_sb = st23.enter_context(sbT("v_sbF", [128, 64 * 256], BF16))
        qkT4 = qT + kT
        with sbT("xnblk", [128, 16 * 2048], BF16) as xnblk, \
             sbT("wchP2", [128, 2048], BF16) as wch, \
             sbT("wvP2", [128, 16 * 256], BF16) as wv_sb2, \
             sbT("bcolP2", [128, 1], F32) as bcol, \
             sbT("bvrow2", [1, 256], F32) as bvrow, \
             sbT("bvb2", [128, 256], F32) as bvb, \
             sbT("qko2", [128, 512], BF16) as qko, \
             nc.psum_tensor("pmmP2", [128, 512], F32) as pmm, \
             nc.psum_tensor("pbcP2", [128, 256], F32) as pbc:
            ser.dma(lambda e: e.dma_start(
                wv_sb2[:, :], dap(wv_sl, 0, [[256, 128], [128 * 256, 16], [1, 256]])))
            ser.dma(lambda e: e.dma_start(
                bvrow[:, :], dap(bv_sl, 0, [[256, 1], [1, 256]])))
            ser.t(lambda e: e.matmul(pbc[:, :], ones_rowf[:, :], bvrow[:, :],
                                     start=True, stop=True))
            ser.v(lambda e: e.tensor_copy(bvb[:, :], pbc[:, :]))
            for blk in range(4):
                b0 = blk * 2048
                for fi in range(16):
                    ser.dma(lambda e, fi=fi, b0=b0: e.dma_start(
                        xnblk[:, fi * 2048:(fi + 1) * 2048],
                        dap(xnT_d, fi * 128 * S + b0, [[S, 128], [1, 2048]])))
                for fo in range(4):
                    ser.dma(lambda e, fo=fo: e.dma_start(
                        wch[:, :], dap(wqk_sl, fo * 128,
                                       [[512, 128], [128 * 512, 16], [1, 128]])))
                    ser.dma(lambda e, fo=fo: e.dma_start(
                        bcol[:, :], dap(bqk_sl, fo * 128, [[1, 128], [1, 1]])))
                    for sh in range(4):
                        for fi in range(16):
                            ser.t(lambda e, fi=fi, sh=sh: e.matmul(
                                pmm[:, :], wch[:, fi * 128:(fi + 1) * 128],
                                xnblk[:, fi * 2048 + sh * 512: fi * 2048 + (sh + 1) * 512],
                                start=(fi == 0), stop=(fi == 15)))
                        if fo < 2:
                            ser.v(lambda e: e.tensor_scalar(
                                qko[:, :], pmm[:, :], bcol[:, :], SCALE,
                                ALU.add, ALU.mult))
                        else:
                            ser.v(lambda e: e.tensor_scalar_add(
                                qko[:, :], pmm[:, :], bcol[:, :]))
                        ser.v(lambda e, fo=fo, b0=b0, sh=sh: e.tensor_copy(
                            qkT4[fo][:, b0 + sh * 512: b0 + (sh + 1) * 512],
                            qko[:, :]))
                for i in range(16):
                    for fi in range(16):
                        ser.t(lambda e, i=i, fi=fi: e.matmul(
                            pbc[:, :], xnblk[:, fi * 2048 + i * 128: fi * 2048 + (i + 1) * 128],
                            wv_sb2[:, fi * 256:(fi + 1) * 256],
                            start=(fi == 0), stop=(fi == 15)))
                    ser.v(lambda e, blk=blk, i=i: e.tensor_add(
                        v_sb[:, (blk * 16 + i) * 256:(blk * 16 + i + 1) * 256],
                        pbc[:, :], bvb[:, :]))
            ser.flush(nc, tok)

        with ExitStack() as s3:
            def e3(name, shape, dtype):
                return s3.enter_context(sbT(name, shape, dtype))

            masks = e3("masksF", [128, 4 * 512], BF16)
            ones_colb = e3("ones_colbF", [128, 1], BF16)
            ones_colf = e3("ones_colfF", [128, 1], F32)
            lrow0 = e3("lrow0", [1, 128], F32)
            lrow1 = e3("lrow1", [1, 128], F32)
            ee0 = e3("ee0", [1, 1], F32)
            ee1 = e3("ee1", [1, 1], F32)
            lamneg = e3("lamnegF", [1, 1], F32)
            sw0 = e3("sw0", [128, 1], F32)
            sw1 = e3("sw1", [128, 1], F32)
            pT = e3("pTF", [128, 512], BF16)
            pTm = e3("pTmF", [128, 512], BF16)
            linv0 = e3("linv0", [1, 512], F32)
            linv1 = e3("linv1", [1, 512], F32)
            bsc0 = e3("bsc0", [128, 512], F32)
            bsc1 = e3("bsc1", [128, 512], F32)
            oc0 = e3("oc0", [128, 512], F32)
            oc1 = e3("oc1", [128, 512], F32)
            tmp = e3("tmpBF", [128, 512], F32)
            sq0 = e3("sq0", [128, 512], F32)
            sq1 = e3("sq1", [128, 512], F32)
            rrow = e3("rrowF", [1, 512], F32)
            rb = e3("rbF", [128, 512], F32)
            ost = e3("ostF", [128, 512], BF16)
            pS = s3.enter_context(nc.psum_tensor("pSF", [128, 512], F32))
            pO00 = s3.enter_context(nc.psum_tensor("pO00", [128, 512], F32))
            pO01 = s3.enter_context(nc.psum_tensor("pO01", [128, 512], F32))
            pO10 = s3.enter_context(nc.psum_tensor("pO10", [128, 512], F32))
            pO11 = s3.enter_context(nc.psum_tensor("pO11", [128, 512], F32))
            pL0 = s3.enter_context(nc.psum_tensor("pL0", [1, 512], F32))
            pL1 = s3.enter_context(nc.psum_tensor("pL1", [1, 512], F32))
            pB = s3.enter_context(nc.psum_tensor("pBF", [128, 512], F32))
            pO = [[pO00, pO01], [pO10, pO11]]
            pL = [pL0, pL1]
            linv = [linv0, linv1]
            bsc = [bsc0, bsc1]
            oc = [oc0, oc1]
            sq = [sq0, sq1]
            sw = [sw0, sw1]
            lrow = [lrow0, lrow1]
            ee = [ee0, ee1]
            ser.dma(lambda e: e.dma_start(masks[:, :], masksB.ap()))
            ser.v(lambda e: e.memset(ones_colb[:, :], 1.0))
            ser.v(lambda e: e.memset(ones_colf[:, :], 1.0))
            lsrc = [(lq1, lk1), (lq2, lk2)]
            for m in range(2):
                a, b = lsrc[m]
                ser.dma(lambda e, a=a, m=m: e.dma_start(
                    lrow[m][:, :], dap(a, 0, [[128, 1], [1, 128]])))
                ser.dma(lambda e, b=b: e.dma_start(
                    tmp[:1, :128], dap(b, 0, [[128, 1], [1, 128]])))
                ser.v(lambda e, m=m: e.tensor_mul(lrow[m][:, :], lrow[m][:, :],
                                                  tmp[:1, :128]))
                ser.v(lambda e, m=m: e.reduce_sum(ee[m][:, :], lrow[m][:, :], AX.X))
                ser.s(lambda e, m=m: e.activation(ee[m][:, :], ee[m][:, :], AF.Exp))
            ser.v(lambda e: e.tensor_sub(lamneg[:, :], ee[1][:, :], ee[0][:, :]))
            ser.v(lambda e: e.tensor_scalar_add(lamneg[:, :], lamneg[:, :],
                                                float(-LAMBDA_INIT)))
            for h in range(2):
                ser.dma(lambda e, h=h: e.dma_start(
                    sw[h][:, :], dap(subln, h * 128, [[1, 128], [1, 1]])))
                ser.v(lambda e, h=h: e.tensor_scalar_mul(
                    sw[h][:, :], sw[h][:, :], float(1.0 - LAMBDA_INIT)))
            for qb in range(16):
                q0 = qb * 512
                nk = (q0 + 512) // 128
                for m in range(2):
                    for kt in range(nk):
                        last = (kt == nk - 1)
                        ser.t(lambda e, m=m, kt=kt, q0=q0: e.matmul(
                            pS[:, :], kT[m][:, kt * 128:(kt + 1) * 128],
                            qT[m][:, q0:q0 + 512], start=True, stop=True))
                        if kt * 128 >= q0:
                            off = (kt * 128 - q0) // 128
                            ser.s(lambda e: e.activation(pTm[:, :], pS[:, :], AF.Exp))
                            ser.v(lambda e, off=off: e.tensor_mul(
                                pT[:, :], pTm[:, :],
                                masks[:, off * 512:(off + 1) * 512]))
                        else:
                            ser.s(lambda e: e.activation(pT[:, :], pS[:, :], AF.Exp))
                        ser.t(lambda e, m=m, kt=kt, last=last: e.matmul(
                            pL[m][:, :], ones_colb[:, :], pT[:, :],
                            start=(kt == 0), stop=last))
                        for h in range(2):
                            ser.t(lambda e, m=m, h=h, kt=kt, last=last: e.matmul(
                                pO[m][h][:, :],
                                v_sb[:, kt * 256 + h * 128: kt * 256 + (h + 1) * 128],
                                pT[:, :], start=(kt == 0), stop=last))
                for m in range(2):
                    ser.v(lambda e, m=m: e.reciprocal(linv[m][:, :], pL[m][:, :]))
                ser.v(lambda e: e.tensor_scalar_mul(linv[1][:, :], linv[1][:, :],
                                                    lamneg[:1, :1]))
                for m in range(2):
                    ser.t(lambda e, m=m: e.matmul(pB[:, :], ones_rowf[:, :],
                                                  linv[m][:, :], start=True, stop=True))
                    ser.v(lambda e, m=m: e.tensor_copy(bsc[m][:, :], pB[:, :]))
                for h in range(2):
                    ser.v(lambda e, h=h: e.tensor_mul(oc[h][:, :], pO[0][h][:, :],
                                                      bsc[0][:, :]))
                    ser.v(lambda e, h=h: e.tensor_mul(tmp[:, :], pO[1][h][:, :],
                                                      bsc[1][:, :]))
                    ser.v(lambda e, h=h: e.tensor_add(oc[h][:, :], oc[h][:, :],
                                                      tmp[:, :]))
                    ser.s(lambda e, h=h: e.activation(sq[h][:, :], oc[h][:, :],
                                                      AF.Square))
                    ser.t(lambda e, h=h: e.matmul(pL[0][:, :], ones_colf[:, :],
                                                  sq[h][:, :], start=(h == 0),
                                                  stop=(h == 1)))
                ser.v(lambda e: e.tensor_scalar(rrow[:, :], pL[0][:, :], 1.0 / 256,
                                                EPS, ALU.mult, ALU.add))
                ser.v(lambda e: e.reciprocal(rrow[:, :], rrow[:, :]))
                ser.s(lambda e: e.activation(rrow[:, :], rrow[:, :], AF.Sqrt))
                ser.t(lambda e: e.matmul(pB[:, :], ones_rowf[:, :], rrow[:, :],
                                         start=True, stop=True))
                ser.v(lambda e: e.tensor_copy(rb[:, :], pB[:, :]))
                for h in range(2):
                    ser.v(lambda e, h=h: e.scalar_tensor_tensor(
                        ost[:, :], oc[h][:, :], sw[h][:, :], rb[:, :],
                        ALU.mult, ALU.mult))
                    ser.dma(lambda e, h=h, q0=q0: e.dma_start(
                        dap(cin_oT, h * 128 * S + q0, [[S, 128], [1, 512]]),
                        ost[:, :]), eng="gpsimd")
            ser.flush(nc, tok)
        st23.close()

        ser.op("gpsimd", lambda e: e.collective_compute(
            "AllGather", ALU.bypass, ins=[cin_oT.ap().opt()],
            outs=[oT_all.ap().opt()], replica_groups=[list(range(NC))]), 1)

        emit_fused_tail(nc, ser, dict(
            oT_all=oT_all, attnT_h=attnT_h, x1_d=x1_d, gate_d=gate_d,
            cin_xn1T=cin_xn1T, xn1T_all=xn1T_all, xsB_d=xsB_d, cin_y=cin_y,
            identb=identb, cin_oT_h=cin_oT, tok=tok,
            **({"d_xnT": d_xnT, "d_biga": d_biga, "d_oT": d_oT, "d_oTall": d_oTall,
                "d_x1": d_x1, "d_xn1": d_xn1, "d_y": d_y, "d_yown": d_yown,
                "d_gate": d_gate, "d_xsB": d_xsB, "d_attnT": d_attnT,
                "d_f0": d_f0, "d_h": d_h} if dbg else {}),
            y_all=y_all, y_own=y_own, moT_h=moT_h, xnT_d=xnT_d, o_out=o_out,
            woT=woT, outb=outb, gwT=gwT, uwT=uwT, gs=gs, us=us, dwT=dwT,
            ds=ds, winT=winT, win_sl=win_sl, cw=cw, cb=cb, dtb=dtb,
            alog=alog, dpar=dpar, ut01=ut01, eye4d=eye4d, idb2d=idb2d,
            nrm=nrm, woutT=woutT, wg0=wg0, wg1=wg1, wlin=wlin, x0s=x0s,
            ident=ident, ones_rowf=ones_rowf, grb=grb, gry=gry, grt=grt))
        ser.build(nc, tok)
    return nc


def emit_fused_tail(nc, ser, H):
    g = type("G", (), H)
    NT = SC // 128
    GC = float(np.sqrt(2.0 / np.pi))
    ident = H["ident"]
    ones_rowf = H["ones_rowf"]
    grb, gry, grt = H["grb"], H["gry"], H["grt"]

    def sbT(name, shape, dtype):
        return nc.sbuf_tensor(name, shape, dtype)

    with ExitStack() as s4:
        big_a = s4.enter_context(sbT("big_a4", [128, 16 * SC], BF16))
        big_b = s4.enter_context(sbT("big_b4", [128, 16 * SC], BF16))
        hbig = s4.enter_context(sbT("hbig4", [128, 44 * SC], BF16))
        wch = s4.enter_context(sbT("wch4", [128, 2048], BF16))
        wch2 = s4.enter_context(sbT("wch24", [128, 2048], BF16))
        wch44 = s4.enter_context(sbT("wch444", [128, 44 * 128], BF16))
        scol = s4.enter_context(sbT("scol4", [128, 1], F32))
        scol2 = s4.enter_context(sbT("scol24", [128, 1], F32))
        mcol = s4.enter_context(sbT("mcol4", [128, 1], F32))
        vcol = s4.enter_context(sbT("vcol4", [128, 1], F32))
        tmpf = s4.enter_context(sbT("tmpf4", [128, 512], F32))
        tmpf2 = s4.enter_context(sbT("tmpf24", [128, 512], F32))
        tmpf3 = s4.enter_context(sbT("tmpf34", [128, 512], F32))
        st_ch = s4.enter_context(sbT("st_ch4", [128, 1024], BF16))
        piece = s4.enter_context(sbT("piece4", [128, 128], BF16))
        x1t = s4.enter_context(sbT("x1t4", [128, DM], F32))
        scr = s4.enter_context(sbT("scr4_", [128, DM], F32))
        xnbf = s4.enter_context(sbT("xnbf4", [128, DM], BF16))
        pg = s4.enter_context(nc.psum_tensor("pg4", [128, 512], F32))
        pu = s4.enter_context(nc.psum_tensor("pu4", [128, 512], F32))
        ptr = s4.enter_context(nc.psum_tensor("ptr4", [128, 128], BF16))

        ser.dma(lambda e: e.dma_start(
            big_a[:, :], bass.AP(H["oT_all"], grb,
                                 [[S, 128], [128 * S, 16], [1, SC]])),
            eng="gpsimd")
        for fo in range(16):
            ser.dma(lambda e, fo=fo: e.dma_start(
                wch[:, :], dap(H["woT"], fo * 128,
                               [[DM, 128], [128 * DM, 16], [1, 128]])))
            ser.dma(lambda e, fo=fo: e.dma_start(
                scol[:, :], dap(H["outb"], fo * 128, [[1, 128], [1, 1]])))
            for sh in range(2):
                for fi in range(16):
                    ser.t(lambda e, fi=fi, sh=sh: e.matmul(
                        pg[:, :], wch[:, fi * 128:(fi + 1) * 128],
                        big_a[:, fi * SC + sh * 512: fi * SC + (sh + 1) * 512],
                        start=(fi == 0), stop=(fi == 15)))
                ser.v(lambda e, sh=sh: e.tensor_scalar_add(
                    st_ch[:, sh * 512:(sh + 1) * 512], pg[:, :], scol[:, :]))
            ser.dma(lambda e, fo=fo: e.dma_start(
                dap(H["attnT_h"], fo * 128 * SC, [[SC, 128], [1, SC]]),
                st_ch[:, :]), eng="gpsimd")
        ser.dma(lambda e: e.dma_start(
            big_a[:, :], bass.AP(H["xnT_d"], grb,
                                 [[S, 128], [128 * S, 16], [1, SC]])),
            eng="gpsimd")
        if "d_biga" in H:
            for f_ in range(16):
                ser.dma(lambda e, f_=f_: e.dma_start(
                    dap(H["d_biga"], f_ * 128 * SC, [[SC, 128], [1, SC]]),
                    big_a[:, f_ * SC:(f_ + 1) * SC]), eng="gpsimd")
        for fo in range(44):
            ser.dma(lambda e, fo=fo: e.dma_start(
                wch[:, :], dap(H["gwT"], fo * 128,
                               [[INTER0, 128], [128 * INTER0, 16], [1, 128]])))
            ser.dma(lambda e, fo=fo: e.dma_start(
                wch2[:, :], dap(H["uwT"], fo * 128,
                                [[INTER0, 128], [128 * INTER0, 16], [1, 128]])))
            ser.dma(lambda e, fo=fo: e.dma_start(
                scol[:, :], dap(H["gs"], fo * 128, [[1, 128], [1, 1]])))
            ser.dma(lambda e, fo=fo: e.dma_start(
                scol2[:, :], dap(H["us"], fo * 128, [[1, 128], [1, 1]])))
            for sh in range(2):
                for fi in range(16):
                    ser.t(lambda e, fi=fi, sh=sh: e.matmul(
                        pg[:, :], wch[:, fi * 128:(fi + 1) * 128],
                        big_a[:, fi * SC + sh * 512: fi * SC + (sh + 1) * 512],
                        start=(fi == 0), stop=(fi == 15)))
                for fi in range(16):
                    ser.t(lambda e, fi=fi, sh=sh: e.matmul(
                        pu[:, :], wch2[:, fi * 128:(fi + 1) * 128],
                        big_a[:, fi * SC + sh * 512: fi * SC + (sh + 1) * 512],
                        start=(fi == 0), stop=(fi == 15)))
                ser.v(lambda e: e.tensor_scalar_mul(tmpf[:, :], pg[:, :], scol[:, :]))
                ser.s(lambda e: e.activation(tmpf2[:, :], tmpf[:, :], AF.Silu))
                ser.v(lambda e: e.tensor_scalar_mul(tmpf3[:, :], pu[:, :], scol2[:, :]))
                ser.v(lambda e, fo=fo, sh=sh: e.tensor_mul(
                    hbig[:, fo * SC + sh * 512: fo * SC + (sh + 1) * 512],
                    tmpf2[:, :], tmpf3[:, :]))
        if "d_h" in H:
            for f_ in range(44):
                ser.dma(lambda e, f_=f_: e.dma_start(
                    dap(H["d_h"], f_ * 128 * SC, [[SC, 128], [1, SC]]),
                    hbig[:, f_ * SC:(f_ + 1) * SC]), eng="gpsimd")
        for do in range(16):
            ser.dma(lambda e, do=do: e.dma_start(
                wch44[:, :], dap(H["dwT"], do * 128,
                                 [[DM, 128], [128 * DM, 44], [1, 128]])))
            ser.dma(lambda e, do=do: e.dma_start(
                scol[:, :], dap(H["ds"], do * 128, [[1, 128], [1, 1]])))
            for sh in range(2):
                for fi in range(44):
                    ser.t(lambda e, fi=fi, sh=sh: e.matmul(
                        pg[:, :], wch44[:, fi * 128:(fi + 1) * 128],
                        hbig[:, fi * SC + sh * 512: fi * SC + (sh + 1) * 512],
                        start=(fi == 0), stop=(fi == 43)))
                ser.v(lambda e, do=do, sh=sh: e.tensor_scalar_mul(
                    big_b[:, do * SC + sh * 512: do * SC + (sh + 1) * 512],
                    pg[:, :], scol[:, :]))
        if "d_f0" in H:
            for f_ in range(16):
                ser.dma(lambda e, f_=f_: e.dma_start(
                    dap(H["d_f0"], f_ * 128 * SC, [[SC, 128], [1, SC]]),
                    big_b[:, f_ * SC:(f_ + 1) * SC]), eng="gpsimd")
        for i in range(NT):
            ser.dma(lambda e, i=i: e.dma_start(
                x1t[:, :], dap(H["x0s"], i * 128 * DM, [[DM, 128], [1, DM]])))
            for f in range(16):
                ser.dma(lambda e, i=i, f=f: e.dma_start(
                    piece[:, :], dap(H["attnT_h"], f * 128 * SC + i * 128,
                                     [[SC, 128], [1, 128]])))
                ser.t(lambda e: e.transpose(ptr[:, :], piece[:, :], ident[:, :]))
                ser.v(lambda e, f=f: e.tensor_add(
                    x1t[:, f * 128:(f + 1) * 128], x1t[:, f * 128:(f + 1) * 128],
                    ptr[:, :]))
                ser.t(lambda e, i=i, f=f: e.transpose(
                    ptr[:, :], big_b[:, f * SC + i * 128: f * SC + (i + 1) * 128],
                    ident[:, :]))
                ser.v(lambda e, f=f: e.tensor_add(
                    x1t[:, f * 128:(f + 1) * 128], x1t[:, f * 128:(f + 1) * 128],
                    ptr[:, :]))
            ser.dma(lambda e, i=i: e.dma_start(
                dap(H["x1_d"], i * 128 * DM, [[DM, 128], [1, DM]]), x1t[:, :]),
                eng="gpsimd")
            emit_layernorm(ser, x1t, xnbf, scr, mcol, vcol)
            for f in range(16):
                ser.t(lambda e, f=f: e.transpose(
                    ptr[:, :], xnbf[:, f * 128:(f + 1) * 128], ident[:, :]))
                ser.v(lambda e: e.tensor_copy(piece[:, :], ptr[:, :]))
                ser.dma(lambda e, i=i, f=f: e.dma_start(
                    dap(H["cin_xn1T"], f * 128 * SC + i * 128,
                        [[SC, 128], [1, 128]]), piece[:, :]), eng="gpsimd")
        ser.dma(lambda e: e.dma_start(
            big_a[:, :], dap(H["cin_xn1T"], 0,
                             [[SC, 128], [128 * SC, 16], [1, SC]])))
        for fo in range(16):
            ser.dma(lambda e, fo=fo: e.dma_start(
                wch[:, :], dap(H["winT"], fo * 128,
                               [[PROJ, 128], [128 * PROJ, 16], [1, 128]])))
            for sh in range(2):
                for fi in range(16):
                    ser.t(lambda e, fi=fi, sh=sh: e.matmul(
                        pg[:, :], wch[:, fi * 128:(fi + 1) * 128],
                        big_a[:, fi * SC + sh * 512: fi * SC + (sh + 1) * 512],
                        start=(fi == 0), stop=(fi == 15)))
                ser.v(lambda e, sh=sh: e.tensor_copy(
                    st_ch[:, sh * 512:(sh + 1) * 512], pg[:, :]))
            for i in range(NT):
                ser.t(lambda e, i=i: e.transpose(
                    ptr[:, :], st_ch[:, i * 128:(i + 1) * 128], ident[:, :]))
                ser.v(lambda e: e.tensor_copy(piece[:, :], ptr[:, :]))
                ser.dma(lambda e, i=i, fo=fo: e.dma_start(
                    dap(H["gate_d"], i * 128 * M_INTER + fo * 128,
                        [[M_INTER, 128], [1, 128]]), piece[:, :]),
                    eng="gpsimd")
        ser.flush(nc, H["tok"])

    ser.op("gpsimd", lambda e: e.collective_compute(
        "AllGather", ALU.bypass, ins=[H["cin_xn1T"].ap().opt()],
        outs=[H["xn1T_all"].ap().opt()], replica_groups=[list(range(NC))]), 1)

    with ExitStack() as s5o:
        dt4 = s5o.enter_context(sbT("dt4F", [4, S], F32))
        with ExitStack() as s5:
            xnblk = s5.enter_context(sbT("xnblk5", [128, 16 * SC], BF16))
            wch = s5.enter_context(sbT("wch5", [128, 2048], BF16))
            stg = s5.enter_context(sbT("stg5", [128, 1024], BF16))
            pg = s5.enter_context(nc.psum_tensor("pg5", [128, 512], F32))
            for r in range(NC):
                for fi in range(16):
                    ser.dma(lambda e, r=r, fi=fi: e.dma_start(
                        xnblk[:, fi * SC:(fi + 1) * SC],
                        dap(H["xn1T_all"], (r * DM + fi * 128) * SC,
                            [[SC, 128], [1, SC]])))
                for fo in range(5):
                    w = 128 if fo < 4 else 4
                    ser.dma(lambda e, fo=fo, w=w: e.dma_start(
                        wch[:, :16 * w], dap(H["win_sl"], fo * 128,
                                             [[516, 128], [128 * 516, 16], [1, w]])))
                    for sh in range(2):
                        for fi in range(16):
                            ser.t(lambda e, fi=fi, sh=sh, w=w: e.matmul(
                                pg[:w, :], wch[:, fi * w:(fi + 1) * w],
                                xnblk[:, fi * SC + sh * 512: fi * SC + (sh + 1) * 512],
                                start=(fi == 0), stop=(fi == 15)))
                        if fo < 4:
                            ser.v(lambda e, sh=sh: e.tensor_copy(
                                stg[:, sh * 512:(sh + 1) * 512], pg[:, :]))
                        else:
                            ser.v(lambda e, r=r, sh=sh: e.tensor_copy(
                                dt4[:, r * SC + sh * 512: r * SC + (sh + 1) * 512],
                                pg[:4, :]))
                    if fo < 4:
                        ser.dma(lambda e, fo=fo, r=r: e.dma_start(
                            dap(H["xsB_d"][fo], r * SC, [[S, 128], [1, SC]]),
                            stg[:, :]), eng="gpsimd")
            ser.flush(nc, H["tok"])

        emit_scan(nc, ser, H, dt4, ident, ones_rowf)

    ser.op("gpsimd", lambda e: e.collective_compute(
        "AllGather", ALU.bypass, ins=[H["cin_y"].ap().opt()],
        outs=[H["y_all"].ap().opt()], replica_groups=[list(range(NC))]), 1)
    for eb in range(NC):
        ser.op("gpsimd", lambda e, eb=eb: e.reg_mov(grt, eb * S * 256), 0)
        ser.op("gpsimd", lambda e: e.reg_add(grt, grt, gry), 0)
        ser.dma(lambda e, eb=eb: e.dma_start(
            dap(H["y_own"], eb * 256, [[M_INTER, 128], [128 * M_INTER, 8], [1, 256]]),
            bass.AP(H["y_all"], grt, [[256, 128], [128 * 256, 8], [1, 256]])),
            eng="gpsimd")

    emit_final(nc, ser, H, ident, ones_rowf)
    if "d_xnT" in H:
        def cp(dst, srct, rows, cols, srccols):
            ser.dma(lambda e, dst=dst, srct=srct, rows=rows, cols=cols, srccols=srccols:
                    e.dma_start(
                        dap(dst, 0, [[cols, 128], [128 * cols, rows // 128], [1, cols]]),
                        dap(srct, 0, [[srccols, 128], [128 * srccols, rows // 128], [1, cols]])),
                    eng="gpsimd")
        cp(H["d_xnT"], H["xnT_d"], DM, S, S)
        cp(H["d_oT"], H["cin_oT_h"], 256, S, S)
        cp(H["d_oTall"], H["oT_all"], DM, 512, S)
        cp(H["d_x1"], H["x1_d"], SC, DM, DM)
        cp(H["d_xn1"], H["cin_xn1T"], DM, SC, SC)
        cp(H["d_y"], H["cin_y"], S, 256, 256)
        cp(H["d_yown"], H["y_own"], SC, M_INTER, M_INTER)
        cp(H["d_gate"], H["gate_d"], SC, M_INTER, M_INTER)
        cp(H["d_xsB"], H["xsB_d"][0], 128, S, S)
        cp(H["d_attnT"], H["attnT_h"], DM, SC, SC)


def emit_scan(nc, ser, HD, dt4, identP, ones_rowfP):
    tok = HD["tok"]
    with ExitStack() as st:
        def sb(name, shape, dtype):
            return st.enter_context(nc.sbuf_tensor(name, shape, dtype))

        def ps(name, shape, dtype):
            return st.enter_context(nc.psum_tensor(name, shape, dtype))

        xsc = [sb(f"xsc{j}", [128, S], BF16) for j in range(2)]
        btc = sb("btc", [128, S], BF16)
        ctc = sb("ctc", [128, S], BF16)
        u_t = sb("u_t", [128, S + 3], BF16)
        acc = sb("acc", [128, 2048], F32)
        scr4 = sb("scr4", [4, S], F32)
        ag4 = sb("ag4", [4, S], F32)
        wcol = sb("wcol", [128, MK], F32)
        bcolE = sb("bcolE", [128, 1], F32)
        identB = sb("identB", [128, 128], BF16)
        idb2 = sb("idb2", [128, 64], BF16)
        ut = sb("ut", [128, 128], F32)
        eye4 = sb("eye4", [4, 4], F32)
        ones_rowf = sb("ones_rowfE", [1, 128], F32)
        prow = sb("prow", [1, 4], F32)
        arow = sb("arow", [1, 4], F32)
        drow = sb("drow", [1, 4], F32)
        acol = sb("acol", [4, 1], F32)
        bcol4 = sb("bcol4", [4, 1], F32)
        dcol = [sb(f"dcol{h}", [128, 1], F32) for h in range(4)]
        H = [[sb(f"H{h}_{pp}", [128, MP], F32) for pp in range(2)] for h in range(4)]
        Hbf = sb("Hbf", [128, MP], BF16)
        al = sb("al", [1, 128], F32)
        prevs = sb("prevs", [1, 1], F32)
        negalT = sb("negalT", [128, 1], F32)
        eac = sb("eac", [128, 1], F32)
        dtn = sb("dtn", [128, 1], F32)
        eeh = sb("eeh", [1, 1], F32)
        ebc = sb("ebc", [128, 1], F32)
        BCm = sb("BCm", [128, 128], F32)
        Bnat = sb("Bnat", [128, 128], BF16)
        dec_sb = sb("dec_sb", [128, 128], F32)
        dec2 = sb("dec2", [128, 128], F32)
        Mt = sb("Mt", [128, 128], BF16)
        xnat = sb("xnat", [128, MP], BF16)
        yv = sb("yv", [128, MP], F32)
        y2 = sb("y2", [128, MP], F32)
        dxt = sb("dxt", [128, MP], F32)
        Ssb = sb("Ssb", [128, MP], F32)
        yst = sb("yst", [128, 256], BF16)
        psA = ps("psA", [128, 128], F32)
        psB = ps("psB", [128, 128], F32)
        ptrb = ps("ptrb", [128, 128], BF16)
        ptf = ps("ptf", [128, 128], F32)
        psY = ps("psY", [128, MP], F32)
        psYi = ps("psYi", [128, MP], F32)
        psS = ps("psS", [128, MP], F32)
        psE = ps("psE", [128, 1], F32)

        ser.dma(lambda e: e.dma_start(identB[:, :], HD["identb"].ap()))
        ser.dma(lambda e: e.dma_start(idb2[:, :], HD["idb2d"].ap()))
        ser.dma(lambda e: e.dma_start(ut[:, :], HD["ut01"].ap()))
        ser.dma(lambda e: e.dma_start(eye4[:, :], HD["eye4d"].ap()))
        ser.v(lambda e: e.memset(ones_rowf[:, :], 1.0))

        conv_src = HD["xsB_d"]
        conv_dst = [xsc[0], xsc[1], btc, ctc]
        for j in range(4):
            ser.v(lambda e: e.memset(u_t[:, :3], 0.0))
            ser.dma(lambda e, j=j: e.dma_start(u_t[:, 3:], conv_src[j].ap()))
            ser.dma(lambda e, j=j: e.dma_start(wcol[:, :], HD["cw"][j].ap()))
            ser.dma(lambda e, j=j: e.dma_start(
                bcolE[:, :], dap(HD["cb"][j], 0, [[1, 128], [1, 1]])))
            for q in range(4):
                q0 = q * 2048
                ser.v(lambda e, q0=q0: e.tensor_scalar_mul(
                    acc[:, :], u_t[:, q0:q0 + 2048], wcol[:, 0:1]))
                for kk in range(1, MK):
                    ser.v(lambda e, q0=q0, kk=kk: e.scalar_tensor_tensor(
                        acc[:, :], u_t[:, q0 + kk:q0 + kk + 2048],
                        wcol[:, kk:kk + 1], acc[:, :], ALU.mult, ALU.add))
                ser.s(lambda e, j=j, q0=q0: e.activation(
                    conv_dst[j][:, q0:q0 + 2048], acc[:, :], AF.Silu,
                    bias=bcolE[:, :]))

        ser.dma(lambda e: e.dma_start(prow[:, :], dap(HD["dtb"], 0, [[4, 1], [1, 4]])))
        ser.dma(lambda e: e.dma_start(arow[:, :], dap(HD["alog"], 0, [[4, 1], [1, 4]])))
        ser.dma(lambda e: e.dma_start(drow[:, :], dap(HD["dpar"], 0, [[4, 1], [1, 4]])))
        ser.s(lambda e: e.activation(arow[:, :], arow[:, :], AF.Exp))
        ser.v(lambda e: e.tensor_scalar_mul(arow[:, :], arow[:, :], -1.0))
        ser.t(lambda e: e.matmul(psE[:4, :1], arow[:, :], ones_rowf[:1, :1],
                                 start=True, stop=True))
        ser.v(lambda e: e.tensor_copy(acol[:, :], psE[:4, :1]))
        ser.t(lambda e: e.matmul(psE[:4, :1], prow[:, :], ones_rowf[:1, :1],
                                 start=True, stop=True))
        ser.v(lambda e: e.tensor_copy(bcol4[:, :], psE[:4, :1]))
        ser.v(lambda e: e.tensor_scalar_add(dt4[:, :], dt4[:, :], bcol4[:, :]))
        ser.s(lambda e: e.activation(scr4[:, :], dt4[:, :], AF.Abs))
        ser.s(lambda e: e.activation(scr4[:, :], scr4[:, :], AF.Exp, scale=-1.0))
        ser.s(lambda e: e.activation(scr4[:, :], scr4[:, :], AF.Ln, bias=1.0))
        ser.v(lambda e: e.tensor_scalar_max(dt4[:, :], dt4[:, :], 0.0))
        ser.v(lambda e: e.tensor_add(dt4[:, :], dt4[:, :], scr4[:, :]))
        ser.v(lambda e: e.tensor_scalar_mul(scr4[:, :], dt4[:, :], acol[:, :]))
        ser.v(lambda e: e.tensor_tensor_scan(ag4[:, :], scr4[:, :], scr4[:, :],
                                             0.0, ALU.add, ALU.bypass))
        for h in range(4):
            ser.t(lambda e, h=h: e.matmul(psE[:, :], ones_rowf[:, :],
                                          drow[:1, h:h + 1], start=True, stop=True))
            ser.v(lambda e, h=h: e.tensor_copy(dcol[h][:, :], psE[:, :]))
            ser.v(lambda e, h=h: e.memset(H[h][0][:, :], 0.0))

        for t in range(NCHUNK):
            t0 = t * 128
            cur, nxt = t % 2, (t + 1) % 2
            ser.t(lambda e, t0=t0: e.matmul(psA[:, :], btc[:, t0:t0 + 128],
                                            ctc[:, t0:t0 + 128], start=True, stop=True))
            ser.v(lambda e: e.tensor_mul(BCm[:, :], psA[:, :], ut[:, :]))
            ser.t(lambda e, t0=t0: e.transpose(ptrb[:, :], btc[:, t0:t0 + 128],
                                               identB[:, :]))
            ser.v(lambda e: e.tensor_copy(Bnat[:, :], ptrb[:, :]))
            for h in range(4):
                c0 = h // 2
                p0 = (h % 2) * 64
                if t > 0:
                    ser.t(lambda e, h=h, t0=t0: e.matmul(
                        ptf[:1, :1], eye4[:, h:h + 1], ag4[:4, t0 - 1:t0],
                        start=True, stop=True))
                    ser.v(lambda e: e.tensor_copy(prevs[:, :], ptf[:1, :1]))
                ser.t(lambda e, h=h, t0=t0: e.matmul(
                    ptf[:1, :128], eye4[:, h:h + 1], ag4[:4, t0:t0 + 128],
                    start=True, stop=True))
                if t > 0:
                    ser.v(lambda e: e.tensor_scalar_sub(al[:, :], ptf[:1, :128],
                                                        prevs[:1, :1]))
                else:
                    ser.v(lambda e: e.tensor_copy(al[:, :], ptf[:1, :128]))
                ser.t(lambda e: e.matmul(ptf[:, :1], al[:, :], ones_rowf[:1, :1],
                                         start=True, stop=True))
                ser.s(lambda e: e.activation(eac[:, :], ptf[:, :1], AF.Exp))
                ser.v(lambda e: e.tensor_scalar_mul(negalT[:, :], ptf[:, :1], -1.0))
                ser.t(lambda e, h=h, t0=t0: e.matmul(
                    ptf[:, :1], dt4[:4, t0:t0 + 128], eye4[:, h:h + 1],
                    start=True, stop=True))
                ser.v(lambda e: e.tensor_copy(dtn[:, :], ptf[:, :1]))
                ser.t(lambda e: e.matmul(psB[:, :], ones_rowf[:, :], al[:, :],
                                         start=True, stop=True))
                ser.v(lambda e: e.tensor_scalar(dec_sb[:, :], psB[:, :],
                                                negalT[:, :], 0.0,
                                                ALU.add, ALU.min))
                ser.s(lambda e: e.activation(dec_sb[:, :], dec_sb[:, :], AF.Exp))
                ser.v(lambda e: e.tensor_scalar_mul(dec2[:, :], dec_sb[:, :],
                                                    dtn[:, :]))
                ser.v(lambda e: e.tensor_mul(Mt[:, :], dec2[:, :], BCm[:, :]))
                idap = (lambda e: identB[:64, :64]) if p0 == 0 else (
                    lambda e: idb2[64:128, :64])
                ser.t(lambda e, c0=c0, p0=p0, t0=t0, idap=idap: e.transpose(
                    ptrb[:, :MP], xsc[c0][p0:p0 + 64, t0:t0 + 128], idap(e)))
                ser.v(lambda e: e.tensor_copy(xnat[:, :], ptrb[:, :MP]))
                ser.t(lambda e: e.matmul(psY[:, :], Mt[:, :], xnat[:, :],
                                         start=True, stop=True))
                ser.v(lambda e, h=h, cur=cur: e.tensor_copy(Hbf[:, :],
                                                            H[h][cur][:, :]))
                ser.t(lambda e, t0=t0: e.matmul(
                    psYi[:, :], ctc[:, t0:t0 + 128], Hbf[:, :],
                    start=True, stop=True))
                ser.v(lambda e: e.tensor_scalar_mul(yv[:, :], psYi[:, :], eac[:, :]))
                ser.v(lambda e: e.tensor_add(y2[:, :], psY[:, :], yv[:, :]))
                ser.v(lambda e, h=h: e.tensor_scalar_mul(dxt[:, :], xnat[:, :],
                                                         dcol[h][:, :]))
                ser.v(lambda e, h=h: e.tensor_add(
                    yst[:, h * MP:(h + 1) * MP], dxt[:, :], y2[:, :]))
                ser.v(lambda e: e.tensor_scalar_mul(Mt[:, :], Bnat[:, :],
                                                    dec2[:, 127:128]))
                ser.t(lambda e: e.matmul(psS[:, :], Mt[:, :], xnat[:, :],
                                         start=True, stop=True))
                ser.v(lambda e: e.tensor_copy(Ssb[:, :], psS[:, :]))
                ser.s(lambda e: e.activation(eeh[:, :], al[:1, 127:128], AF.Exp))
                ser.t(lambda e: e.matmul(psE[:, :], ones_rowf[:, :], eeh[:, :],
                                         start=True, stop=True))
                ser.v(lambda e: e.tensor_copy(ebc[:, :], psE[:, :]))
                ser.v(lambda e, h=h, cur=cur, nxt=nxt: e.scalar_tensor_tensor(
                    H[h][nxt][:, :], H[h][cur][:, :], ebc[:, :], Ssb[:, :],
                    ALU.mult, ALU.add))
            ser.dma(lambda e, t0=t0: e.dma_start(
                dap(HD["cin_y"], t0 * 256, [[256, 128], [1, 256]]), yst[:, :]),
                eng="gpsimd")
        ser.flush(nc, tok)




def emit_final(nc, ser, H, identP, ones_rowfP):
    tok = H["tok"]
    NT = SC // 128
    GC = float(np.sqrt(2.0 / np.pi))
    with ExitStack() as st:
        def sb(name, shape, dtype):
            return st.enter_context(nc.sbuf_tensor(name, shape, dtype))

        def ps(name, shape, dtype):
            return st.enter_context(nc.psum_tensor(name, shape, dtype))

        gnT = sb("gnT", [128, 16 * SC], BF16)
        xn1T = sb("xn1T_sbF", [128, 16 * SC], BF16)
        hid = sb("hid", [128, 64 * 256], BF16)
        fT = sb("fT", [128, 16 * 256], BF16)
        wch = sb("wchF", [128, 2048], BF16)
        wch2 = sb("wch2F", [128, 2048], BF16)
        wch64 = sb("wch64", [128, 64 * 128], BF16)
        nwb = sb("nwb", [128, 2048], F32)
        nrow = sb("nrow", [1, 2048], F32)
        ones_rowf = sb("ones_rowfF", [1, 128], F32)
        ybf = sb("ybf", [128, 2048], BF16)
        gbf = sb("gbf", [128, 2048], BF16)
        sgate = sb("sgate", [128, 2048], BF16)
        g_f = sb("g_f", [128, 2048], F32)
        scr = sb("scrF", [128, 2048], F32)
        gnw = sb("gnw", [128, 2048], BF16)
        ident = sb("identF_", [128, 128], BF16)
        sscol = sb("sscol", [128, 1], F32)
        x2s = sb("x2s", [128, 256], F32)
        x3s = sb("x3s", [128, 256], F32)
        inn = sb("inn", [128, 256], F32)
        ths = sb("ths", [128, 256], F32)
        t3s = sb("t3s", [128, 256], F32)
        ot = sb("ot", [128, 2048], F32)
        piece = sb("pieceF", [128, 128], BF16)
        pg0 = ps("pg0", [128, 512], F32)
        pg1 = ps("pg1", [128, 512], F32)
        ptr = ps("ptrF", [128, 128], BF16)
        pbc = ps("pbcF", [128, 512], F32)

        ser.dma(lambda e: e.dma_start(ident[:, :], H["identb"].ap()))
        ser.v(lambda e: e.memset(ones_rowf[:, :], 1.0))
        ser.dma(lambda e: e.dma_start(nrow[:, :], dap(H["nrm"], 0, [[DM, 1], [1, DM]])))
        for q in range(4):
            ser.t(lambda e, q=q: e.matmul(pbc[:, :], ones_rowf[:, :],
                                          nrow[:1, q * 512:(q + 1) * 512],
                                          start=True, stop=True))
            ser.v(lambda e, q=q: e.tensor_copy(nwb[:, q * 512:(q + 1) * 512],
                                               pbc[:, :]))
        ser.dma(lambda e: e.dma_start(
            xn1T[:, :], dap(H["cin_xn1T"], 0, [[SC, 128], [128 * SC, 16], [1, SC]])))

        for i in range(NT):
            ser.dma(lambda e, i=i: e.dma_start(
                ybf[:, :], dap(H["y_own"], i * 128 * M_INTER, [[M_INTER, 128], [1, M_INTER]])))
            ser.dma(lambda e, i=i: e.dma_start(
                gbf[:, :], dap(H["gate_d"], i * 128 * M_INTER, [[M_INTER, 128], [1, M_INTER]])))
            ser.s(lambda e: e.activation(sgate[:, :], gbf[:, :], AF.Silu))
            ser.v(lambda e: e.tensor_mul(g_f[:, :], ybf[:, :], sgate[:, :]))
            ser.s(lambda e: e.activation(scr[:, :], g_f[:, :], AF.Square,
                                         accum_out=sscol[:, :]))
            ser.v(lambda e: e.tensor_scalar(sscol[:, :], sscol[:, :], 1.0 / DM,
                                            EPS, ALU.mult, ALU.add))
            ser.v(lambda e: e.reciprocal(sscol[:, :], sscol[:, :]))
            ser.s(lambda e: e.activation(sscol[:, :], sscol[:, :], AF.Sqrt))
            ser.v(lambda e: e.tensor_scalar_mul(g_f[:, :], g_f[:, :], sscol[:, :]))
            ser.v(lambda e: e.tensor_mul(gnw[:, :], g_f[:, :], nwb[:, :]))
            for f in range(16):
                ser.t(lambda e, f=f: e.transpose(
                    ptr[:, :], gnw[:, f * 128:(f + 1) * 128], ident[:, :]))
                ser.v(lambda e, f=f, i=i: e.tensor_copy(
                    gnT[:, f * SC + i * 128: f * SC + (i + 1) * 128], ptr[:, :]))
        for fo in range(16):
            ser.dma(lambda e, fo=fo: e.dma_start(
                wch[:, :], dap(H["woutT"], fo * 128, [[DM, 128], [128 * DM, 16], [1, 128]])))
            for sh in range(2):
                for fi in range(16):
                    ser.t(lambda e, fi=fi, sh=sh: e.matmul(
                        pg0[:, :], wch[:, fi * 128:(fi + 1) * 128],
                        gnT[:, fi * SC + sh * 512: fi * SC + (sh + 1) * 512],
                        start=(fi == 0), stop=(fi == 15)))
                ser.v(lambda e, sh=sh: e.tensor_copy(
                    ybf[:, sh * 512:(sh + 1) * 512], pg0[:, :]))
            ser.dma(lambda e, fo=fo: e.dma_start(
                dap(H["moT_h"], fo * 128 * SC, [[SC, 128], [1, SC]]),
                ybf[:, :1024]), eng="gpsimd")

        for qf in range(4):
            s0 = qf * 256
            for fo in range(64):
                ser.dma(lambda e, fo=fo: e.dma_start(
                    wch[:, :], dap(H["wg0"], fo * 128,
                                   [[FF1, 128], [128 * FF1, 16], [1, 128]])))
                ser.dma(lambda e, fo=fo: e.dma_start(
                    wch2[:, :], dap(H["wg1"], fo * 128,
                                    [[FF1, 128], [128 * FF1, 16], [1, 128]])))
                for fi in range(16):
                    ser.t(lambda e, fi=fi, s0=s0: e.matmul(
                        pg0[:, :256], wch[:, fi * 128:(fi + 1) * 128],
                        xn1T[:, fi * SC + s0: fi * SC + s0 + 256],
                        start=(fi == 0), stop=(fi == 15)))
                for fi in range(16):
                    ser.t(lambda e, fi=fi, s0=s0: e.matmul(
                        pg1[:, :256], wch2[:, fi * 128:(fi + 1) * 128],
                        xn1T[:, fi * SC + s0: fi * SC + s0 + 256],
                        start=(fi == 0), stop=(fi == 15)))
                ser.s(lambda e: e.activation(x2s[:, :], pg0[:, :256], AF.Square))
                ser.v(lambda e: e.tensor_mul(x3s[:, :], x2s[:, :], pg0[:, :256]))
                ser.v(lambda e: e.scalar_tensor_tensor(
                    inn[:, :], x3s[:, :], 0.044715, pg0[:, :256], ALU.mult, ALU.add))
                ser.s(lambda e: e.activation(ths[:, :], inn[:, :], AF.Tanh, scale=GC))
                ser.v(lambda e: e.tensor_scalar(ths[:, :], ths[:, :], 1.0, 0.5,
                                                ALU.add, ALU.mult))
                ser.v(lambda e: e.tensor_mul(t3s[:, :], ths[:, :], pg0[:, :256]))
                ser.v(lambda e, fo=fo: e.tensor_mul(
                    hid[:, fo * 256:(fo + 1) * 256], t3s[:, :], pg1[:, :256]))
            for do in range(16):
                ser.dma(lambda e, do=do: e.dma_start(
                    wch64[:, :], dap(H["wlin"], do * 128,
                                     [[DM, 128], [128 * DM, 64], [1, 128]])))
                for fi in range(64):
                    ser.t(lambda e, fi=fi: e.matmul(
                        pg0[:, :256], wch64[:, fi * 128:(fi + 1) * 128],
                        hid[:, fi * 256:(fi + 1) * 256],
                        start=(fi == 0), stop=(fi == 63)))
                ser.v(lambda e, do=do: e.tensor_copy(
                    fT[:, do * 256:(do + 1) * 256], pg0[:, :256]))
            for ii in range(2):
                ig = qf * 2 + ii
                ser.dma(lambda e, ig=ig: e.dma_start(
                    ot[:, :], dap(H["x1_d"], ig * 128 * DM, [[DM, 128], [1, DM]])))
                for f in range(16):
                    ser.dma(lambda e, ig=ig, f=f: e.dma_start(
                        piece[:, :], dap(H["moT_h"], f * 128 * SC + ig * 128,
                                         [[SC, 128], [1, 128]])))
                    ser.t(lambda e: e.transpose(ptr[:, :], piece[:, :], ident[:, :]))
                    ser.v(lambda e, f=f: e.tensor_add(
                        ot[:, f * 128:(f + 1) * 128], ot[:, f * 128:(f + 1) * 128],
                        ptr[:, :]))
                    ser.t(lambda e, f=f, ii=ii: e.transpose(
                        ptr[:, :], fT[:, f * 256 + ii * 128: f * 256 + (ii + 1) * 128],
                        ident[:, :]))
                    ser.v(lambda e, f=f: e.tensor_add(
                        ot[:, f * 128:(f + 1) * 128], ot[:, f * 128:(f + 1) * 128],
                        ptr[:, :]))
                ser.dma(lambda e, ig=ig: e.dma_start(
                    dap(H["o_out"], ig * 128 * DM, [[DM, 128], [1, DM]]), ot[:, :]),
                    eng="gpsimd")
        ser.flush(nc, tok)




def kernel_fused(**inp):
    g = {k: np.asarray(v) for k, v in inp.items()}
    x0 = f32(g["L0_attn_hidden_states"].reshape(S, DM))
    wqkv = g["L0_attn_Wqkv_w"]
    b_all = f32(g["L0_attn_Wqkv_b"])
    identb_np = bf(np.eye(128))
    masksB = np.zeros((128, 4 * 512), np.float32)
    for off in range(4):
        for kj in range(128):
            masksB[kj, off * 512 + off * 128 + kj:(off + 1) * 512] = 1.0
    cwr = f32(g["L1_attn_conv_w"]).reshape(18, 128, MK)
    cbr = f32(g["L1_attn_conv_b"]).reshape(18, 128)
    ut01 = np.ascontiguousarray(np.triu(np.ones((128, 128), np.float32)))
    idb2 = np.zeros((128, 64), np.float32)
    for i in range(64):
        idb2[64 + i, i] = 1.0
    win = np.asarray(g["L1_attn_W_in"])

    ncF = build_FUSED()
    im = []
    for c in range(NC):
        gq = c // 2
        wq1 = wqkv[c * 256:c * 256 + 128]
        wq2 = wqkv[c * 256 + 128:c * 256 + 256]
        wk1 = wqkv[2048 + gq * 256:2048 + gq * 256 + 128]
        wk2 = wqkv[2048 + gq * 256 + 128:2048 + gq * 256 + 256]
        wqk_sl = bf(np.concatenate([wq1, wq2, wk1, wk2], axis=0).T)
        bqk_sl = f32(np.concatenate([
            b_all[c * 256:c * 256 + 256],
            b_all[2048 + gq * 256:2048 + gq * 256 + 256]]))
        wv_sl = bf(wqkv[3072 + gq * 256:3072 + (gq + 1) * 256].T)
        bv_sl = f32(b_all[3072 + gq * 256:3072 + (gq + 1) * 256])
        win_rows = np.concatenate([
            win[M_INTER + c * 256: M_INTER + (c + 1) * 256],
            win[M_INTER + 2048:M_INTER + 2048 + 128],
            win[M_INTER + 2048 + 128:M_INTER + 2048 + 256],
            win[M_INTER + CONV_DIM + 4 * c: M_INTER + CONV_DIM + 4 * c + 4]],
            axis=0)
        win_sl = bf(win_rows.T)
        offs_np = np.array([[c * SC, c * SC * 256, 0, 0]], np.int32)
        im.append({
            "x0f": x0, "x0s": x0[c * SC:(c + 1) * SC], "offs": offs_np,
            "wqk_sl": wqk_sl, "bqk_sl": bqk_sl, "wv_sl": wv_sl, "bv_sl": bv_sl,
            "lq1": f32(g["L0_attn_lambda_q1"]), "lk1": f32(g["L0_attn_lambda_k1"]),
            "lq2": f32(g["L0_attn_lambda_q2"]), "lk2": f32(g["L0_attn_lambda_k2"]),
            "subln": f32(g["L0_attn_subln_w"]).reshape(2, 128),
            "masksB": bf(masksB),
            "woT": bf(g["L0_attn_out_w"].T), "outb": f32(g["L0_attn_out_b"]),
            "gwT": bf(g["L0_ffn_gate_w"].T), "uwT": bf(g["L0_ffn_up_w"].T),
            "gs": f32(g["L0_ffn_gate_s"]), "us": f32(g["L0_ffn_up_s"]),
            "dwT": bf(g["L0_ffn_down_w"].T), "ds": f32(g["L0_ffn_down_s"]),
            "winT": bf(win.T), "win_sl": win_sl,
            "cw0": cwr[2 * c], "cw1": cwr[2 * c + 1], "cw2": cwr[16], "cw3": cwr[17],
            "cb0": cbr[2 * c], "cb1": cbr[2 * c + 1], "cb2": cbr[16], "cb3": cbr[17],
            "dtb": f32(g["L1_attn_dt_bias"][4 * c:4 * c + 4]),
            "alog": f32(g["L1_attn_A_log"][4 * c:4 * c + 4]),
            "dpar": f32(g["L1_attn_D_param"][4 * c:4 * c + 4]),
            "ut01": ut01, "eye4d": np.eye(4, dtype=np.float32),
            "identb": identb_np, "idb2d": bf(idb2),
            "nrm": f32(g["L1_attn_norm_w"]),
            "woutT": bf(g["L1_attn_W_out"].T),
            "wg0": bf(g["L1_ffn_w_gating"][0]), "wg1": bf(g["L1_ffn_w_gating"][1]),
            "wlin": bf(g["L1_ffn_w_linear"]),
        })
    r = run_bass_kernel_spmd(ncF, im, core_ids=list(range(NC))).results
    out = np.concatenate([r[c]["out"] for c in range(NC)], axis=0)
    return out.reshape(1, S, DM).astype(np.float32)


kernel = kernel_fused


# version 04
# speedup vs baseline: 1.3117x; 1.3117x over previous
import numpy as np
import ml_dtypes
from contextlib import ExitStack

import concourse.bass as bass
import concourse.mybir as mybir
from concourse.bass_utils import run_bass_kernel_spmd

F32 = mybir.dt.float32
BF16 = mybir.dt.bfloat16
AF = mybir.ActivationFunctionType
ALU = mybir.AluOpType
AX = mybir.AxisListType

BF = ml_dtypes.bfloat16

S = 8192
DM = 2048
NC = 8
SC = S // NC
H2 = 8
DH = 128
LAMBDA_INIT = 0.8 - 0.6 * np.exp(-0.3 * 1.0)
SCALE = DH ** -0.5
INTER0 = 5632
MH = 32
MP = 64
MN = 128
MK = 4
M_INTER = 2048
CONV_DIM = M_INTER + 2 * MN
PROJ = M_INTER + CONV_DIM + MH
CHUNK = 128
NCHUNK = S // CHUNK
FF1 = 8192
EPS = 1e-5


class Ser:

    def __init__(self):
        self.ops = []

    def op(self, eng, fn, inc=1, acc=False):
        self.ops.append((eng, fn, inc, acc))

    def t(self, fn, acc=False):
        self.op("tensor", fn, 1, acc)

    def v(self, fn):
        self.op("vector", fn, 1)

    def s(self, fn):
        self.op("scalar", fn, 1)

    def g(self, fn):
        self.op("gpsimd", fn, 1)

    def dma(self, fn, eng="sync"):
        self.op(eng, fn, 16)

    def flush(self, nc, tok, final=False):
        if not self.ops and not final:
            return getattr(self, "base", 0)
        base = getattr(self, "base", 0)
        pre = []
        c = base
        for (_, _, inc, _) in self.ops:
            pre.append(c)
            c += inc
        total = c
        with nc.Block() as block:
            for ename in ["sync", "tensor", "vector", "scalar", "gpsimd"]:
                ops = [(i, fn, inc, acc) for i, (e, fn, inc, acc)
                       in enumerate(self.ops) if e == ename]
                if not ops and not (final and ename == "gpsimd"):
                    continue

                def make(ops, ename):
                    def body(eng):
                        for i, fn, inc, acc in ops:
                            if pre[i] > 0 and not acc:
                                eng.wait_ge(tok, pre[i])
                            inst = fn(eng)
                            inst.then_inc(tok, inc)
                        if final and ename == "gpsimd":
                            eng.wait_ge(tok, total)
                    return body

                getattr(block, ename)(make(ops, ename))
        self.base = total
        self.ops = []
        return total

    def build(self, nc, tok):
        return self.flush(nc, tok, final=True)


def dap(h, off, dims):
    return bass.AP(h, off, [list(d) for d in dims])


def bf(a):
    return np.ascontiguousarray(np.asarray(a).astype(BF))


def f32(a):
    return np.ascontiguousarray(np.asarray(a).astype(np.float32))



def emit_layernorm(ser, x_f32, xn_bf, scr_f32, m_col, v_col):
    W = 2048
    ser.v(lambda e, a=x_f32, o=m_col: e.reduce_sum(o[:, :], a[:, :], AX.X))
    ser.v(lambda e, o=m_col: e.tensor_scalar_mul(o[:, :], o[:, :], -1.0 / W))
    ser.v(lambda e, a=x_f32, o=scr_f32, m=m_col:
          e.tensor_scalar_add(o[:, :], a[:, :], m[:, :]))
    ser.s(lambda e, a=scr_f32, o=x_f32, acc=v_col:
          e.activation(o[:, :], a[:, :], AF.Square, accum_out=acc[:, :]))
    ser.v(lambda e, o=v_col: e.tensor_scalar(o[:, :], o[:, :], 1.0 / W, EPS,
                                             ALU.mult, ALU.add))
    ser.v(lambda e, o=v_col: e.reciprocal(o[:, :], o[:, :]))
    ser.s(lambda e, o=v_col: e.activation(o[:, :], o[:, :], AF.Sqrt))
    ser.v(lambda e, a=scr_f32, o=xn_bf, r=v_col:
          e.tensor_scalar_mul(o[:, :], a[:, :], r[:, :]))



def build_A():
    nc = bass.Bass("TRN2", target_bir_lowering=False)
    x0 = nc.dram_tensor("x0", [SC, DM], F32, kind="ExternalInput")
    wqkT = nc.dram_tensor("wqkT", [DM, 3072], BF16, kind="ExternalInput")
    wvT = nc.dram_tensor("wvT", [DM, 1024], BF16, kind="ExternalInput")
    bqk = nc.dram_tensor("bqk", [3072], F32, kind="ExternalInput")
    bv = nc.dram_tensor("bv", [1024], F32, kind="ExternalInput")
    identb = nc.dram_tensor("identb", [128, 128], BF16, kind="ExternalInput")
    o_xnT = nc.dram_tensor("xnT", [DM, SC], BF16, kind="ExternalOutput")
    o_qkT = nc.dram_tensor("qkT", [3072, SC], BF16, kind="ExternalOutput")
    o_v = nc.dram_tensor("v", [SC, 1024], BF16, kind="ExternalOutput")

    ser = Ser()
    NT = SC // 128
    with ExitStack() as st:
        def sb(name, shape, dtype):
            return st.enter_context(nc.sbuf_tensor(name, shape, dtype))

        def ps(name, shape, dtype):
            return st.enter_context(nc.psum_tensor(name, shape, dtype))

        xt = sb("xt", [128, DM], F32)
        scr = sb("scr", [128, DM], F32)
        xn_bf = sb("xn_bf", [128, NT * DM], BF16)
        xnT = sb("xnT_sb", [128, 16 * SC], BF16)
        wv_sb = sb("wv_sb", [128, 16 * 1024], BF16)
        wch = sb("wch", [128, 2048], BF16)
        bcol = sb("bcol", [128, 1], F32)
        mcol = sb("mcol", [128, 1], F32)
        vcol = sb("vcol", [128, 1], F32)
        ident = sb("ident", [128, 128], BF16)
        ones_row = sb("ones_row", [1, 128], F32)
        bvrow = sb("bvrow", [1, 1024], F32)
        bvb = sb("bvb", [128, 1024], F32)
        qk_out = sb("qk_out", [128, 1024], BF16)
        v_out = sb("v_out", [128, 1024], BF16)
        pmm = ps("pmm", [128, 512], F32)
        ptr = ps("ptr", [128, 128], BF16)
        pbc = ps("pbc", [128, 512], F32)
        tok = st.enter_context(nc.semaphore("tok"))

        ser.dma(lambda e: e.dma_start(ident[:, :], identb.ap()))
        ser.v(lambda e: e.memset(ones_row[:, :], 1.0))
        ser.dma(lambda e: e.dma_start(bvrow[:, :], dap(bv, 0, [[1024, 1], [1, 1024]])))
        for half in range(2):
            ser.t(lambda e, h=half: e.matmul(pbc[:, :], ones_row[:, :],
                                             bvrow[:, h * 512:(h + 1) * 512],
                                             start=True, stop=True))
            ser.v(lambda e, h=half: e.tensor_copy(bvb[:, h * 512:(h + 1) * 512], pbc[:, :]))

        for i in range(NT):
            ser.dma(lambda e, i=i: e.dma_start(
                xt[:, :], dap(x0, i * 128 * DM, [[DM, 128], [1, DM]])))
            xnb_i = (lambda i: lambda: xn_bf[:, i * DM:(i + 1) * DM])(i)
            emit_layernorm(ser, xt, _View(xn_bf, i * DM, DM), scr, mcol, vcol)
            for f in range(16):
                ser.t(lambda e, i=i, f=f: e.transpose(
                    ptr[:, :], xn_bf[:, i * DM + f * 128: i * DM + (f + 1) * 128],
                    ident[:, :]))
                ser.v(lambda e, i=i, f=f: e.tensor_copy(
                    xnT[:, f * SC + i * 128: f * SC + i * 128 + 128], ptr[:, :]))

        for f in range(16):
            ser.dma(lambda e, f=f: e.dma_start(
                dap(o_xnT, f * 128 * SC, [[SC, 128], [1, SC]]),
                xnT[:, f * SC:(f + 1) * SC]), eng="gpsimd")

        for fo in range(24):
            ser.dma(lambda e, fo=fo: e.dma_start(
                wch[:, :], dap(wqkT, fo * 128,
                               [[3072, 128], [128 * 3072, 16], [1, 128]])))
            ser.dma(lambda e, fo=fo: e.dma_start(
                bcol[:, :], dap(bqk, fo * 128, [[1, 128], [1, 1]])))
            for sh in range(2):
                for fi in range(16):
                    ser.t(lambda e, fi=fi, sh=sh: e.matmul(
                        pmm[:, :], wch[:, fi * 128:(fi + 1) * 128],
                        xnT[:, fi * SC + sh * 512: fi * SC + (sh + 1) * 512],
                        start=(fi == 0), stop=(fi == 15)), acc=(fi > 0))
                if fo < 16:
                    ser.v(lambda e, sh=sh: e.tensor_scalar(
                        qk_out[:, sh * 512:(sh + 1) * 512], pmm[:, :],
                        bcol[:, :], SCALE, ALU.add, ALU.mult))
                else:
                    ser.v(lambda e, sh=sh: e.tensor_scalar_add(
                        qk_out[:, sh * 512:(sh + 1) * 512], pmm[:, :], bcol[:, :]))
            ser.dma(lambda e, fo=fo: e.dma_start(
                dap(o_qkT, fo * 128 * SC, [[SC, 128], [1, 1024]]),
                qk_out[:, :]), eng="gpsimd")

        ser.dma(lambda e: e.dma_start(
            wv_sb[:, :], dap(wvT, 0, [[1024, 128], [128 * 1024, 16], [1, 1024]])))
        for i in range(NT):
            for vh in range(2):
                for fi in range(16):
                    ser.t(lambda e, i=i, vh=vh, fi=fi: e.matmul(
                        pmm[:, :], xnT[:, fi * SC + i * 128: fi * SC + (i + 1) * 128],
                        wv_sb[:, fi * 1024 + vh * 512: fi * 1024 + (vh + 1) * 512],
                        start=(fi == 0), stop=(fi == 15)), acc=(fi > 0))
                ser.v(lambda e, vh=vh: e.tensor_add(
                    v_out[:, vh * 512:(vh + 1) * 512], pmm[:, :],
                    bvb[:, vh * 512:(vh + 1) * 512]))
            ser.dma(lambda e, i=i: e.dma_start(
                dap(o_v, i * 128 * 1024, [[1024, 128], [1, 1024]]),
                v_out[:, :]), eng="gpsimd")

        ser.build(nc, tok)
    return nc


class _View:

    def __init__(self, t, off, width):
        self.t = t
        self.off = off
        self.width = width

    def __getitem__(self, key):
        p, f = key
        if isinstance(f, slice):
            start = f.start or 0
            stop = f.stop if f.stop is not None else self.width
        else:
            raise TypeError(f)
        return self.t[p, self.off + start: self.off + stop]



def build_B():
    nc = bass.Bass("TRN2", target_bir_lowering=False)
    q1T = nc.dram_tensor("q1T", [128, S], BF16, kind="ExternalInput")
    q2T = nc.dram_tensor("q2T", [128, S], BF16, kind="ExternalInput")
    k1T = nc.dram_tensor("k1T", [128, S], BF16, kind="ExternalInput")
    k2T = nc.dram_tensor("k2T", [128, S], BF16, kind="ExternalInput")
    vg = nc.dram_tensor("vg", [S, 256], BF16, kind="ExternalInput")
    lq1 = nc.dram_tensor("lq1", [128], F32, kind="ExternalInput")
    lk1 = nc.dram_tensor("lk1", [128], F32, kind="ExternalInput")
    lq2 = nc.dram_tensor("lq2", [128], F32, kind="ExternalInput")
    lk2 = nc.dram_tensor("lk2", [128], F32, kind="ExternalInput")
    subln = nc.dram_tensor("subln", [2, 128], F32, kind="ExternalInput")
    masksB = nc.dram_tensor("masksB", [128, 4 * 512], BF16, kind="ExternalInput")
    o_oT = nc.dram_tensor("oT", [256, S], BF16, kind="ExternalOutput")

    ser = Ser()
    with ExitStack() as st:
        def sb(name, shape, dtype):
            return st.enter_context(nc.sbuf_tensor(name, shape, dtype))

        def ps(name, shape, dtype):
            return st.enter_context(nc.psum_tensor(name, shape, dtype))

        qT = [sb("q1T_sb", [128, S], BF16), sb("q2T_sb", [128, S], BF16)]
        kT = [sb("k1T_sb", [128, S], BF16), sb("k2T_sb", [128, S], BF16)]
        v_sb = sb("v_sb", [128, 64 * 256], BF16)
        masks = sb("masks", [128, 4 * 512], BF16)
        ones_colb = sb("ones_colb", [128, 1], BF16)
        ones_colf = sb("ones_colf", [128, 1], F32)
        ones_rowf = sb("ones_rowf", [1, 128], F32)
        lrow = [sb(f"lrow{i}", [1, 128], F32) for i in range(2)]
        ee = [sb(f"ee{i}", [1, 1], F32) for i in range(2)]
        lamneg = sb("lamneg", [1, 1], F32)
        sw = [sb(f"sw{i}", [128, 1], F32) for i in range(2)]
        pT = sb("pT", [128, 512], BF16)
        pTm = sb("pTm", [128, 512], BF16)
        linv = [sb(f"linv{i}", [1, 512], F32) for i in range(2)]
        bsc = [sb(f"bsc{i}", [128, 512], F32) for i in range(2)]
        oc = [sb(f"oc{i}", [128, 512], F32) for i in range(2)]
        tmp = sb("tmpB", [128, 512], F32)
        sq = [sb(f"sq{i}", [128, 512], F32) for i in range(2)]
        rrow = sb("rrow", [1, 512], F32)
        rb = sb("rb", [128, 512], F32)
        ost = sb("ost", [128, 512], BF16)
        pS = ps("pS", [128, 512], F32)
        pO = [[ps(f"pO{m}{h}", [128, 512], F32) for h in range(2)] for m in range(2)]
        pL = [ps(f"pL{m}", [1, 512], F32) for m in range(2)]
        pB = ps("pB", [128, 512], F32)
        tok = st.enter_context(nc.semaphore("tok"))

        qkin = [q1T, q2T, k1T, k2T]
        for i, dst in enumerate(qT + kT):
            ser.dma(lambda e, d=dst, src=qkin[i]: e.dma_start(d[:, :], src.ap()))
        ser.dma(lambda e: e.dma_start(
            v_sb[:, :], dap(vg, 0, [[256, 128], [128 * 256, 64], [1, 256]])))
        ser.dma(lambda e: e.dma_start(masks[:, :], masksB.ap()))
        ser.v(lambda e: e.memset(ones_colb[:, :], 1.0))
        ser.v(lambda e: e.memset(ones_colf[:, :], 1.0))
        ser.v(lambda e: e.memset(ones_rowf[:, :], 1.0))
        lsrc = [(lq1, lk1), (lq2, lk2)]
        for m in range(2):
            a, b = lsrc[m]
            ser.dma(lambda e, a=a, m=m: e.dma_start(
                lrow[m][:, :], dap(a, 0, [[128, 1], [1, 128]])))
            ser.dma(lambda e, b=b, m=m: e.dma_start(
                pT[:1, :128] if False else tmp[:1, :128], dap(b, 0, [[128, 1], [1, 128]])))
            ser.v(lambda e, m=m: e.tensor_mul(lrow[m][:, :], lrow[m][:, :], tmp[:1, :128]))
            ser.v(lambda e, m=m: e.reduce_sum(ee[m][:, :], lrow[m][:, :], AX.X))
            ser.s(lambda e, m=m: e.activation(ee[m][:, :], ee[m][:, :], AF.Exp))
        ser.v(lambda e: e.tensor_sub(lamneg[:, :], ee[1][:, :], ee[0][:, :]))
        ser.v(lambda e: e.tensor_scalar_add(lamneg[:, :], lamneg[:, :],
                                            float(-LAMBDA_INIT)))
        for h in range(2):
            ser.dma(lambda e, h=h: e.dma_start(
                sw[h][:, :], dap(subln, h * 128, [[1, 128], [1, 1]])))
            ser.v(lambda e, h=h: e.tensor_scalar_mul(sw[h][:, :], sw[h][:, :],
                                                     float(1.0 - LAMBDA_INIT)))

        for qb in range(16):
            q0 = qb * 512
            nk = (q0 + 512) // 128
            for m in range(2):
                for kt in range(nk):
                    last = (kt == nk - 1)
                    ser.t(lambda e, m=m, kt=kt, q0=q0: e.matmul(
                        pS[:, :], kT[m][:, kt * 128:(kt + 1) * 128],
                        qT[m][:, q0:q0 + 512], start=True, stop=True))
                    straddle = kt * 128 >= q0
                    if straddle:
                        off = (kt * 128 - q0) // 128
                        ser.s(lambda e: e.activation(pTm[:, :], pS[:, :], AF.Exp))
                        ser.v(lambda e, off=off: e.tensor_mul(
                            pT[:, :], pTm[:, :], masks[:, off * 512:(off + 1) * 512]))
                    else:
                        ser.s(lambda e: e.activation(pT[:, :], pS[:, :], AF.Exp))
                    ser.t(lambda e, m=m, kt=kt, last=last: e.matmul(
                        pL[m][:, :], ones_colb[:, :], pT[:, :],
                        start=(kt == 0), stop=last))
                    for h in range(2):
                        ser.t(lambda e, m=m, h=h, kt=kt, last=last: e.matmul(
                            pO[m][h][:, :],
                            v_sb[:, kt * 256 + h * 128: kt * 256 + (h + 1) * 128],
                            pT[:, :], start=(kt == 0), stop=last))
            for m in range(2):
                ser.v(lambda e, m=m: e.reciprocal(linv[m][:, :], pL[m][:, :]))
            ser.v(lambda e: e.tensor_scalar_mul(linv[1][:, :], linv[1][:, :],
                                                lamneg[:1, :1]))
            for m in range(2):
                ser.t(lambda e, m=m: e.matmul(pB[:, :], ones_rowf[:, :],
                                              linv[m][:, :], start=True, stop=True))
                ser.v(lambda e, m=m: e.tensor_copy(bsc[m][:, :], pB[:, :]))
            for h in range(2):
                ser.v(lambda e, h=h: e.tensor_mul(oc[h][:, :], pO[0][h][:, :],
                                                  bsc[0][:, :]))
                ser.v(lambda e, h=h: e.tensor_mul(tmp[:, :], pO[1][h][:, :],
                                                  bsc[1][:, :]))
                ser.v(lambda e, h=h: e.tensor_add(oc[h][:, :], oc[h][:, :],
                                                  tmp[:, :]))
                ser.s(lambda e, h=h: e.activation(sq[h][:, :], oc[h][:, :],
                                                  AF.Square))
                ser.t(lambda e, h=h: e.matmul(pL[0][:, :], ones_colf[:, :],
                                              sq[h][:, :], start=(h == 0),
                                              stop=(h == 1)))
            ser.v(lambda e: e.tensor_scalar(rrow[:, :], pL[0][:, :], 1.0 / 256,
                                            EPS, ALU.mult, ALU.add))
            ser.v(lambda e: e.reciprocal(rrow[:, :], rrow[:, :]))
            ser.s(lambda e: e.activation(rrow[:, :], rrow[:, :], AF.Sqrt))
            ser.t(lambda e: e.matmul(pB[:, :], ones_rowf[:, :], rrow[:, :],
                                     start=True, stop=True))
            ser.v(lambda e: e.tensor_copy(rb[:, :], pB[:, :]))
            for h in range(2):
                ser.v(lambda e, h=h: e.scalar_tensor_tensor(
                    ost[:, :], oc[h][:, :], sw[h][:, :], rb[:, :],
                    ALU.mult, ALU.mult))
                ser.dma(lambda e, h=h, q0=q0: e.dma_start(
                    dap(o_oT, h * 128 * S + q0, [[S, 128], [1, 512]]),
                    ost[:, :]), eng="gpsimd")

        ser.build(nc, tok)
    return nc



def build_CD():
    nc = bass.Bass("TRN2", target_bir_lowering=False)
    oTs = nc.dram_tensor("oTs", [DM, SC], BF16, kind="ExternalInput")
    woT = nc.dram_tensor("woT", [DM, DM], BF16, kind="ExternalInput")
    outb = nc.dram_tensor("outb", [DM], F32, kind="ExternalInput")
    xn0Ts = nc.dram_tensor("xn0Ts", [DM, SC], BF16, kind="ExternalInput")
    gwT = nc.dram_tensor("gwT", [DM, INTER0], BF16, kind="ExternalInput")
    uwT = nc.dram_tensor("uwT", [DM, INTER0], BF16, kind="ExternalInput")
    gs = nc.dram_tensor("gs", [INTER0], F32, kind="ExternalInput")
    us = nc.dram_tensor("us", [INTER0], F32, kind="ExternalInput")
    dwT = nc.dram_tensor("dwT", [INTER0, DM], BF16, kind="ExternalInput")
    ds = nc.dram_tensor("ds", [DM], F32, kind="ExternalInput")
    x0 = nc.dram_tensor("x0", [SC, DM], F32, kind="ExternalInput")
    winT = nc.dram_tensor("winT", [DM, PROJ], BF16, kind="ExternalInput")
    identb = nc.dram_tensor("identb", [128, 128], BF16, kind="ExternalInput")
    attnT_h = nc.dram_tensor("attnT_h", [DM, SC], BF16)
    o_x1 = nc.dram_tensor("x1", [SC, DM], F32, kind="ExternalOutput")
    o_xn1T = nc.dram_tensor("xn1T", [DM, SC], BF16, kind="ExternalOutput")
    o_gate = nc.dram_tensor("gate", [SC, M_INTER], BF16, kind="ExternalOutput")
    o_hbct = nc.dram_tensor("hbct", [CONV_DIM, SC], BF16, kind="ExternalOutput")
    o_dtt = nc.dram_tensor("dtt", [MH, SC], F32, kind="ExternalOutput")

    ser = Ser()
    NT = SC // 128
    with ExitStack() as st:
        def sb(name, shape, dtype):
            return st.enter_context(nc.sbuf_tensor(name, shape, dtype))

        def ps(name, shape, dtype):
            return st.enter_context(nc.psum_tensor(name, shape, dtype))

        big_a = sb("big_a", [128, 16 * SC], BF16)
        big_b = sb("big_b", [128, 16 * SC], BF16)
        hbig = sb("hbig", [128, 44 * SC], BF16)
        wch = sb("wch", [128, 2048], BF16)
        wch2 = sb("wch2", [128, 2048], BF16)
        wch44 = sb("wch44", [128, 44 * 128], BF16)
        scol = sb("scol", [128, 1], F32)
        scol2 = sb("scol2", [128, 1], F32)
        mcol = sb("mcol", [128, 1], F32)
        vcol = sb("vcol", [128, 1], F32)
        ident = sb("ident", [128, 128], BF16)
        tmpf = sb("tmpf", [128, 512], F32)
        tmpf2 = sb("tmpf2", [128, 512], F32)
        tmpf3 = sb("tmpf3", [128, 512], F32)
        st_ch = sb("st_ch", [128, 1024], BF16)
        dt_st = sb("dt_st", [32, 1024], F32)
        piece = sb("piece", [128, 128], BF16)
        x1t = sb("x1t", [128, DM], F32)
        scr = sb("scr", [128, DM], F32)
        xnbf = sb("xnbf", [128, DM], BF16)
        pg = ps("pg", [128, 512], F32)
        pu = ps("pu", [128, 512], F32)
        ptr = ps("ptr", [128, 128], BF16)
        tok = st.enter_context(nc.semaphore("tok"))

        ser.dma(lambda e: e.dma_start(ident[:, :], identb.ap()))

        ser.dma(lambda e: e.dma_start(
            big_a[:, :], dap(oTs, 0, [[SC, 128], [128 * SC, 16], [1, SC]])))
        for fo in range(16):
            ser.dma(lambda e, fo=fo: e.dma_start(
                wch[:, :], dap(woT, fo * 128, [[DM, 128], [128 * DM, 16], [1, 128]])))
            ser.dma(lambda e, fo=fo: e.dma_start(
                scol[:, :], dap(outb, fo * 128, [[1, 128], [1, 1]])))
            for sh in range(2):
                for fi in range(16):
                    ser.t(lambda e, fi=fi, sh=sh: e.matmul(
                        pg[:, :], wch[:, fi * 128:(fi + 1) * 128],
                        big_a[:, fi * SC + sh * 512: fi * SC + (sh + 1) * 512],
                        start=(fi == 0), stop=(fi == 15)), acc=(fi > 0))
                ser.v(lambda e, sh=sh: e.tensor_scalar_add(
                    st_ch[:, sh * 512:(sh + 1) * 512], pg[:, :], scol[:, :]))
            ser.dma(lambda e, fo=fo: e.dma_start(
                dap(attnT_h, fo * 128 * SC, [[SC, 128], [1, SC]]),
                st_ch[:, :]), eng="gpsimd")

        ser.dma(lambda e: e.dma_start(
            big_a[:, :], dap(xn0Ts, 0, [[SC, 128], [128 * SC, 16], [1, SC]])))
        for fo in range(44):
            ser.dma(lambda e, fo=fo: e.dma_start(
                wch[:, :], dap(gwT, fo * 128,
                               [[INTER0, 128], [128 * INTER0, 16], [1, 128]])))
            ser.dma(lambda e, fo=fo: e.dma_start(
                wch2[:, :], dap(uwT, fo * 128,
                                [[INTER0, 128], [128 * INTER0, 16], [1, 128]])))
            ser.dma(lambda e, fo=fo: e.dma_start(
                scol[:, :], dap(gs, fo * 128, [[1, 128], [1, 1]])))
            ser.dma(lambda e, fo=fo: e.dma_start(
                scol2[:, :], dap(us, fo * 128, [[1, 128], [1, 1]])))
            for sh in range(2):
                for fi in range(16):
                    ser.t(lambda e, fi=fi, sh=sh: e.matmul(
                        pg[:, :], wch[:, fi * 128:(fi + 1) * 128],
                        big_a[:, fi * SC + sh * 512: fi * SC + (sh + 1) * 512],
                        start=(fi == 0), stop=(fi == 15)), acc=(fi > 0))
                for fi in range(16):
                    ser.t(lambda e, fi=fi, sh=sh: e.matmul(
                        pu[:, :], wch2[:, fi * 128:(fi + 1) * 128],
                        big_a[:, fi * SC + sh * 512: fi * SC + (sh + 1) * 512],
                        start=(fi == 0), stop=(fi == 15)), acc=(fi > 0))
                ser.v(lambda e: e.tensor_scalar_mul(tmpf[:, :], pg[:, :], scol[:, :]))
                ser.s(lambda e: e.activation(tmpf2[:, :], tmpf[:, :], AF.Silu))
                ser.v(lambda e: e.tensor_scalar_mul(tmpf3[:, :], pu[:, :], scol2[:, :]))
                ser.v(lambda e, fo=fo, sh=sh: e.tensor_mul(
                    hbig[:, fo * SC + sh * 512: fo * SC + (sh + 1) * 512],
                    tmpf2[:, :], tmpf3[:, :]))
        for do in range(16):
            ser.dma(lambda e, do=do: e.dma_start(
                wch44[:, :], dap(dwT, do * 128, [[DM, 128], [128 * DM, 44], [1, 128]])))
            ser.dma(lambda e, do=do: e.dma_start(
                scol[:, :], dap(ds, do * 128, [[1, 128], [1, 1]])))
            for sh in range(2):
                for fi in range(44):
                    ser.t(lambda e, fi=fi, sh=sh: e.matmul(
                        pg[:, :], wch44[:, fi * 128:(fi + 1) * 128],
                        hbig[:, fi * SC + sh * 512: fi * SC + (sh + 1) * 512],
                        start=(fi == 0), stop=(fi == 43)), acc=(fi > 0))
                ser.v(lambda e, do=do, sh=sh: e.tensor_scalar_mul(
                    big_b[:, do * SC + sh * 512: do * SC + (sh + 1) * 512],
                    pg[:, :], scol[:, :]))

        for i in range(NT):
            ser.dma(lambda e, i=i: e.dma_start(
                x1t[:, :], dap(x0, i * 128 * DM, [[DM, 128], [1, DM]])))
            for f in range(16):
                ser.dma(lambda e, i=i, f=f: e.dma_start(
                    piece[:, :], dap(attnT_h, f * 128 * SC + i * 128,
                                     [[SC, 128], [1, 128]])))
                ser.t(lambda e: e.transpose(ptr[:, :], piece[:, :], ident[:, :]))
                ser.v(lambda e, f=f: e.tensor_add(
                    x1t[:, f * 128:(f + 1) * 128], x1t[:, f * 128:(f + 1) * 128],
                    ptr[:, :]))
                ser.t(lambda e, i=i, f=f: e.transpose(
                    ptr[:, :], big_b[:, f * SC + i * 128: f * SC + (i + 1) * 128],
                    ident[:, :]))
                ser.v(lambda e, f=f: e.tensor_add(
                    x1t[:, f * 128:(f + 1) * 128], x1t[:, f * 128:(f + 1) * 128],
                    ptr[:, :]))
            ser.dma(lambda e, i=i: e.dma_start(
                dap(o_x1, i * 128 * DM, [[DM, 128], [1, DM]]), x1t[:, :]),
                eng="gpsimd")
            emit_layernorm(ser, x1t, xnbf, scr, mcol, vcol)
            for f in range(16):
                ser.t(lambda e, f=f: e.transpose(
                    ptr[:, :], xnbf[:, f * 128:(f + 1) * 128], ident[:, :]))
                ser.v(lambda e: e.tensor_copy(piece[:, :], ptr[:, :]))
                ser.dma(lambda e, i=i, f=f: e.dma_start(
                    dap(o_xn1T, f * 128 * SC + i * 128, [[SC, 128], [1, 128]]),
                    piece[:, :]), eng="gpsimd")

        ser.dma(lambda e: e.dma_start(
            big_a[:, :], dap(o_xn1T, 0, [[SC, 128], [128 * SC, 16], [1, SC]])))
        for fo in range(35):
            w = 128 if fo < 34 else 32
            ser.dma(lambda e, fo=fo, w=w: e.dma_start(
                wch[:, :16 * w], dap(winT, fo * 128,
                                     [[PROJ, 128], [128 * PROJ, 16], [1, w]])))
            for sh in range(2):
                for fi in range(16):
                    ser.t(lambda e, fi=fi, sh=sh, w=w: e.matmul(
                        pg[:w, :], wch[:, fi * w:(fi + 1) * w],
                        big_a[:, fi * SC + sh * 512: fi * SC + (sh + 1) * 512],
                        start=(fi == 0), stop=(fi == 15)), acc=(fi > 0))
                if fo < 16:
                    ser.v(lambda e, sh=sh: e.tensor_copy(
                        st_ch[:, sh * 512:(sh + 1) * 512], pg[:, :]))
                elif fo < 34:
                    ser.v(lambda e, sh=sh: e.tensor_copy(
                        st_ch[:, sh * 512:(sh + 1) * 512], pg[:, :]))
                else:
                    ser.v(lambda e, sh=sh: e.tensor_copy(
                        dt_st[:, sh * 512:(sh + 1) * 512], pg[:32, :]))
            if fo < 16:
                for i in range(NT):
                    ser.t(lambda e, i=i: e.transpose(
                        ptr[:, :], st_ch[:, i * 128:(i + 1) * 128], ident[:, :]))
                    ser.v(lambda e: e.tensor_copy(piece[:, :], ptr[:, :]))
                    ser.dma(lambda e, i=i, fo=fo: e.dma_start(
                        dap(o_gate, i * 128 * M_INTER + fo * 128,
                            [[M_INTER, 128], [1, 128]]), piece[:, :]),
                        eng="gpsimd")
            elif fo < 34:
                ser.dma(lambda e, fo=fo: e.dma_start(
                    dap(o_hbct, (fo - 16) * 128 * SC, [[SC, 128], [1, SC]]),
                    st_ch[:, :]), eng="gpsimd")
            else:
                ser.dma(lambda e: e.dma_start(
                    dap(o_dtt, 0, [[SC, 32], [1, SC]]), dt_st[:, :]),
                    eng="gpsimd")

        ser.build(nc, tok)
    return nc



def build_E():
    nc = bass.Bass("TRN2", target_bir_lowering=False)
    xs_in = [nc.dram_tensor(f"xs{j}", [128, S], BF16, kind="ExternalInput")
             for j in range(2)]
    bt_in = nc.dram_tensor("bt_in", [128, S], BF16, kind="ExternalInput")
    ct_in = nc.dram_tensor("ct_in", [128, S], BF16, kind="ExternalInput")
    cw = [nc.dram_tensor(f"cw{j}", [128, MK], F32, kind="ExternalInput")
          for j in range(4)]
    cb = [nc.dram_tensor(f"cb{j}", [128], F32, kind="ExternalInput")
          for j in range(4)]
    dtr = nc.dram_tensor("dtr", [4, S], F32, kind="ExternalInput")
    dtb = nc.dram_tensor("dtb", [4], F32, kind="ExternalInput")
    alog = nc.dram_tensor("alog", [4], F32, kind="ExternalInput")
    dpar = nc.dram_tensor("dpar", [4], F32, kind="ExternalInput")
    ut01 = nc.dram_tensor("ut01", [128, 128], F32, kind="ExternalInput")
    eye4d = nc.dram_tensor("eye4d", [4, 4], F32, kind="ExternalInput")
    identb = nc.dram_tensor("identb", [128, 128], BF16, kind="ExternalInput")
    idb2d = nc.dram_tensor("idb2d", [128, 64], BF16, kind="ExternalInput")
    o_y = nc.dram_tensor("ynat", [S, 256], BF16, kind="ExternalOutput")

    ser = Ser()
    with ExitStack() as st:
        def sb(name, shape, dtype):
            return st.enter_context(nc.sbuf_tensor(name, shape, dtype))

        def ps(name, shape, dtype):
            return st.enter_context(nc.psum_tensor(name, shape, dtype))

        xsc = [sb(f"xsc{j}", [128, S], BF16) for j in range(2)]
        btc = sb("btc", [128, S], BF16)
        ctc = sb("ctc", [128, S], BF16)
        u_t = sb("u_t", [128, S + 3], BF16)
        acc = sb("acc", [128, 2048], F32)
        dt4 = sb("dt4", [4, S], F32)
        scr4 = sb("scr4", [4, S], F32)
        ag4 = sb("ag4", [4, S], F32)
        wcol = sb("wcol", [128, MK], F32)
        bcolE = sb("bcolE", [128, 1], F32)
        identB = sb("identB", [128, 128], BF16)
        idb2 = sb("idb2", [128, 64], BF16)
        ut = sb("ut", [128, 128], F32)
        eye4 = sb("eye4", [4, 4], F32)
        ones_rowf = sb("ones_rowfE", [1, 128], F32)
        prow = sb("prow", [1, 4], F32)
        arow = sb("arow", [1, 4], F32)
        drow = sb("drow", [1, 4], F32)
        acol = sb("acol", [4, 1], F32)
        bcol4 = sb("bcol4", [4, 1], F32)
        dcol = [sb(f"dcol{h}", [128, 1], F32) for h in range(4)]
        H = [[sb(f"H{h}_{pp}", [128, MP], F32) for pp in range(2)] for h in range(4)]
        Hbf = sb("Hbf", [128, MP], BF16)
        al = sb("al", [1, 128], F32)
        prevs = sb("prevs", [1, 1], F32)
        negalT = sb("negalT", [128, 1], F32)
        eac = sb("eac", [128, 1], F32)
        dtn = sb("dtn", [128, 1], F32)
        eeh = sb("eeh", [1, 1], F32)
        ebc = sb("ebc", [128, 1], F32)
        BCm = sb("BCm", [128, 128], F32)
        Bnat = sb("Bnat", [128, 128], BF16)
        dec_sb = sb("dec_sb", [128, 128], F32)
        dec2 = sb("dec2", [128, 128], F32)
        Mt = sb("Mt", [128, 128], BF16)
        xnat = sb("xnat", [128, MP], BF16)
        yv = sb("yv", [128, MP], F32)
        y2 = sb("y2", [128, MP], F32)
        dxt = sb("dxt", [128, MP], F32)
        Ssb = sb("Ssb", [128, MP], F32)
        yst = sb("yst", [128, 256], BF16)
        psA = ps("psA", [128, 128], F32)
        psB = ps("psB", [128, 128], F32)
        ptrb = ps("ptrb", [128, 128], BF16)
        ptf = ps("ptf", [128, 128], F32)
        psY = ps("psY", [128, MP], F32)
        psYi = ps("psYi", [128, MP], F32)
        psS = ps("psS", [128, MP], F32)
        psE = ps("psE", [128, 1], F32)
        tok = st.enter_context(nc.semaphore("tok"))

        ser.dma(lambda e: e.dma_start(identB[:, :], identb.ap()))
        ser.dma(lambda e: e.dma_start(idb2[:, :], idb2d.ap()))
        ser.dma(lambda e: e.dma_start(ut[:, :], ut01.ap()))
        ser.dma(lambda e: e.dma_start(eye4[:, :], eye4d.ap()))
        ser.v(lambda e: e.memset(ones_rowf[:, :], 1.0))

        conv_src = [xs_in[0], xs_in[1], bt_in, ct_in]
        conv_dst = [xsc[0], xsc[1], btc, ctc]
        for j in range(4):
            ser.v(lambda e: e.memset(u_t[:, :3], 0.0))
            ser.dma(lambda e, j=j: e.dma_start(u_t[:, 3:], conv_src[j].ap()))
            ser.dma(lambda e, j=j: e.dma_start(wcol[:, :], cw[j].ap()))
            ser.dma(lambda e, j=j: e.dma_start(
                bcolE[:, :], dap(cb[j], 0, [[1, 128], [1, 1]])))
            for q in range(4):
                q0 = q * 2048
                ser.v(lambda e, q0=q0: e.tensor_scalar_mul(
                    acc[:, :], u_t[:, q0:q0 + 2048], wcol[:, 0:1]))
                for kk in range(1, MK):
                    ser.v(lambda e, q0=q0, kk=kk: e.scalar_tensor_tensor(
                        acc[:, :], u_t[:, q0 + kk:q0 + kk + 2048],
                        wcol[:, kk:kk + 1], acc[:, :], ALU.mult, ALU.add))
                ser.s(lambda e, j=j, q0=q0: e.activation(
                    conv_dst[j][:, q0:q0 + 2048], acc[:, :], AF.Silu,
                    bias=bcolE[:, :]))

        ser.dma(lambda e: e.dma_start(prow[:, :], dap(dtb, 0, [[4, 1], [1, 4]])))
        ser.dma(lambda e: e.dma_start(arow[:, :], dap(alog, 0, [[4, 1], [1, 4]])))
        ser.dma(lambda e: e.dma_start(drow[:, :], dap(dpar, 0, [[4, 1], [1, 4]])))
        ser.dma(lambda e: e.dma_start(dt4[:, :], dtr.ap()))
        ser.s(lambda e: e.activation(arow[:, :], arow[:, :], AF.Exp))
        ser.v(lambda e: e.tensor_scalar_mul(arow[:, :], arow[:, :], -1.0))
        ser.t(lambda e: e.matmul(psE[:4, :1], arow[:, :], ones_rowf[:1, :1],
                                 start=True, stop=True))
        ser.v(lambda e: e.tensor_copy(acol[:, :], psE[:4, :1]))
        ser.t(lambda e: e.matmul(psE[:4, :1], prow[:, :], ones_rowf[:1, :1],
                                 start=True, stop=True))
        ser.v(lambda e: e.tensor_copy(bcol4[:, :], psE[:4, :1]))
        ser.v(lambda e: e.tensor_scalar_add(dt4[:, :], dt4[:, :], bcol4[:, :]))
        ser.s(lambda e: e.activation(scr4[:, :], dt4[:, :], AF.Abs))
        ser.s(lambda e: e.activation(scr4[:, :], scr4[:, :], AF.Exp, scale=-1.0))
        ser.s(lambda e: e.activation(scr4[:, :], scr4[:, :], AF.Ln, bias=1.0))
        ser.v(lambda e: e.tensor_scalar_max(dt4[:, :], dt4[:, :], 0.0))
        ser.v(lambda e: e.tensor_add(dt4[:, :], dt4[:, :], scr4[:, :]))
        ser.v(lambda e: e.tensor_scalar_mul(scr4[:, :], dt4[:, :], acol[:, :]))
        ser.v(lambda e: e.tensor_tensor_scan(ag4[:, :], scr4[:, :], scr4[:, :],
                                             0.0, ALU.add, ALU.bypass))
        for h in range(4):
            ser.t(lambda e, h=h: e.matmul(psE[:, :], ones_rowf[:, :],
                                          drow[:1, h:h + 1], start=True, stop=True))
            ser.v(lambda e, h=h: e.tensor_copy(dcol[h][:, :], psE[:, :]))
            ser.v(lambda e, h=h: e.memset(H[h][0][:, :], 0.0))

        for t in range(NCHUNK):
            t0 = t * 128
            cur, nxt = t % 2, (t + 1) % 2
            ser.t(lambda e, t0=t0: e.matmul(psA[:, :], btc[:, t0:t0 + 128],
                                            ctc[:, t0:t0 + 128], start=True, stop=True))
            ser.v(lambda e: e.tensor_mul(BCm[:, :], psA[:, :], ut[:, :]))
            ser.t(lambda e, t0=t0: e.transpose(ptrb[:, :], btc[:, t0:t0 + 128],
                                               identB[:, :]))
            ser.v(lambda e: e.tensor_copy(Bnat[:, :], ptrb[:, :]))
            for h in range(4):
                c0 = h // 2
                p0 = (h % 2) * 64
                if t > 0:
                    ser.t(lambda e, h=h, t0=t0: e.matmul(
                        ptf[:1, :1], eye4[:, h:h + 1], ag4[:4, t0 - 1:t0],
                        start=True, stop=True))
                    ser.v(lambda e: e.tensor_copy(prevs[:, :], ptf[:1, :1]))
                ser.t(lambda e, h=h, t0=t0: e.matmul(
                    ptf[:1, :128], eye4[:, h:h + 1], ag4[:4, t0:t0 + 128],
                    start=True, stop=True))
                if t > 0:
                    ser.v(lambda e: e.tensor_scalar_sub(al[:, :], ptf[:1, :128],
                                                        prevs[:1, :1]))
                else:
                    ser.v(lambda e: e.tensor_copy(al[:, :], ptf[:1, :128]))
                ser.t(lambda e: e.matmul(ptf[:, :1], al[:, :], ones_rowf[:1, :1],
                                         start=True, stop=True))
                ser.s(lambda e: e.activation(eac[:, :], ptf[:, :1], AF.Exp))
                ser.v(lambda e: e.tensor_scalar_mul(negalT[:, :], ptf[:, :1], -1.0))
                ser.t(lambda e, h=h, t0=t0: e.matmul(
                    ptf[:, :1], dt4[:4, t0:t0 + 128], eye4[:, h:h + 1],
                    start=True, stop=True))
                ser.v(lambda e: e.tensor_copy(dtn[:, :], ptf[:, :1]))
                ser.t(lambda e: e.matmul(psB[:, :], ones_rowf[:, :], al[:, :],
                                         start=True, stop=True))
                ser.v(lambda e: e.tensor_scalar(dec_sb[:, :], psB[:, :],
                                                negalT[:, :], 0.0,
                                                ALU.add, ALU.min))
                ser.s(lambda e: e.activation(dec_sb[:, :], dec_sb[:, :], AF.Exp))
                ser.v(lambda e: e.tensor_scalar_mul(dec2[:, :], dec_sb[:, :],
                                                    dtn[:, :]))
                ser.v(lambda e: e.tensor_mul(Mt[:, :], dec2[:, :], BCm[:, :]))
                idap = (lambda e: identB[:64, :64]) if p0 == 0 else (
                    lambda e: idb2[64:128, :64])
                ser.t(lambda e, c0=c0, p0=p0, t0=t0, idap=idap: e.transpose(
                    ptrb[:, :MP], xsc[c0][p0:p0 + 64, t0:t0 + 128], idap(e)))
                ser.v(lambda e: e.tensor_copy(xnat[:, :], ptrb[:, :MP]))
                ser.t(lambda e: e.matmul(psY[:, :], Mt[:, :], xnat[:, :],
                                         start=True, stop=True))
                ser.v(lambda e, h=h, cur=cur: e.tensor_copy(Hbf[:, :],
                                                            H[h][cur][:, :]))
                ser.t(lambda e, t0=t0: e.matmul(
                    psYi[:, :], ctc[:, t0:t0 + 128], Hbf[:, :],
                    start=True, stop=True))
                ser.v(lambda e: e.tensor_scalar_mul(yv[:, :], psYi[:, :], eac[:, :]))
                ser.v(lambda e: e.tensor_add(y2[:, :], psY[:, :], yv[:, :]))
                ser.v(lambda e, h=h: e.tensor_scalar_mul(dxt[:, :], xnat[:, :],
                                                         dcol[h][:, :]))
                ser.v(lambda e, h=h: e.tensor_add(
                    yst[:, h * MP:(h + 1) * MP], dxt[:, :], y2[:, :]))
                ser.v(lambda e: e.tensor_scalar_mul(Mt[:, :], Bnat[:, :],
                                                    dec2[:, 127:128]))
                ser.t(lambda e: e.matmul(psS[:, :], Mt[:, :], xnat[:, :],
                                         start=True, stop=True))
                ser.v(lambda e: e.tensor_copy(Ssb[:, :], psS[:, :]))
                ser.s(lambda e: e.activation(eeh[:, :], al[:1, 127:128], AF.Exp))
                ser.t(lambda e: e.matmul(psE[:, :], ones_rowf[:, :], eeh[:, :],
                                         start=True, stop=True))
                ser.v(lambda e: e.tensor_copy(ebc[:, :], psE[:, :]))
                ser.v(lambda e, h=h, cur=cur, nxt=nxt: e.scalar_tensor_tensor(
                    H[h][nxt][:, :], H[h][cur][:, :], ebc[:, :], Ssb[:, :],
                    ALU.mult, ALU.add))
            ser.dma(lambda e, t0=t0: e.dma_start(
                dap(o_y, t0 * 256, [[256, 128], [1, 256]]), yst[:, :]),
                eng="gpsimd")

        ser.build(nc, tok)
    return nc



def build_F():
    nc = bass.Bass("TRN2", target_bir_lowering=False)
    ynat = nc.dram_tensor("ynat", [SC, M_INTER], BF16, kind="ExternalInput")
    gate = nc.dram_tensor("gate", [SC, M_INTER], BF16, kind="ExternalInput")
    nrm = nc.dram_tensor("nrm", [DM], F32, kind="ExternalInput")
    woutT = nc.dram_tensor("woutT", [DM, DM], BF16, kind="ExternalInput")
    xn1Ts = nc.dram_tensor("xn1Ts", [DM, SC], BF16, kind="ExternalInput")
    wg0 = nc.dram_tensor("wg0", [DM, FF1], BF16, kind="ExternalInput")
    wg1 = nc.dram_tensor("wg1", [DM, FF1], BF16, kind="ExternalInput")
    wlin = nc.dram_tensor("wlin", [FF1, DM], BF16, kind="ExternalInput")
    x1 = nc.dram_tensor("x1", [SC, DM], F32, kind="ExternalInput")
    identb = nc.dram_tensor("identb", [128, 128], BF16, kind="ExternalInput")
    moT_h = nc.dram_tensor("moT_h", [DM, SC], BF16)
    o_out = nc.dram_tensor("out", [SC, DM], F32, kind="ExternalOutput")

    ser = Ser()
    NT = SC // 128
    GC = float(np.sqrt(2.0 / np.pi))
    with ExitStack() as st:
        def sb(name, shape, dtype):
            return st.enter_context(nc.sbuf_tensor(name, shape, dtype))

        def ps(name, shape, dtype):
            return st.enter_context(nc.psum_tensor(name, shape, dtype))

        gnT = sb("gnT", [128, 16 * SC], BF16)
        xn1T = sb("xn1T_sbF", [128, 16 * SC], BF16)
        hid = sb("hid", [128, 64 * 256], BF16)
        fT = sb("fT", [128, 16 * 256], BF16)
        wch = sb("wchF", [128, 2048], BF16)
        wch2 = sb("wch2F", [128, 2048], BF16)
        wch64 = sb("wch64", [128, 64 * 128], BF16)
        nwb = sb("nwb", [128, 2048], F32)
        nrow = sb("nrow", [1, 2048], F32)
        ones_rowf = sb("ones_rowfF", [1, 128], F32)
        ybf = sb("ybf", [128, 2048], BF16)
        gbf = sb("gbf", [128, 2048], BF16)
        sgate = sb("sgate", [128, 2048], BF16)
        g_f = sb("g_f", [128, 2048], F32)
        scr = sb("scrF", [128, 2048], F32)
        gnw = sb("gnw", [128, 2048], BF16)
        ident = sb("identF_", [128, 128], BF16)
        sscol = sb("sscol", [128, 1], F32)
        x2s = sb("x2s", [128, 256], F32)
        x3s = sb("x3s", [128, 256], F32)
        inn = sb("inn", [128, 256], F32)
        ths = sb("ths", [128, 256], F32)
        t3s = sb("t3s", [128, 256], F32)
        ot = sb("ot", [128, 2048], F32)
        piece = sb("pieceF", [128, 128], BF16)
        pg0 = ps("pg0", [128, 512], F32)
        pg1 = ps("pg1", [128, 512], F32)
        ptr = ps("ptrF", [128, 128], BF16)
        pbc = ps("pbcF", [128, 512], F32)
        tok = st.enter_context(nc.semaphore("tok"))

        ser.dma(lambda e: e.dma_start(ident[:, :], identb.ap()))
        ser.v(lambda e: e.memset(ones_rowf[:, :], 1.0))
        ser.dma(lambda e: e.dma_start(nrow[:, :], dap(nrm, 0, [[DM, 1], [1, DM]])))
        for q in range(4):
            ser.t(lambda e, q=q: e.matmul(pbc[:, :], ones_rowf[:, :],
                                          nrow[:1, q * 512:(q + 1) * 512],
                                          start=True, stop=True))
            ser.v(lambda e, q=q: e.tensor_copy(nwb[:, q * 512:(q + 1) * 512],
                                               pbc[:, :]))
        ser.dma(lambda e: e.dma_start(
            xn1T[:, :], dap(xn1Ts, 0, [[SC, 128], [128 * SC, 16], [1, SC]])))

        for i in range(NT):
            ser.dma(lambda e, i=i: e.dma_start(
                ybf[:, :], dap(ynat, i * 128 * M_INTER, [[M_INTER, 128], [1, M_INTER]])))
            ser.dma(lambda e, i=i: e.dma_start(
                gbf[:, :], dap(gate, i * 128 * M_INTER, [[M_INTER, 128], [1, M_INTER]])))
            ser.s(lambda e: e.activation(sgate[:, :], gbf[:, :], AF.Silu))
            ser.v(lambda e: e.tensor_mul(g_f[:, :], ybf[:, :], sgate[:, :]))
            ser.s(lambda e: e.activation(scr[:, :], g_f[:, :], AF.Square,
                                         accum_out=sscol[:, :]))
            ser.v(lambda e: e.tensor_scalar(sscol[:, :], sscol[:, :], 1.0 / DM,
                                            EPS, ALU.mult, ALU.add))
            ser.v(lambda e: e.reciprocal(sscol[:, :], sscol[:, :]))
            ser.s(lambda e: e.activation(sscol[:, :], sscol[:, :], AF.Sqrt))
            ser.v(lambda e: e.tensor_scalar_mul(g_f[:, :], g_f[:, :], sscol[:, :]))
            ser.v(lambda e: e.tensor_mul(gnw[:, :], g_f[:, :], nwb[:, :]))
            for f in range(16):
                ser.t(lambda e, f=f: e.transpose(
                    ptr[:, :], gnw[:, f * 128:(f + 1) * 128], ident[:, :]))
                ser.v(lambda e, f=f, i=i: e.tensor_copy(
                    gnT[:, f * SC + i * 128: f * SC + (i + 1) * 128], ptr[:, :]))
        for fo in range(16):
            ser.dma(lambda e, fo=fo: e.dma_start(
                wch[:, :], dap(woutT, fo * 128, [[DM, 128], [128 * DM, 16], [1, 128]])))
            for sh in range(2):
                for fi in range(16):
                    ser.t(lambda e, fi=fi, sh=sh: e.matmul(
                        pg0[:, :], wch[:, fi * 128:(fi + 1) * 128],
                        gnT[:, fi * SC + sh * 512: fi * SC + (sh + 1) * 512],
                        start=(fi == 0), stop=(fi == 15)), acc=(fi > 0))
                ser.v(lambda e, sh=sh: e.tensor_copy(
                    ybf[:, sh * 512:(sh + 1) * 512], pg0[:, :]))
            ser.dma(lambda e, fo=fo: e.dma_start(
                dap(moT_h, fo * 128 * SC, [[SC, 128], [1, SC]]),
                ybf[:, :1024]), eng="gpsimd")

        for qf in range(4):
            s0 = qf * 256
            for fo in range(64):
                ser.dma(lambda e, fo=fo: e.dma_start(
                    wch[:, :], dap(wg0, fo * 128,
                                   [[FF1, 128], [128 * FF1, 16], [1, 128]])))
                ser.dma(lambda e, fo=fo: e.dma_start(
                    wch2[:, :], dap(wg1, fo * 128,
                                    [[FF1, 128], [128 * FF1, 16], [1, 128]])))
                for fi in range(16):
                    ser.t(lambda e, fi=fi, s0=s0: e.matmul(
                        pg0[:, :256], wch[:, fi * 128:(fi + 1) * 128],
                        xn1T[:, fi * SC + s0: fi * SC + s0 + 256],
                        start=(fi == 0), stop=(fi == 15)), acc=(fi > 0))
                for fi in range(16):
                    ser.t(lambda e, fi=fi, s0=s0: e.matmul(
                        pg1[:, :256], wch2[:, fi * 128:(fi + 1) * 128],
                        xn1T[:, fi * SC + s0: fi * SC + s0 + 256],
                        start=(fi == 0), stop=(fi == 15)), acc=(fi > 0))
                ser.s(lambda e: e.activation(x2s[:, :], pg0[:, :256], AF.Square))
                ser.v(lambda e: e.tensor_mul(x3s[:, :], x2s[:, :], pg0[:, :256]))
                ser.v(lambda e: e.scalar_tensor_tensor(
                    inn[:, :], x3s[:, :], 0.044715, pg0[:, :256], ALU.mult, ALU.add))
                ser.s(lambda e: e.activation(ths[:, :], inn[:, :], AF.Tanh, scale=GC))
                ser.v(lambda e: e.tensor_scalar(ths[:, :], ths[:, :], 1.0, 0.5,
                                                ALU.add, ALU.mult))
                ser.v(lambda e: e.tensor_mul(t3s[:, :], ths[:, :], pg0[:, :256]))
                ser.v(lambda e, fo=fo: e.tensor_mul(
                    hid[:, fo * 256:(fo + 1) * 256], t3s[:, :], pg1[:, :256]))
            for do in range(16):
                ser.dma(lambda e, do=do: e.dma_start(
                    wch64[:, :], dap(wlin, do * 128,
                                     [[DM, 128], [128 * DM, 64], [1, 128]])))
                for fi in range(64):
                    ser.t(lambda e, fi=fi: e.matmul(
                        pg0[:, :256], wch64[:, fi * 128:(fi + 1) * 128],
                        hid[:, fi * 256:(fi + 1) * 256],
                        start=(fi == 0), stop=(fi == 63)), acc=(fi > 0))
                ser.v(lambda e, do=do: e.tensor_copy(
                    fT[:, do * 256:(do + 1) * 256], pg0[:, :256]))
            for ii in range(2):
                ig = qf * 2 + ii
                ser.dma(lambda e, ig=ig: e.dma_start(
                    ot[:, :], dap(x1, ig * 128 * DM, [[DM, 128], [1, DM]])))
                for f in range(16):
                    ser.dma(lambda e, ig=ig, f=f: e.dma_start(
                        piece[:, :], dap(moT_h, f * 128 * SC + ig * 128,
                                         [[SC, 128], [1, 128]])))
                    ser.t(lambda e: e.transpose(ptr[:, :], piece[:, :], ident[:, :]))
                    ser.v(lambda e, f=f: e.tensor_add(
                        ot[:, f * 128:(f + 1) * 128], ot[:, f * 128:(f + 1) * 128],
                        ptr[:, :]))
                    ser.t(lambda e, f=f, ii=ii: e.transpose(
                        ptr[:, :], fT[:, f * 256 + ii * 128: f * 256 + (ii + 1) * 128],
                        ident[:, :]))
                    ser.v(lambda e, f=f: e.tensor_add(
                        ot[:, f * 128:(f + 1) * 128], ot[:, f * 128:(f + 1) * 128],
                        ptr[:, :]))
                ser.dma(lambda e, ig=ig: e.dma_start(
                    dap(o_out, ig * 128 * DM, [[DM, 128], [1, DM]]), ot[:, :]),
                    eng="gpsimd")

        ser.build(nc, tok)
    return nc



def kernel_5launch(**inp):
    g = {k: np.asarray(v) for k, v in inp.items()}
    x0 = f32(g["L0_attn_hidden_states"].reshape(S, DM))
    wqkv = g["L0_attn_Wqkv_w"]
    b_all = f32(g["L0_attn_Wqkv_b"])
    identb = bf(np.eye(128))
    cores = list(range(NC))

    ncA = build_A()
    im = [{
        "x0": x0[c * SC:(c + 1) * SC],
        "wqkT": bf(wqkv[:3072].T), "wvT": bf(wqkv[3072:].T),
        "bqk": f32(b_all[:3072]), "bv": f32(b_all[3072:]),
        "identb": identb,
    } for c in range(NC)]
    rA = run_bass_kernel_spmd(ncA, im, core_ids=cores).results
    xnT = np.concatenate([rA[c]["xnT"] for c in range(NC)], axis=1)
    qkT = np.concatenate([rA[c]["qkT"] for c in range(NC)], axis=1)
    v = np.concatenate([rA[c]["v"] for c in range(NC)], axis=0)

    masksB = np.zeros((128, 4 * 512), np.float32)
    for off in range(4):
        for kj in range(128):
            masksB[kj, off * 512 + off * 128 + kj:(off + 1) * 512] = 1.0
    ncB = build_B()
    im = [{
        "q1T": np.ascontiguousarray(qkT[h * 256:h * 256 + 128]),
        "q2T": np.ascontiguousarray(qkT[h * 256 + 128:h * 256 + 256]),
        "k1T": np.ascontiguousarray(qkT[2048 + (h // 2) * 256:2048 + (h // 2) * 256 + 128]),
        "k2T": np.ascontiguousarray(qkT[2048 + (h // 2) * 256 + 128:2048 + (h // 2) * 256 + 256]),
        "vg": np.ascontiguousarray(v[:, (h // 2) * 256:(h // 2 + 1) * 256]),
        "lq1": f32(g["L0_attn_lambda_q1"]), "lk1": f32(g["L0_attn_lambda_k1"]),
        "lq2": f32(g["L0_attn_lambda_q2"]), "lk2": f32(g["L0_attn_lambda_k2"]),
        "subln": f32(g["L0_attn_subln_w"]).reshape(2, 128),
        "masksB": bf(masksB),
    } for h in range(NC)]
    rB = run_bass_kernel_spmd(ncB, im, core_ids=cores).results
    oT = np.concatenate([rB[h]["oT"] for h in range(NC)], axis=0)

    ncCD = build_CD()
    im = [{
        "oTs": np.ascontiguousarray(oT[:, c * SC:(c + 1) * SC]),
        "woT": bf(g["L0_attn_out_w"].T),
        "outb": f32(g["L0_attn_out_b"]),
        "xn0Ts": np.ascontiguousarray(xnT[:, c * SC:(c + 1) * SC]),
        "gwT": bf(g["L0_ffn_gate_w"].T),
        "uwT": bf(g["L0_ffn_up_w"].T),
        "gs": f32(g["L0_ffn_gate_s"]), "us": f32(g["L0_ffn_up_s"]),
        "dwT": bf(g["L0_ffn_down_w"].T),
        "ds": f32(g["L0_ffn_down_s"]),
        "x0": x0[c * SC:(c + 1) * SC],
        "winT": bf(g["L1_attn_W_in"].T),
        "identb": identb,
    } for c in range(NC)]
    rCD = run_bass_kernel_spmd(ncCD, im, core_ids=cores).results
    x1 = np.concatenate([rCD[c]["x1"] for c in range(NC)], axis=0)
    xn1T = np.concatenate([rCD[c]["xn1T"] for c in range(NC)], axis=1)
    gate = np.concatenate([rCD[c]["gate"] for c in range(NC)], axis=0)
    hbct = np.concatenate([rCD[c]["hbct"] for c in range(NC)], axis=1)
    dtt = np.concatenate([rCD[c]["dtt"] for c in range(NC)], axis=1)

    cwr = f32(g["L1_attn_conv_w"]).reshape(18, 128, MK)
    cbr = f32(g["L1_attn_conv_b"]).reshape(18, 128)
    ut01 = np.ascontiguousarray(np.triu(np.ones((128, 128), np.float32)))
    idb2 = np.zeros((128, 64), np.float32)
    for i in range(64):
        idb2[64 + i, i] = 1.0
    ncE = build_E()
    im = [{
        "xs0": np.ascontiguousarray(hbct[c * 256:c * 256 + 128]),
        "xs1": np.ascontiguousarray(hbct[c * 256 + 128:c * 256 + 256]),
        "bt_in": np.ascontiguousarray(hbct[2048:2176]),
        "ct_in": np.ascontiguousarray(hbct[2176:2304]),
        "cw0": cwr[2 * c], "cw1": cwr[2 * c + 1], "cw2": cwr[16], "cw3": cwr[17],
        "cb0": cbr[2 * c], "cb1": cbr[2 * c + 1], "cb2": cbr[16], "cb3": cbr[17],
        "dtr": np.ascontiguousarray(dtt[4 * c:4 * c + 4]),
        "dtb": f32(g["L1_attn_dt_bias"][4 * c:4 * c + 4]),
        "alog": f32(g["L1_attn_A_log"][4 * c:4 * c + 4]),
        "dpar": f32(g["L1_attn_D_param"][4 * c:4 * c + 4]),
        "ut01": ut01, "eye4d": np.eye(4, dtype=np.float32),
        "identb": identb, "idb2d": bf(idb2),
    } for c in range(NC)]
    rE = run_bass_kernel_spmd(ncE, im, core_ids=cores).results
    y = np.concatenate([rE[c]["ynat"] for c in range(NC)], axis=1)

    ncF = build_F()
    im = [{
        "ynat": np.ascontiguousarray(y[c * SC:(c + 1) * SC]),
        "gate": np.ascontiguousarray(gate[c * SC:(c + 1) * SC]),
        "nrm": f32(g["L1_attn_norm_w"]),
        "woutT": bf(g["L1_attn_W_out"].T),
        "xn1Ts": np.ascontiguousarray(xn1T[:, c * SC:(c + 1) * SC]),
        "wg0": bf(g["L1_ffn_w_gating"][0]),
        "wg1": bf(g["L1_ffn_w_gating"][1]),
        "wlin": bf(g["L1_ffn_w_linear"]),
        "x1": x1[c * SC:(c + 1) * SC],
        "identb": identb,
    } for c in range(NC)]
    rF = run_bass_kernel_spmd(ncF, im, core_ids=cores).results
    out = np.concatenate([rF[c]["out"] for c in range(NC)], axis=0)
    return out.reshape(1, S, DM).astype(np.float32)



def build_FUSED():
    nc = bass.Bass("TRN2", target_bir_lowering=False)
    x0f = nc.dram_tensor("x0f", [S, DM], F32, kind="ExternalInput")
    x0s = nc.dram_tensor("x0s", [SC, DM], F32, kind="ExternalInput")
    offs = nc.dram_tensor("offs", [1, 4], mybir.dt.int32, kind="ExternalInput")
    wqk_sl = nc.dram_tensor("wqk_sl", [DM, 512], BF16, kind="ExternalInput")
    bqk_sl = nc.dram_tensor("bqk_sl", [512], F32, kind="ExternalInput")
    wv_sl = nc.dram_tensor("wv_sl", [DM, 256], BF16, kind="ExternalInput")
    bv_sl = nc.dram_tensor("bv_sl", [256], F32, kind="ExternalInput")
    lq1 = nc.dram_tensor("lq1", [128], F32, kind="ExternalInput")
    lk1 = nc.dram_tensor("lk1", [128], F32, kind="ExternalInput")
    lq2 = nc.dram_tensor("lq2", [128], F32, kind="ExternalInput")
    lk2 = nc.dram_tensor("lk2", [128], F32, kind="ExternalInput")
    subln = nc.dram_tensor("subln", [2, 128], F32, kind="ExternalInput")
    masksB = nc.dram_tensor("masksB", [128, 4 * 512], BF16, kind="ExternalInput")
    woT = nc.dram_tensor("woT", [DM, DM], BF16, kind="ExternalInput")
    outb = nc.dram_tensor("outb", [DM], F32, kind="ExternalInput")
    gwT = nc.dram_tensor("gwT", [DM, INTER0], BF16, kind="ExternalInput")
    uwT = nc.dram_tensor("uwT", [DM, INTER0], BF16, kind="ExternalInput")
    gs = nc.dram_tensor("gs", [INTER0], F32, kind="ExternalInput")
    us = nc.dram_tensor("us", [INTER0], F32, kind="ExternalInput")
    dwT = nc.dram_tensor("dwT", [INTER0, DM], BF16, kind="ExternalInput")
    ds = nc.dram_tensor("ds", [DM], F32, kind="ExternalInput")
    winT = nc.dram_tensor("winT", [DM, PROJ], BF16, kind="ExternalInput")
    win_sl = nc.dram_tensor("win_sl", [DM, 516], BF16, kind="ExternalInput")
    cw = [nc.dram_tensor(f"cw{j}", [128, MK], F32, kind="ExternalInput")
          for j in range(4)]
    cb = [nc.dram_tensor(f"cb{j}", [128], F32, kind="ExternalInput")
          for j in range(4)]
    dtb = nc.dram_tensor("dtb", [4], F32, kind="ExternalInput")
    alog = nc.dram_tensor("alog", [4], F32, kind="ExternalInput")
    dpar = nc.dram_tensor("dpar", [4], F32, kind="ExternalInput")
    ut01 = nc.dram_tensor("ut01", [128, 128], F32, kind="ExternalInput")
    eye4d = nc.dram_tensor("eye4d", [4, 4], F32, kind="ExternalInput")
    identb = nc.dram_tensor("identb", [128, 128], BF16, kind="ExternalInput")
    idb2d = nc.dram_tensor("idb2d", [128, 64], BF16, kind="ExternalInput")
    nrm = nc.dram_tensor("nrm", [DM], F32, kind="ExternalInput")
    woutT = nc.dram_tensor("woutT", [DM, DM], BF16, kind="ExternalInput")
    wg0 = nc.dram_tensor("wg0", [DM, FF1], BF16, kind="ExternalInput")
    wg1 = nc.dram_tensor("wg1", [DM, FF1], BF16, kind="ExternalInput")
    wlin = nc.dram_tensor("wlin", [FF1, DM], BF16, kind="ExternalInput")
    xnT_d = nc.dram_tensor("xnT_d", [DM, S], BF16)
    cin_oT = nc.dram_tensor("cin_oT", [256, S], BF16)
    oT_all = nc.dram_tensor("oT_all", [DM, S], BF16)
    attnT_h = nc.dram_tensor("attnT_h", [DM, SC], BF16)
    x1_d = nc.dram_tensor("x1_d", [SC, DM], F32)
    gate_d = nc.dram_tensor("gate_d", [SC, M_INTER], BF16)
    cin_xn1T = nc.dram_tensor("cin_xn1T", [DM, SC], BF16)
    xn1T_all = nc.dram_tensor("xn1T_all", [NC * DM, SC], BF16)
    xsB_d = [nc.dram_tensor(f"xsB_d{j}", [128, S], BF16) for j in range(4)]
    cin_y = nc.dram_tensor("cin_y", [S, 256], BF16)
    y_all = nc.dram_tensor("y_all", [NC * S, 256], BF16)
    y_own = nc.dram_tensor("y_own", [SC, M_INTER], BF16)
    moT_h = nc.dram_tensor("moT_h", [DM, SC], BF16)
    o_out = nc.dram_tensor("out", [SC, DM], F32, kind="ExternalOutput")
    import os as _os
    dbg = _os.environ.get("FUSED_DEBUG") == "1"
    if dbg:
        d_xnT = nc.dram_tensor("d_xnT", [DM, S], BF16, kind="ExternalOutput")
        d_biga = nc.dram_tensor("d_biga", [DM, SC], BF16, kind="ExternalOutput")
        d_oT = nc.dram_tensor("d_oT", [256, S], BF16, kind="ExternalOutput")
        d_oTall = nc.dram_tensor("d_oTall", [DM, 512], BF16, kind="ExternalOutput")
        d_x1 = nc.dram_tensor("d_x1", [SC, DM], F32, kind="ExternalOutput")
        d_xn1 = nc.dram_tensor("d_xn1", [DM, SC], BF16, kind="ExternalOutput")
        d_y = nc.dram_tensor("d_y", [S, 256], BF16, kind="ExternalOutput")
        d_yown = nc.dram_tensor("d_yown", [SC, M_INTER], BF16, kind="ExternalOutput")
        d_gate = nc.dram_tensor("d_gate", [SC, M_INTER], BF16, kind="ExternalOutput")
        d_xsB = nc.dram_tensor("d_xsB", [128, S], BF16, kind="ExternalOutput")
        d_attnT = nc.dram_tensor("d_attnT", [DM, SC], BF16, kind="ExternalOutput")
        d_h = nc.dram_tensor("d_h", [INTER0, SC], BF16, kind="ExternalOutput")
        d_f0 = nc.dram_tensor("d_f0", [DM, SC], BF16, kind="ExternalOutput")

    ser = Ser()
    NT = SC // 128
    GC = float(np.sqrt(2.0 / np.pi))
    with ExitStack() as st:
        tok = st.enter_context(nc.semaphore("tok"))
        grb = st.enter_context(nc.gpsimd.register("grb"))
        gry = st.enter_context(nc.gpsimd.register("gry"))
        grt = st.enter_context(nc.gpsimd.register("grt"))

        def sbT(name, shape, dtype):
            return nc.sbuf_tensor(name, shape, dtype)

        ident = st.enter_context(sbT("identP", [128, 128], BF16))
        ones_rowf = st.enter_context(sbT("ones_rowfP", [1, 128], F32))
        offs_sb = st.enter_context(sbT("offs_sb", [1, 4], mybir.dt.int32))
        ser.dma(lambda e: e.dma_start(ident[:, :], identb.ap()))
        ser.dma(lambda e: e.dma_start(offs_sb[:, :], offs.ap()))
        ser.v(lambda e: e.memset(ones_rowf[:, :], 1.0))
        ser.op("gpsimd", lambda e: e.reg_load(grb, offs_sb[:1, :1]), 0)
        ser.op("gpsimd", lambda e: e.reg_load(gry, offs_sb[:1, 1:2]), 0)

        with sbT("xtP", [128, DM], F32) as xt, \
             sbT("scrP", [128, DM], F32) as scr, \
             sbT("xnbfP", [128, DM], BF16) as xnbf, \
             sbT("slabP", [128, 16 * 128], BF16) as slab, \
             sbT("mcolP", [128, 1], F32) as mcol, \
             sbT("vcolP", [128, 1], F32) as vcol, \
             nc.psum_tensor("ptrP", [128, 128], BF16) as ptr:
            for i in range(S // 128):
                ser.dma(lambda e, i=i: e.dma_start(
                    xt[:, :], dap(x0f, i * 128 * DM, [[DM, 128], [1, DM]])))
                emit_layernorm(ser, xt, xnbf, scr, mcol, vcol)
                for f in range(16):
                    ser.t(lambda e, f=f: e.transpose(
                        ptr[:, :], xnbf[:, f * 128:(f + 1) * 128], ident[:, :]))
                    ser.v(lambda e, f=f: e.tensor_copy(
                        slab[:, f * 128:(f + 1) * 128], ptr[:, :]))
                ser.dma(lambda e, i=i: e.dma_start(
                    dap(xnT_d, i * 128, [[S, 128], [128 * S, 16], [1, 128]]),
                    slab[:, :]), eng="gpsimd")
            ser.flush(nc, tok)

        st23 = ExitStack()
        qT = [st23.enter_context(sbT(f"qTf{m}", [128, S], BF16)) for m in range(2)]
        kT = [st23.enter_context(sbT(f"kTf{m}", [128, S], BF16)) for m in range(2)]
        v_sb = st23.enter_context(sbT("v_sbF", [128, 64 * 256], BF16))
        qkT4 = qT + kT
        with sbT("xnblk", [128, 16 * 2048], BF16) as xnblk, \
             sbT("wchP2", [128, 2048], BF16) as wch, \
             sbT("wvP2", [128, 16 * 256], BF16) as wv_sb2, \
             sbT("bcolP2", [128, 1], F32) as bcol, \
             sbT("bvrow2", [1, 256], F32) as bvrow, \
             sbT("bvb2", [128, 256], F32) as bvb, \
             sbT("qko2", [128, 512], BF16) as qko, \
             nc.psum_tensor("pmmP2", [128, 512], F32) as pmm, \
             nc.psum_tensor("pbcP2", [128, 256], F32) as pbc:
            ser.dma(lambda e: e.dma_start(
                wv_sb2[:, :], dap(wv_sl, 0, [[256, 128], [128 * 256, 16], [1, 256]])))
            ser.dma(lambda e: e.dma_start(
                bvrow[:, :], dap(bv_sl, 0, [[256, 1], [1, 256]])))
            ser.t(lambda e: e.matmul(pbc[:, :], ones_rowf[:, :], bvrow[:, :],
                                     start=True, stop=True))
            ser.v(lambda e: e.tensor_copy(bvb[:, :], pbc[:, :]))
            for blk in range(4):
                b0 = blk * 2048
                for fi in range(16):
                    ser.dma(lambda e, fi=fi, b0=b0: e.dma_start(
                        xnblk[:, fi * 2048:(fi + 1) * 2048],
                        dap(xnT_d, fi * 128 * S + b0, [[S, 128], [1, 2048]])))
                for fo in range(4):
                    ser.dma(lambda e, fo=fo: e.dma_start(
                        wch[:, :], dap(wqk_sl, fo * 128,
                                       [[512, 128], [128 * 512, 16], [1, 128]])))
                    ser.dma(lambda e, fo=fo: e.dma_start(
                        bcol[:, :], dap(bqk_sl, fo * 128, [[1, 128], [1, 1]])))
                    for sh in range(4):
                        for fi in range(16):
                            ser.t(lambda e, fi=fi, sh=sh: e.matmul(
                                pmm[:, :], wch[:, fi * 128:(fi + 1) * 128],
                                xnblk[:, fi * 2048 + sh * 512: fi * 2048 + (sh + 1) * 512],
                                start=(fi == 0), stop=(fi == 15)), acc=(fi > 0))
                        if fo < 2:
                            ser.v(lambda e: e.tensor_scalar(
                                qko[:, :], pmm[:, :], bcol[:, :], SCALE,
                                ALU.add, ALU.mult))
                        else:
                            ser.v(lambda e: e.tensor_scalar_add(
                                qko[:, :], pmm[:, :], bcol[:, :]))
                        ser.v(lambda e, fo=fo, b0=b0, sh=sh: e.tensor_copy(
                            qkT4[fo][:, b0 + sh * 512: b0 + (sh + 1) * 512],
                            qko[:, :]))
                for i in range(16):
                    for fi in range(16):
                        ser.t(lambda e, i=i, fi=fi: e.matmul(
                            pbc[:, :], xnblk[:, fi * 2048 + i * 128: fi * 2048 + (i + 1) * 128],
                            wv_sb2[:, fi * 256:(fi + 1) * 256],
                            start=(fi == 0), stop=(fi == 15)), acc=(fi > 0))
                    ser.v(lambda e, blk=blk, i=i: e.tensor_add(
                        v_sb[:, (blk * 16 + i) * 256:(blk * 16 + i + 1) * 256],
                        pbc[:, :], bvb[:, :]))
            ser.flush(nc, tok)

        with ExitStack() as s3:
            def e3(name, shape, dtype):
                return s3.enter_context(sbT(name, shape, dtype))

            masks = e3("masksF", [128, 4 * 512], BF16)
            ones_colb = e3("ones_colbF", [128, 1], BF16)
            ones_colf = e3("ones_colfF", [128, 1], F32)
            lrow0 = e3("lrow0", [1, 128], F32)
            lrow1 = e3("lrow1", [1, 128], F32)
            ee0 = e3("ee0", [1, 1], F32)
            ee1 = e3("ee1", [1, 1], F32)
            lamneg = e3("lamnegF", [1, 1], F32)
            sw0 = e3("sw0", [128, 1], F32)
            sw1 = e3("sw1", [128, 1], F32)
            pT = e3("pTF", [128, 512], BF16)
            pTm = e3("pTmF", [128, 512], BF16)
            linv0 = e3("linv0", [1, 512], F32)
            linv1 = e3("linv1", [1, 512], F32)
            bsc0 = e3("bsc0", [128, 512], F32)
            bsc1 = e3("bsc1", [128, 512], F32)
            oc0 = e3("oc0", [128, 512], F32)
            oc1 = e3("oc1", [128, 512], F32)
            tmp = e3("tmpBF", [128, 512], F32)
            sq0 = e3("sq0", [128, 512], F32)
            sq1 = e3("sq1", [128, 512], F32)
            rrow = e3("rrowF", [1, 512], F32)
            rb = e3("rbF", [128, 512], F32)
            ost = e3("ostF", [128, 512], BF16)
            pS = s3.enter_context(nc.psum_tensor("pSF", [128, 512], F32))
            pO00 = s3.enter_context(nc.psum_tensor("pO00", [128, 512], F32))
            pO01 = s3.enter_context(nc.psum_tensor("pO01", [128, 512], F32))
            pO10 = s3.enter_context(nc.psum_tensor("pO10", [128, 512], F32))
            pO11 = s3.enter_context(nc.psum_tensor("pO11", [128, 512], F32))
            pL0 = s3.enter_context(nc.psum_tensor("pL0", [1, 512], F32))
            pL1 = s3.enter_context(nc.psum_tensor("pL1", [1, 512], F32))
            pB = s3.enter_context(nc.psum_tensor("pBF", [128, 512], F32))
            pO = [[pO00, pO01], [pO10, pO11]]
            pL = [pL0, pL1]
            linv = [linv0, linv1]
            bsc = [bsc0, bsc1]
            oc = [oc0, oc1]
            sq = [sq0, sq1]
            sw = [sw0, sw1]
            lrow = [lrow0, lrow1]
            ee = [ee0, ee1]
            ser.dma(lambda e: e.dma_start(masks[:, :], masksB.ap()))
            ser.v(lambda e: e.memset(ones_colb[:, :], 1.0))
            ser.v(lambda e: e.memset(ones_colf[:, :], 1.0))
            lsrc = [(lq1, lk1), (lq2, lk2)]
            for m in range(2):
                a, b = lsrc[m]
                ser.dma(lambda e, a=a, m=m: e.dma_start(
                    lrow[m][:, :], dap(a, 0, [[128, 1], [1, 128]])))
                ser.dma(lambda e, b=b: e.dma_start(
                    tmp[:1, :128], dap(b, 0, [[128, 1], [1, 128]])))
                ser.v(lambda e, m=m: e.tensor_mul(lrow[m][:, :], lrow[m][:, :],
                                                  tmp[:1, :128]))
                ser.v(lambda e, m=m: e.reduce_sum(ee[m][:, :], lrow[m][:, :], AX.X))
                ser.s(lambda e, m=m: e.activation(ee[m][:, :], ee[m][:, :], AF.Exp))
            ser.v(lambda e: e.tensor_sub(lamneg[:, :], ee[1][:, :], ee[0][:, :]))
            ser.v(lambda e: e.tensor_scalar_add(lamneg[:, :], lamneg[:, :],
                                                float(-LAMBDA_INIT)))
            for h in range(2):
                ser.dma(lambda e, h=h: e.dma_start(
                    sw[h][:, :], dap(subln, h * 128, [[1, 128], [1, 1]])))
                ser.v(lambda e, h=h: e.tensor_scalar_mul(
                    sw[h][:, :], sw[h][:, :], float(1.0 - LAMBDA_INIT)))
            for qb in range(16):
                q0 = qb * 512
                nk = (q0 + 512) // 128
                for m in range(2):
                    for kt in range(nk):
                        last = (kt == nk - 1)
                        ser.t(lambda e, m=m, kt=kt, q0=q0: e.matmul(
                            pS[:, :], kT[m][:, kt * 128:(kt + 1) * 128],
                            qT[m][:, q0:q0 + 512], start=True, stop=True))
                        if kt * 128 >= q0:
                            off = (kt * 128 - q0) // 128
                            ser.s(lambda e: e.activation(pTm[:, :], pS[:, :], AF.Exp))
                            ser.v(lambda e, off=off: e.tensor_mul(
                                pT[:, :], pTm[:, :],
                                masks[:, off * 512:(off + 1) * 512]))
                        else:
                            ser.s(lambda e: e.activation(pT[:, :], pS[:, :], AF.Exp))
                        ser.t(lambda e, m=m, kt=kt, last=last: e.matmul(
                            pL[m][:, :], ones_colb[:, :], pT[:, :],
                            start=(kt == 0), stop=last))
                        for h in range(2):
                            ser.t(lambda e, m=m, h=h, kt=kt, last=last: e.matmul(
                                pO[m][h][:, :],
                                v_sb[:, kt * 256 + h * 128: kt * 256 + (h + 1) * 128],
                                pT[:, :], start=(kt == 0), stop=last))
                for m in range(2):
                    ser.v(lambda e, m=m: e.reciprocal(linv[m][:, :], pL[m][:, :]))
                ser.v(lambda e: e.tensor_scalar_mul(linv[1][:, :], linv[1][:, :],
                                                    lamneg[:1, :1]))
                for m in range(2):
                    ser.t(lambda e, m=m: e.matmul(pB[:, :], ones_rowf[:, :],
                                                  linv[m][:, :], start=True, stop=True))
                    ser.v(lambda e, m=m: e.tensor_copy(bsc[m][:, :], pB[:, :]))
                for h in range(2):
                    ser.v(lambda e, h=h: e.tensor_mul(oc[h][:, :], pO[0][h][:, :],
                                                      bsc[0][:, :]))
                    ser.v(lambda e, h=h: e.tensor_mul(tmp[:, :], pO[1][h][:, :],
                                                      bsc[1][:, :]))
                    ser.v(lambda e, h=h: e.tensor_add(oc[h][:, :], oc[h][:, :],
                                                      tmp[:, :]))
                    ser.s(lambda e, h=h: e.activation(sq[h][:, :], oc[h][:, :],
                                                      AF.Square))
                    ser.t(lambda e, h=h: e.matmul(pL[0][:, :], ones_colf[:, :],
                                                  sq[h][:, :], start=(h == 0),
                                                  stop=(h == 1)))
                ser.v(lambda e: e.tensor_scalar(rrow[:, :], pL[0][:, :], 1.0 / 256,
                                                EPS, ALU.mult, ALU.add))
                ser.v(lambda e: e.reciprocal(rrow[:, :], rrow[:, :]))
                ser.s(lambda e: e.activation(rrow[:, :], rrow[:, :], AF.Sqrt))
                ser.t(lambda e: e.matmul(pB[:, :], ones_rowf[:, :], rrow[:, :],
                                         start=True, stop=True))
                ser.v(lambda e: e.tensor_copy(rb[:, :], pB[:, :]))
                for h in range(2):
                    ser.v(lambda e, h=h: e.scalar_tensor_tensor(
                        ost[:, :], oc[h][:, :], sw[h][:, :], rb[:, :],
                        ALU.mult, ALU.mult))
                    ser.dma(lambda e, h=h, q0=q0: e.dma_start(
                        dap(cin_oT, h * 128 * S + q0, [[S, 128], [1, 512]]),
                        ost[:, :]), eng="gpsimd")
            ser.flush(nc, tok)
        st23.close()

        ser.op("gpsimd", lambda e: e.collective_compute(
            "AllGather", ALU.bypass, ins=[cin_oT.ap().opt()],
            outs=[oT_all.ap().opt()], replica_groups=[list(range(NC))]), 1)

        emit_fused_tail(nc, ser, dict(
            oT_all=oT_all, attnT_h=attnT_h, x1_d=x1_d, gate_d=gate_d,
            cin_xn1T=cin_xn1T, xn1T_all=xn1T_all, xsB_d=xsB_d, cin_y=cin_y,
            identb=identb, cin_oT_h=cin_oT, tok=tok,
            **({"d_xnT": d_xnT, "d_biga": d_biga, "d_oT": d_oT, "d_oTall": d_oTall,
                "d_x1": d_x1, "d_xn1": d_xn1, "d_y": d_y, "d_yown": d_yown,
                "d_gate": d_gate, "d_xsB": d_xsB, "d_attnT": d_attnT,
                "d_f0": d_f0, "d_h": d_h} if dbg else {}),
            y_all=y_all, y_own=y_own, moT_h=moT_h, xnT_d=xnT_d, o_out=o_out,
            woT=woT, outb=outb, gwT=gwT, uwT=uwT, gs=gs, us=us, dwT=dwT,
            ds=ds, winT=winT, win_sl=win_sl, cw=cw, cb=cb, dtb=dtb,
            alog=alog, dpar=dpar, ut01=ut01, eye4d=eye4d, idb2d=idb2d,
            nrm=nrm, woutT=woutT, wg0=wg0, wg1=wg1, wlin=wlin, x0s=x0s,
            ident=ident, ones_rowf=ones_rowf, grb=grb, gry=gry, grt=grt))
        ser.build(nc, tok)
    return nc


def emit_fused_tail(nc, ser, H):
    g = type("G", (), H)
    NT = SC // 128
    GC = float(np.sqrt(2.0 / np.pi))
    ident = H["ident"]
    ones_rowf = H["ones_rowf"]
    grb, gry, grt = H["grb"], H["gry"], H["grt"]

    def sbT(name, shape, dtype):
        return nc.sbuf_tensor(name, shape, dtype)

    with ExitStack() as s4:
        big_a = s4.enter_context(sbT("big_a4", [128, 16 * SC], BF16))
        big_b = s4.enter_context(sbT("big_b4", [128, 16 * SC], BF16))
        hbig = s4.enter_context(sbT("hbig4", [128, 44 * SC], BF16))
        wch = s4.enter_context(sbT("wch4", [128, 2048], BF16))
        wch2 = s4.enter_context(sbT("wch24", [128, 2048], BF16))
        wch44 = s4.enter_context(sbT("wch444", [128, 44 * 128], BF16))
        scol = s4.enter_context(sbT("scol4", [128, 1], F32))
        scol2 = s4.enter_context(sbT("scol24", [128, 1], F32))
        mcol = s4.enter_context(sbT("mcol4", [128, 1], F32))
        vcol = s4.enter_context(sbT("vcol4", [128, 1], F32))
        tmpf = s4.enter_context(sbT("tmpf4", [128, 512], F32))
        tmpf2 = s4.enter_context(sbT("tmpf24", [128, 512], F32))
        tmpf3 = s4.enter_context(sbT("tmpf34", [128, 512], F32))
        st_ch = s4.enter_context(sbT("st_ch4", [128, 1024], BF16))
        piece = s4.enter_context(sbT("piece4", [128, 128], BF16))
        x1t = s4.enter_context(sbT("x1t4", [128, DM], F32))
        scr = s4.enter_context(sbT("scr4_", [128, DM], F32))
        xnbf = s4.enter_context(sbT("xnbf4", [128, DM], BF16))
        pg = s4.enter_context(nc.psum_tensor("pg4", [128, 512], F32))
        pu = s4.enter_context(nc.psum_tensor("pu4", [128, 512], F32))
        ptr = s4.enter_context(nc.psum_tensor("ptr4", [128, 128], BF16))

        ser.dma(lambda e: e.dma_start(
            big_a[:, :], bass.AP(H["oT_all"], grb,
                                 [[S, 128], [128 * S, 16], [1, SC]])),
            eng="gpsimd")
        for fo in range(16):
            ser.dma(lambda e, fo=fo: e.dma_start(
                wch[:, :], dap(H["woT"], fo * 128,
                               [[DM, 128], [128 * DM, 16], [1, 128]])))
            ser.dma(lambda e, fo=fo: e.dma_start(
                scol[:, :], dap(H["outb"], fo * 128, [[1, 128], [1, 1]])))
            for sh in range(2):
                for fi in range(16):
                    ser.t(lambda e, fi=fi, sh=sh: e.matmul(
                        pg[:, :], wch[:, fi * 128:(fi + 1) * 128],
                        big_a[:, fi * SC + sh * 512: fi * SC + (sh + 1) * 512],
                        start=(fi == 0), stop=(fi == 15)), acc=(fi > 0))
                ser.v(lambda e, sh=sh: e.tensor_scalar_add(
                    st_ch[:, sh * 512:(sh + 1) * 512], pg[:, :], scol[:, :]))
            ser.dma(lambda e, fo=fo: e.dma_start(
                dap(H["attnT_h"], fo * 128 * SC, [[SC, 128], [1, SC]]),
                st_ch[:, :]), eng="gpsimd")
        ser.dma(lambda e: e.dma_start(
            big_a[:, :], bass.AP(H["xnT_d"], grb,
                                 [[S, 128], [128 * S, 16], [1, SC]])),
            eng="gpsimd")
        if "d_biga" in H:
            for f_ in range(16):
                ser.dma(lambda e, f_=f_: e.dma_start(
                    dap(H["d_biga"], f_ * 128 * SC, [[SC, 128], [1, SC]]),
                    big_a[:, f_ * SC:(f_ + 1) * SC]), eng="gpsimd")
        for fo in range(44):
            ser.dma(lambda e, fo=fo: e.dma_start(
                wch[:, :], dap(H["gwT"], fo * 128,
                               [[INTER0, 128], [128 * INTER0, 16], [1, 128]])))
            ser.dma(lambda e, fo=fo: e.dma_start(
                wch2[:, :], dap(H["uwT"], fo * 128,
                                [[INTER0, 128], [128 * INTER0, 16], [1, 128]])))
            ser.dma(lambda e, fo=fo: e.dma_start(
                scol[:, :], dap(H["gs"], fo * 128, [[1, 128], [1, 1]])))
            ser.dma(lambda e, fo=fo: e.dma_start(
                scol2[:, :], dap(H["us"], fo * 128, [[1, 128], [1, 1]])))
            for sh in range(2):
                for fi in range(16):
                    ser.t(lambda e, fi=fi, sh=sh: e.matmul(
                        pg[:, :], wch[:, fi * 128:(fi + 1) * 128],
                        big_a[:, fi * SC + sh * 512: fi * SC + (sh + 1) * 512],
                        start=(fi == 0), stop=(fi == 15)), acc=(fi > 0))
                for fi in range(16):
                    ser.t(lambda e, fi=fi, sh=sh: e.matmul(
                        pu[:, :], wch2[:, fi * 128:(fi + 1) * 128],
                        big_a[:, fi * SC + sh * 512: fi * SC + (sh + 1) * 512],
                        start=(fi == 0), stop=(fi == 15)), acc=(fi > 0))
                ser.v(lambda e: e.tensor_scalar_mul(tmpf[:, :], pg[:, :], scol[:, :]))
                ser.s(lambda e: e.activation(tmpf2[:, :], tmpf[:, :], AF.Silu))
                ser.v(lambda e: e.tensor_scalar_mul(tmpf3[:, :], pu[:, :], scol2[:, :]))
                ser.v(lambda e, fo=fo, sh=sh: e.tensor_mul(
                    hbig[:, fo * SC + sh * 512: fo * SC + (sh + 1) * 512],
                    tmpf2[:, :], tmpf3[:, :]))
        if "d_h" in H:
            for f_ in range(44):
                ser.dma(lambda e, f_=f_: e.dma_start(
                    dap(H["d_h"], f_ * 128 * SC, [[SC, 128], [1, SC]]),
                    hbig[:, f_ * SC:(f_ + 1) * SC]), eng="gpsimd")
        for do in range(16):
            ser.dma(lambda e, do=do: e.dma_start(
                wch44[:, :], dap(H["dwT"], do * 128,
                                 [[DM, 128], [128 * DM, 44], [1, 128]])))
            ser.dma(lambda e, do=do: e.dma_start(
                scol[:, :], dap(H["ds"], do * 128, [[1, 128], [1, 1]])))
            for sh in range(2):
                for fi in range(44):
                    ser.t(lambda e, fi=fi, sh=sh: e.matmul(
                        pg[:, :], wch44[:, fi * 128:(fi + 1) * 128],
                        hbig[:, fi * SC + sh * 512: fi * SC + (sh + 1) * 512],
                        start=(fi == 0), stop=(fi == 43)), acc=(fi > 0))
                ser.v(lambda e, do=do, sh=sh: e.tensor_scalar_mul(
                    big_b[:, do * SC + sh * 512: do * SC + (sh + 1) * 512],
                    pg[:, :], scol[:, :]))
        if "d_f0" in H:
            for f_ in range(16):
                ser.dma(lambda e, f_=f_: e.dma_start(
                    dap(H["d_f0"], f_ * 128 * SC, [[SC, 128], [1, SC]]),
                    big_b[:, f_ * SC:(f_ + 1) * SC]), eng="gpsimd")
        for i in range(NT):
            ser.dma(lambda e, i=i: e.dma_start(
                x1t[:, :], dap(H["x0s"], i * 128 * DM, [[DM, 128], [1, DM]])))
            for f in range(16):
                ser.dma(lambda e, i=i, f=f: e.dma_start(
                    piece[:, :], dap(H["attnT_h"], f * 128 * SC + i * 128,
                                     [[SC, 128], [1, 128]])))
                ser.t(lambda e: e.transpose(ptr[:, :], piece[:, :], ident[:, :]))
                ser.v(lambda e, f=f: e.tensor_add(
                    x1t[:, f * 128:(f + 1) * 128], x1t[:, f * 128:(f + 1) * 128],
                    ptr[:, :]))
                ser.t(lambda e, i=i, f=f: e.transpose(
                    ptr[:, :], big_b[:, f * SC + i * 128: f * SC + (i + 1) * 128],
                    ident[:, :]))
                ser.v(lambda e, f=f: e.tensor_add(
                    x1t[:, f * 128:(f + 1) * 128], x1t[:, f * 128:(f + 1) * 128],
                    ptr[:, :]))
            ser.dma(lambda e, i=i: e.dma_start(
                dap(H["x1_d"], i * 128 * DM, [[DM, 128], [1, DM]]), x1t[:, :]),
                eng="gpsimd")
            emit_layernorm(ser, x1t, xnbf, scr, mcol, vcol)
            for f in range(16):
                ser.t(lambda e, f=f: e.transpose(
                    ptr[:, :], xnbf[:, f * 128:(f + 1) * 128], ident[:, :]))
                ser.v(lambda e: e.tensor_copy(piece[:, :], ptr[:, :]))
                ser.dma(lambda e, i=i, f=f: e.dma_start(
                    dap(H["cin_xn1T"], f * 128 * SC + i * 128,
                        [[SC, 128], [1, 128]]), piece[:, :]), eng="gpsimd")
        ser.dma(lambda e: e.dma_start(
            big_a[:, :], dap(H["cin_xn1T"], 0,
                             [[SC, 128], [128 * SC, 16], [1, SC]])))
        for fo in range(16):
            ser.dma(lambda e, fo=fo: e.dma_start(
                wch[:, :], dap(H["winT"], fo * 128,
                               [[PROJ, 128], [128 * PROJ, 16], [1, 128]])))
            for sh in range(2):
                for fi in range(16):
                    ser.t(lambda e, fi=fi, sh=sh: e.matmul(
                        pg[:, :], wch[:, fi * 128:(fi + 1) * 128],
                        big_a[:, fi * SC + sh * 512: fi * SC + (sh + 1) * 512],
                        start=(fi == 0), stop=(fi == 15)), acc=(fi > 0))
                ser.v(lambda e, sh=sh: e.tensor_copy(
                    st_ch[:, sh * 512:(sh + 1) * 512], pg[:, :]))
            for i in range(NT):
                ser.t(lambda e, i=i: e.transpose(
                    ptr[:, :], st_ch[:, i * 128:(i + 1) * 128], ident[:, :]))
                ser.v(lambda e: e.tensor_copy(piece[:, :], ptr[:, :]))
                ser.dma(lambda e, i=i, fo=fo: e.dma_start(
                    dap(H["gate_d"], i * 128 * M_INTER + fo * 128,
                        [[M_INTER, 128], [1, 128]]), piece[:, :]),
                    eng="gpsimd")
        ser.flush(nc, H["tok"])

    ser.op("gpsimd", lambda e: e.collective_compute(
        "AllGather", ALU.bypass, ins=[H["cin_xn1T"].ap().opt()],
        outs=[H["xn1T_all"].ap().opt()], replica_groups=[list(range(NC))]), 1)

    with ExitStack() as s5o:
        dt4 = s5o.enter_context(sbT("dt4F", [4, S], F32))
        with ExitStack() as s5:
            xnblk = s5.enter_context(sbT("xnblk5", [128, 16 * SC], BF16))
            wch = s5.enter_context(sbT("wch5", [128, 2048], BF16))
            stg = s5.enter_context(sbT("stg5", [128, 1024], BF16))
            pg = s5.enter_context(nc.psum_tensor("pg5", [128, 512], F32))
            for r in range(NC):
                for fi in range(16):
                    ser.dma(lambda e, r=r, fi=fi: e.dma_start(
                        xnblk[:, fi * SC:(fi + 1) * SC],
                        dap(H["xn1T_all"], (r * DM + fi * 128) * SC,
                            [[SC, 128], [1, SC]])))
                for fo in range(5):
                    w = 128 if fo < 4 else 4
                    ser.dma(lambda e, fo=fo, w=w: e.dma_start(
                        wch[:, :16 * w], dap(H["win_sl"], fo * 128,
                                             [[516, 128], [128 * 516, 16], [1, w]])))
                    for sh in range(2):
                        for fi in range(16):
                            ser.t(lambda e, fi=fi, sh=sh, w=w: e.matmul(
                                pg[:w, :], wch[:, fi * w:(fi + 1) * w],
                                xnblk[:, fi * SC + sh * 512: fi * SC + (sh + 1) * 512],
                                start=(fi == 0), stop=(fi == 15)), acc=(fi > 0))
                        if fo < 4:
                            ser.v(lambda e, sh=sh: e.tensor_copy(
                                stg[:, sh * 512:(sh + 1) * 512], pg[:, :]))
                        else:
                            ser.v(lambda e, r=r, sh=sh: e.tensor_copy(
                                dt4[:, r * SC + sh * 512: r * SC + (sh + 1) * 512],
                                pg[:4, :]))
                    if fo < 4:
                        ser.dma(lambda e, fo=fo, r=r: e.dma_start(
                            dap(H["xsB_d"][fo], r * SC, [[S, 128], [1, SC]]),
                            stg[:, :]), eng="gpsimd")
            ser.flush(nc, H["tok"])

        emit_scan(nc, ser, H, dt4, ident, ones_rowf)

    ser.op("gpsimd", lambda e: e.collective_compute(
        "AllGather", ALU.bypass, ins=[H["cin_y"].ap().opt()],
        outs=[H["y_all"].ap().opt()], replica_groups=[list(range(NC))]), 1)
    for eb in range(NC):
        ser.op("gpsimd", lambda e, eb=eb: e.reg_mov(grt, eb * S * 256), 0)
        ser.op("gpsimd", lambda e: e.reg_add(grt, grt, gry), 0)
        ser.dma(lambda e, eb=eb: e.dma_start(
            dap(H["y_own"], eb * 256, [[M_INTER, 128], [128 * M_INTER, 8], [1, 256]]),
            bass.AP(H["y_all"], grt, [[256, 128], [128 * 256, 8], [1, 256]])),
            eng="gpsimd")

    emit_final(nc, ser, H, ident, ones_rowf)
    if "d_xnT" in H:
        def cp(dst, srct, rows, cols, srccols):
            ser.dma(lambda e, dst=dst, srct=srct, rows=rows, cols=cols, srccols=srccols:
                    e.dma_start(
                        dap(dst, 0, [[cols, 128], [128 * cols, rows // 128], [1, cols]]),
                        dap(srct, 0, [[srccols, 128], [128 * srccols, rows // 128], [1, cols]])),
                    eng="gpsimd")
        cp(H["d_xnT"], H["xnT_d"], DM, S, S)
        cp(H["d_oT"], H["cin_oT_h"], 256, S, S)
        cp(H["d_oTall"], H["oT_all"], DM, 512, S)
        cp(H["d_x1"], H["x1_d"], SC, DM, DM)
        cp(H["d_xn1"], H["cin_xn1T"], DM, SC, SC)
        cp(H["d_y"], H["cin_y"], S, 256, 256)
        cp(H["d_yown"], H["y_own"], SC, M_INTER, M_INTER)
        cp(H["d_gate"], H["gate_d"], SC, M_INTER, M_INTER)
        cp(H["d_xsB"], H["xsB_d"][0], 128, S, S)
        cp(H["d_attnT"], H["attnT_h"], DM, SC, SC)


def emit_scan(nc, ser, HD, dt4, identP, ones_rowfP):
    tok = HD["tok"]
    with ExitStack() as st:
        def sb(name, shape, dtype):
            return st.enter_context(nc.sbuf_tensor(name, shape, dtype))

        def ps(name, shape, dtype):
            return st.enter_context(nc.psum_tensor(name, shape, dtype))

        xsc = [sb(f"xsc{j}", [128, S], BF16) for j in range(2)]
        btc = sb("btc", [128, S], BF16)
        ctc = sb("ctc", [128, S], BF16)
        u_t = sb("u_t", [128, S + 3], BF16)
        acc = sb("acc", [128, 2048], F32)
        scr4 = sb("scr4", [4, S], F32)
        ag4 = sb("ag4", [4, S], F32)
        wcol = sb("wcol", [128, MK], F32)
        bcolE = sb("bcolE", [128, 1], F32)
        identB = sb("identB", [128, 128], BF16)
        idb2 = sb("idb2", [128, 64], BF16)
        ut = sb("ut", [128, 128], F32)
        eye4 = sb("eye4", [4, 4], F32)
        ones_rowf = sb("ones_rowfE", [1, 128], F32)
        prow = sb("prow", [1, 4], F32)
        arow = sb("arow", [1, 4], F32)
        drow = sb("drow", [1, 4], F32)
        acol = sb("acol", [4, 1], F32)
        bcol4 = sb("bcol4", [4, 1], F32)
        dcol = [sb(f"dcol{h}", [128, 1], F32) for h in range(4)]
        H = [[sb(f"H{h}_{pp}", [128, MP], F32) for pp in range(2)] for h in range(4)]
        Hbf = sb("Hbf", [128, MP], BF16)
        al = sb("al", [1, 128], F32)
        prevs = sb("prevs", [1, 1], F32)
        negalT = sb("negalT", [128, 1], F32)
        eac = sb("eac", [128, 1], F32)
        dtn = sb("dtn", [128, 1], F32)
        eeh = sb("eeh", [1, 1], F32)
        ebc = sb("ebc", [128, 1], F32)
        BCm = sb("BCm", [128, 128], F32)
        Bnat = sb("Bnat", [128, 128], BF16)
        dec_sb = sb("dec_sb", [128, 128], F32)
        dec2 = sb("dec2", [128, 128], F32)
        Mt = sb("Mt", [128, 128], BF16)
        xnat = sb("xnat", [128, MP], BF16)
        yv = sb("yv", [128, MP], F32)
        y2 = sb("y2", [128, MP], F32)
        dxt = sb("dxt", [128, MP], F32)
        Ssb = sb("Ssb", [128, MP], F32)
        yst = sb("yst", [128, 256], BF16)
        psA = ps("psA", [128, 128], F32)
        psB = ps("psB", [128, 128], F32)
        ptrb = ps("ptrb", [128, 128], BF16)
        ptf = ps("ptf", [128, 128], F32)
        psY = ps("psY", [128, MP], F32)
        psYi = ps("psYi", [128, MP], F32)
        psS = ps("psS", [128, MP], F32)
        psE = ps("psE", [128, 1], F32)

        ser.dma(lambda e: e.dma_start(identB[:, :], HD["identb"].ap()))
        ser.dma(lambda e: e.dma_start(idb2[:, :], HD["idb2d"].ap()))
        ser.dma(lambda e: e.dma_start(ut[:, :], HD["ut01"].ap()))
        ser.dma(lambda e: e.dma_start(eye4[:, :], HD["eye4d"].ap()))
        ser.v(lambda e: e.memset(ones_rowf[:, :], 1.0))

        conv_src = HD["xsB_d"]
        conv_dst = [xsc[0], xsc[1], btc, ctc]
        for j in range(4):
            ser.v(lambda e: e.memset(u_t[:, :3], 0.0))
            ser.dma(lambda e, j=j: e.dma_start(u_t[:, 3:], conv_src[j].ap()))
            ser.dma(lambda e, j=j: e.dma_start(wcol[:, :], HD["cw"][j].ap()))
            ser.dma(lambda e, j=j: e.dma_start(
                bcolE[:, :], dap(HD["cb"][j], 0, [[1, 128], [1, 1]])))
            for q in range(4):
                q0 = q * 2048
                ser.v(lambda e, q0=q0: e.tensor_scalar_mul(
                    acc[:, :], u_t[:, q0:q0 + 2048], wcol[:, 0:1]))
                for kk in range(1, MK):
                    ser.v(lambda e, q0=q0, kk=kk: e.scalar_tensor_tensor(
                        acc[:, :], u_t[:, q0 + kk:q0 + kk + 2048],
                        wcol[:, kk:kk + 1], acc[:, :], ALU.mult, ALU.add))
                ser.s(lambda e, j=j, q0=q0: e.activation(
                    conv_dst[j][:, q0:q0 + 2048], acc[:, :], AF.Silu,
                    bias=bcolE[:, :]))

        ser.dma(lambda e: e.dma_start(prow[:, :], dap(HD["dtb"], 0, [[4, 1], [1, 4]])))
        ser.dma(lambda e: e.dma_start(arow[:, :], dap(HD["alog"], 0, [[4, 1], [1, 4]])))
        ser.dma(lambda e: e.dma_start(drow[:, :], dap(HD["dpar"], 0, [[4, 1], [1, 4]])))
        ser.s(lambda e: e.activation(arow[:, :], arow[:, :], AF.Exp))
        ser.v(lambda e: e.tensor_scalar_mul(arow[:, :], arow[:, :], -1.0))
        ser.t(lambda e: e.matmul(psE[:4, :1], arow[:, :], ones_rowf[:1, :1],
                                 start=True, stop=True))
        ser.v(lambda e: e.tensor_copy(acol[:, :], psE[:4, :1]))
        ser.t(lambda e: e.matmul(psE[:4, :1], prow[:, :], ones_rowf[:1, :1],
                                 start=True, stop=True))
        ser.v(lambda e: e.tensor_copy(bcol4[:, :], psE[:4, :1]))
        ser.v(lambda e: e.tensor_scalar_add(dt4[:, :], dt4[:, :], bcol4[:, :]))
        ser.s(lambda e: e.activation(scr4[:, :], dt4[:, :], AF.Abs))
        ser.s(lambda e: e.activation(scr4[:, :], scr4[:, :], AF.Exp, scale=-1.0))
        ser.s(lambda e: e.activation(scr4[:, :], scr4[:, :], AF.Ln, bias=1.0))
        ser.v(lambda e: e.tensor_scalar_max(dt4[:, :], dt4[:, :], 0.0))
        ser.v(lambda e: e.tensor_add(dt4[:, :], dt4[:, :], scr4[:, :]))
        ser.v(lambda e: e.tensor_scalar_mul(scr4[:, :], dt4[:, :], acol[:, :]))
        ser.v(lambda e: e.tensor_tensor_scan(ag4[:, :], scr4[:, :], scr4[:, :],
                                             0.0, ALU.add, ALU.bypass))
        for h in range(4):
            ser.t(lambda e, h=h: e.matmul(psE[:, :], ones_rowf[:, :],
                                          drow[:1, h:h + 1], start=True, stop=True))
            ser.v(lambda e, h=h: e.tensor_copy(dcol[h][:, :], psE[:, :]))
            ser.v(lambda e, h=h: e.memset(H[h][0][:, :], 0.0))

        for t in range(NCHUNK):
            t0 = t * 128
            cur, nxt = t % 2, (t + 1) % 2
            ser.t(lambda e, t0=t0: e.matmul(psA[:, :], btc[:, t0:t0 + 128],
                                            ctc[:, t0:t0 + 128], start=True, stop=True))
            ser.v(lambda e: e.tensor_mul(BCm[:, :], psA[:, :], ut[:, :]))
            ser.t(lambda e, t0=t0: e.transpose(ptrb[:, :], btc[:, t0:t0 + 128],
                                               identB[:, :]))
            ser.v(lambda e: e.tensor_copy(Bnat[:, :], ptrb[:, :]))
            for h in range(4):
                c0 = h // 2
                p0 = (h % 2) * 64
                if t > 0:
                    ser.t(lambda e, h=h, t0=t0: e.matmul(
                        ptf[:1, :1], eye4[:, h:h + 1], ag4[:4, t0 - 1:t0],
                        start=True, stop=True))
                    ser.v(lambda e: e.tensor_copy(prevs[:, :], ptf[:1, :1]))
                ser.t(lambda e, h=h, t0=t0: e.matmul(
                    ptf[:1, :128], eye4[:, h:h + 1], ag4[:4, t0:t0 + 128],
                    start=True, stop=True))
                if t > 0:
                    ser.v(lambda e: e.tensor_scalar_sub(al[:, :], ptf[:1, :128],
                                                        prevs[:1, :1]))
                else:
                    ser.v(lambda e: e.tensor_copy(al[:, :], ptf[:1, :128]))
                ser.t(lambda e: e.matmul(ptf[:, :1], al[:, :], ones_rowf[:1, :1],
                                         start=True, stop=True))
                ser.s(lambda e: e.activation(eac[:, :], ptf[:, :1], AF.Exp))
                ser.v(lambda e: e.tensor_scalar_mul(negalT[:, :], ptf[:, :1], -1.0))
                ser.t(lambda e, h=h, t0=t0: e.matmul(
                    ptf[:, :1], dt4[:4, t0:t0 + 128], eye4[:, h:h + 1],
                    start=True, stop=True))
                ser.v(lambda e: e.tensor_copy(dtn[:, :], ptf[:, :1]))
                ser.t(lambda e: e.matmul(psB[:, :], ones_rowf[:, :], al[:, :],
                                         start=True, stop=True))
                ser.v(lambda e: e.tensor_scalar(dec_sb[:, :], psB[:, :],
                                                negalT[:, :], 0.0,
                                                ALU.add, ALU.min))
                ser.s(lambda e: e.activation(dec_sb[:, :], dec_sb[:, :], AF.Exp))
                ser.v(lambda e: e.tensor_scalar_mul(dec2[:, :], dec_sb[:, :],
                                                    dtn[:, :]))
                ser.v(lambda e: e.tensor_mul(Mt[:, :], dec2[:, :], BCm[:, :]))
                idap = (lambda e: identB[:64, :64]) if p0 == 0 else (
                    lambda e: idb2[64:128, :64])
                ser.t(lambda e, c0=c0, p0=p0, t0=t0, idap=idap: e.transpose(
                    ptrb[:, :MP], xsc[c0][p0:p0 + 64, t0:t0 + 128], idap(e)))
                ser.v(lambda e: e.tensor_copy(xnat[:, :], ptrb[:, :MP]))
                ser.t(lambda e: e.matmul(psY[:, :], Mt[:, :], xnat[:, :],
                                         start=True, stop=True))
                ser.v(lambda e, h=h, cur=cur: e.tensor_copy(Hbf[:, :],
                                                            H[h][cur][:, :]))
                ser.t(lambda e, t0=t0: e.matmul(
                    psYi[:, :], ctc[:, t0:t0 + 128], Hbf[:, :],
                    start=True, stop=True))
                ser.v(lambda e: e.tensor_scalar_mul(yv[:, :], psYi[:, :], eac[:, :]))
                ser.v(lambda e: e.tensor_add(y2[:, :], psY[:, :], yv[:, :]))
                ser.v(lambda e, h=h: e.tensor_scalar_mul(dxt[:, :], xnat[:, :],
                                                         dcol[h][:, :]))
                ser.v(lambda e, h=h: e.tensor_add(
                    yst[:, h * MP:(h + 1) * MP], dxt[:, :], y2[:, :]))
                ser.v(lambda e: e.tensor_scalar_mul(Mt[:, :], Bnat[:, :],
                                                    dec2[:, 127:128]))
                ser.t(lambda e: e.matmul(psS[:, :], Mt[:, :], xnat[:, :],
                                         start=True, stop=True))
                ser.v(lambda e: e.tensor_copy(Ssb[:, :], psS[:, :]))
                ser.s(lambda e: e.activation(eeh[:, :], al[:1, 127:128], AF.Exp))
                ser.t(lambda e: e.matmul(psE[:, :], ones_rowf[:, :], eeh[:, :],
                                         start=True, stop=True))
                ser.v(lambda e: e.tensor_copy(ebc[:, :], psE[:, :]))
                ser.v(lambda e, h=h, cur=cur, nxt=nxt: e.scalar_tensor_tensor(
                    H[h][nxt][:, :], H[h][cur][:, :], ebc[:, :], Ssb[:, :],
                    ALU.mult, ALU.add))
            ser.dma(lambda e, t0=t0: e.dma_start(
                dap(HD["cin_y"], t0 * 256, [[256, 128], [1, 256]]), yst[:, :]),
                eng="gpsimd")
        ser.flush(nc, tok)




def emit_final(nc, ser, H, identP, ones_rowfP):
    tok = H["tok"]
    NT = SC // 128
    GC = float(np.sqrt(2.0 / np.pi))
    with ExitStack() as st:
        def sb(name, shape, dtype):
            return st.enter_context(nc.sbuf_tensor(name, shape, dtype))

        def ps(name, shape, dtype):
            return st.enter_context(nc.psum_tensor(name, shape, dtype))

        gnT = sb("gnT", [128, 16 * SC], BF16)
        xn1T = sb("xn1T_sbF", [128, 16 * SC], BF16)
        hid = sb("hid", [128, 64 * 256], BF16)
        fT = sb("fT", [128, 16 * 256], BF16)
        wch = sb("wchF", [128, 2048], BF16)
        wch2 = sb("wch2F", [128, 2048], BF16)
        wch64 = sb("wch64", [128, 64 * 128], BF16)
        nwb = sb("nwb", [128, 2048], F32)
        nrow = sb("nrow", [1, 2048], F32)
        ones_rowf = sb("ones_rowfF", [1, 128], F32)
        ybf = sb("ybf", [128, 2048], BF16)
        gbf = sb("gbf", [128, 2048], BF16)
        sgate = sb("sgate", [128, 2048], BF16)
        g_f = sb("g_f", [128, 2048], F32)
        scr = sb("scrF", [128, 2048], F32)
        gnw = sb("gnw", [128, 2048], BF16)
        ident = sb("identF_", [128, 128], BF16)
        sscol = sb("sscol", [128, 1], F32)
        x2s = sb("x2s", [128, 256], F32)
        x3s = sb("x3s", [128, 256], F32)
        inn = sb("inn", [128, 256], F32)
        ths = sb("ths", [128, 256], F32)
        t3s = sb("t3s", [128, 256], F32)
        ot = sb("ot", [128, 2048], F32)
        piece = sb("pieceF", [128, 128], BF16)
        pg0 = ps("pg0", [128, 512], F32)
        pg1 = ps("pg1", [128, 512], F32)
        ptr = ps("ptrF", [128, 128], BF16)
        pbc = ps("pbcF", [128, 512], F32)

        ser.dma(lambda e: e.dma_start(ident[:, :], H["identb"].ap()))
        ser.v(lambda e: e.memset(ones_rowf[:, :], 1.0))
        ser.dma(lambda e: e.dma_start(nrow[:, :], dap(H["nrm"], 0, [[DM, 1], [1, DM]])))
        for q in range(4):
            ser.t(lambda e, q=q: e.matmul(pbc[:, :], ones_rowf[:, :],
                                          nrow[:1, q * 512:(q + 1) * 512],
                                          start=True, stop=True))
            ser.v(lambda e, q=q: e.tensor_copy(nwb[:, q * 512:(q + 1) * 512],
                                               pbc[:, :]))
        ser.dma(lambda e: e.dma_start(
            xn1T[:, :], dap(H["cin_xn1T"], 0, [[SC, 128], [128 * SC, 16], [1, SC]])))

        for i in range(NT):
            ser.dma(lambda e, i=i: e.dma_start(
                ybf[:, :], dap(H["y_own"], i * 128 * M_INTER, [[M_INTER, 128], [1, M_INTER]])))
            ser.dma(lambda e, i=i: e.dma_start(
                gbf[:, :], dap(H["gate_d"], i * 128 * M_INTER, [[M_INTER, 128], [1, M_INTER]])))
            ser.s(lambda e: e.activation(sgate[:, :], gbf[:, :], AF.Silu))
            ser.v(lambda e: e.tensor_mul(g_f[:, :], ybf[:, :], sgate[:, :]))
            ser.s(lambda e: e.activation(scr[:, :], g_f[:, :], AF.Square,
                                         accum_out=sscol[:, :]))
            ser.v(lambda e: e.tensor_scalar(sscol[:, :], sscol[:, :], 1.0 / DM,
                                            EPS, ALU.mult, ALU.add))
            ser.v(lambda e: e.reciprocal(sscol[:, :], sscol[:, :]))
            ser.s(lambda e: e.activation(sscol[:, :], sscol[:, :], AF.Sqrt))
            ser.v(lambda e: e.tensor_scalar_mul(g_f[:, :], g_f[:, :], sscol[:, :]))
            ser.v(lambda e: e.tensor_mul(gnw[:, :], g_f[:, :], nwb[:, :]))
            for f in range(16):
                ser.t(lambda e, f=f: e.transpose(
                    ptr[:, :], gnw[:, f * 128:(f + 1) * 128], ident[:, :]))
                ser.v(lambda e, f=f, i=i: e.tensor_copy(
                    gnT[:, f * SC + i * 128: f * SC + (i + 1) * 128], ptr[:, :]))
        for fo in range(16):
            ser.dma(lambda e, fo=fo: e.dma_start(
                wch[:, :], dap(H["woutT"], fo * 128, [[DM, 128], [128 * DM, 16], [1, 128]])))
            for sh in range(2):
                for fi in range(16):
                    ser.t(lambda e, fi=fi, sh=sh: e.matmul(
                        pg0[:, :], wch[:, fi * 128:(fi + 1) * 128],
                        gnT[:, fi * SC + sh * 512: fi * SC + (sh + 1) * 512],
                        start=(fi == 0), stop=(fi == 15)), acc=(fi > 0))
                ser.v(lambda e, sh=sh: e.tensor_copy(
                    ybf[:, sh * 512:(sh + 1) * 512], pg0[:, :]))
            ser.dma(lambda e, fo=fo: e.dma_start(
                dap(H["moT_h"], fo * 128 * SC, [[SC, 128], [1, SC]]),
                ybf[:, :1024]), eng="gpsimd")

        for qf in range(4):
            s0 = qf * 256
            for fo in range(64):
                ser.dma(lambda e, fo=fo: e.dma_start(
                    wch[:, :], dap(H["wg0"], fo * 128,
                                   [[FF1, 128], [128 * FF1, 16], [1, 128]])))
                ser.dma(lambda e, fo=fo: e.dma_start(
                    wch2[:, :], dap(H["wg1"], fo * 128,
                                    [[FF1, 128], [128 * FF1, 16], [1, 128]])))
                for fi in range(16):
                    ser.t(lambda e, fi=fi, s0=s0: e.matmul(
                        pg0[:, :256], wch[:, fi * 128:(fi + 1) * 128],
                        xn1T[:, fi * SC + s0: fi * SC + s0 + 256],
                        start=(fi == 0), stop=(fi == 15)), acc=(fi > 0))
                for fi in range(16):
                    ser.t(lambda e, fi=fi, s0=s0: e.matmul(
                        pg1[:, :256], wch2[:, fi * 128:(fi + 1) * 128],
                        xn1T[:, fi * SC + s0: fi * SC + s0 + 256],
                        start=(fi == 0), stop=(fi == 15)), acc=(fi > 0))
                ser.s(lambda e: e.activation(x2s[:, :], pg0[:, :256], AF.Square))
                ser.v(lambda e: e.tensor_mul(x3s[:, :], x2s[:, :], pg0[:, :256]))
                ser.v(lambda e: e.scalar_tensor_tensor(
                    inn[:, :], x3s[:, :], 0.044715, pg0[:, :256], ALU.mult, ALU.add))
                ser.s(lambda e: e.activation(ths[:, :], inn[:, :], AF.Tanh, scale=GC))
                ser.v(lambda e: e.tensor_scalar(ths[:, :], ths[:, :], 1.0, 0.5,
                                                ALU.add, ALU.mult))
                ser.v(lambda e: e.tensor_mul(t3s[:, :], ths[:, :], pg0[:, :256]))
                ser.v(lambda e, fo=fo: e.tensor_mul(
                    hid[:, fo * 256:(fo + 1) * 256], t3s[:, :], pg1[:, :256]))
            for do in range(16):
                ser.dma(lambda e, do=do: e.dma_start(
                    wch64[:, :], dap(H["wlin"], do * 128,
                                     [[DM, 128], [128 * DM, 64], [1, 128]])))
                for fi in range(64):
                    ser.t(lambda e, fi=fi: e.matmul(
                        pg0[:, :256], wch64[:, fi * 128:(fi + 1) * 128],
                        hid[:, fi * 256:(fi + 1) * 256],
                        start=(fi == 0), stop=(fi == 63)), acc=(fi > 0))
                ser.v(lambda e, do=do: e.tensor_copy(
                    fT[:, do * 256:(do + 1) * 256], pg0[:, :256]))
            for ii in range(2):
                ig = qf * 2 + ii
                ser.dma(lambda e, ig=ig: e.dma_start(
                    ot[:, :], dap(H["x1_d"], ig * 128 * DM, [[DM, 128], [1, DM]])))
                for f in range(16):
                    ser.dma(lambda e, ig=ig, f=f: e.dma_start(
                        piece[:, :], dap(H["moT_h"], f * 128 * SC + ig * 128,
                                         [[SC, 128], [1, 128]])))
                    ser.t(lambda e: e.transpose(ptr[:, :], piece[:, :], ident[:, :]))
                    ser.v(lambda e, f=f: e.tensor_add(
                        ot[:, f * 128:(f + 1) * 128], ot[:, f * 128:(f + 1) * 128],
                        ptr[:, :]))
                    ser.t(lambda e, f=f, ii=ii: e.transpose(
                        ptr[:, :], fT[:, f * 256 + ii * 128: f * 256 + (ii + 1) * 128],
                        ident[:, :]))
                    ser.v(lambda e, f=f: e.tensor_add(
                        ot[:, f * 128:(f + 1) * 128], ot[:, f * 128:(f + 1) * 128],
                        ptr[:, :]))
                ser.dma(lambda e, ig=ig: e.dma_start(
                    dap(H["o_out"], ig * 128 * DM, [[DM, 128], [1, DM]]), ot[:, :]),
                    eng="gpsimd")
        ser.flush(nc, tok)




def kernel_fused(**inp):
    g = {k: np.asarray(v) for k, v in inp.items()}
    x0 = f32(g["L0_attn_hidden_states"].reshape(S, DM))
    wqkv = g["L0_attn_Wqkv_w"]
    b_all = f32(g["L0_attn_Wqkv_b"])
    identb_np = bf(np.eye(128))
    masksB = np.zeros((128, 4 * 512), np.float32)
    for off in range(4):
        for kj in range(128):
            masksB[kj, off * 512 + off * 128 + kj:(off + 1) * 512] = 1.0
    cwr = f32(g["L1_attn_conv_w"]).reshape(18, 128, MK)
    cbr = f32(g["L1_attn_conv_b"]).reshape(18, 128)
    ut01 = np.ascontiguousarray(np.triu(np.ones((128, 128), np.float32)))
    idb2 = np.zeros((128, 64), np.float32)
    for i in range(64):
        idb2[64 + i, i] = 1.0
    win = np.asarray(g["L1_attn_W_in"])

    ncF = build_FUSED()
    im = []
    for c in range(NC):
        gq = c // 2
        wq1 = wqkv[c * 256:c * 256 + 128]
        wq2 = wqkv[c * 256 + 128:c * 256 + 256]
        wk1 = wqkv[2048 + gq * 256:2048 + gq * 256 + 128]
        wk2 = wqkv[2048 + gq * 256 + 128:2048 + gq * 256 + 256]
        wqk_sl = bf(np.concatenate([wq1, wq2, wk1, wk2], axis=0).T)
        bqk_sl = f32(np.concatenate([
            b_all[c * 256:c * 256 + 256],
            b_all[2048 + gq * 256:2048 + gq * 256 + 256]]))
        wv_sl = bf(wqkv[3072 + gq * 256:3072 + (gq + 1) * 256].T)
        bv_sl = f32(b_all[3072 + gq * 256:3072 + (gq + 1) * 256])
        win_rows = np.concatenate([
            win[M_INTER + c * 256: M_INTER + (c + 1) * 256],
            win[M_INTER + 2048:M_INTER + 2048 + 128],
            win[M_INTER + 2048 + 128:M_INTER + 2048 + 256],
            win[M_INTER + CONV_DIM + 4 * c: M_INTER + CONV_DIM + 4 * c + 4]],
            axis=0)
        win_sl = bf(win_rows.T)
        offs_np = np.array([[c * SC, c * SC * 256, 0, 0]], np.int32)
        im.append({
            "x0f": x0, "x0s": x0[c * SC:(c + 1) * SC], "offs": offs_np,
            "wqk_sl": wqk_sl, "bqk_sl": bqk_sl, "wv_sl": wv_sl, "bv_sl": bv_sl,
            "lq1": f32(g["L0_attn_lambda_q1"]), "lk1": f32(g["L0_attn_lambda_k1"]),
            "lq2": f32(g["L0_attn_lambda_q2"]), "lk2": f32(g["L0_attn_lambda_k2"]),
            "subln": f32(g["L0_attn_subln_w"]).reshape(2, 128),
            "masksB": bf(masksB),
            "woT": bf(g["L0_attn_out_w"].T), "outb": f32(g["L0_attn_out_b"]),
            "gwT": bf(g["L0_ffn_gate_w"].T), "uwT": bf(g["L0_ffn_up_w"].T),
            "gs": f32(g["L0_ffn_gate_s"]), "us": f32(g["L0_ffn_up_s"]),
            "dwT": bf(g["L0_ffn_down_w"].T), "ds": f32(g["L0_ffn_down_s"]),
            "winT": bf(win.T), "win_sl": win_sl,
            "cw0": cwr[2 * c], "cw1": cwr[2 * c + 1], "cw2": cwr[16], "cw3": cwr[17],
            "cb0": cbr[2 * c], "cb1": cbr[2 * c + 1], "cb2": cbr[16], "cb3": cbr[17],
            "dtb": f32(g["L1_attn_dt_bias"][4 * c:4 * c + 4]),
            "alog": f32(g["L1_attn_A_log"][4 * c:4 * c + 4]),
            "dpar": f32(g["L1_attn_D_param"][4 * c:4 * c + 4]),
            "ut01": ut01, "eye4d": np.eye(4, dtype=np.float32),
            "identb": identb_np, "idb2d": bf(idb2),
            "nrm": f32(g["L1_attn_norm_w"]),
            "woutT": bf(g["L1_attn_W_out"].T),
            "wg0": bf(g["L1_ffn_w_gating"][0]), "wg1": bf(g["L1_ffn_w_gating"][1]),
            "wlin": bf(g["L1_ffn_w_linear"]),
        })
    r = run_bass_kernel_spmd(ncF, im, core_ids=list(range(NC))).results
    out = np.concatenate([r[c]["out"] for c in range(NC)], axis=0)
    return out.reshape(1, S, DM).astype(np.float32)


kernel = kernel_fused


# version 05 (final)
# speedup vs baseline: 1.3518x; 1.0306x over previous
import numpy as np
import ml_dtypes
from contextlib import ExitStack

import concourse.bass as bass
import concourse.mybir as mybir
from concourse.bass_utils import run_bass_kernel_spmd

F32 = mybir.dt.float32
BF16 = mybir.dt.bfloat16
AF = mybir.ActivationFunctionType
ALU = mybir.AluOpType
AX = mybir.AxisListType

BF = ml_dtypes.bfloat16

S = 8192
DM = 2048
NC = 8
SC = S // NC
H2 = 8
DH = 128
LAMBDA_INIT = 0.8 - 0.6 * np.exp(-0.3 * 1.0)
SCALE = DH ** -0.5
INTER0 = 5632
MH = 32
MP = 64
MN = 128
MK = 4
M_INTER = 2048
CONV_DIM = M_INTER + 2 * MN
PROJ = M_INTER + CONV_DIM + MH
CHUNK = 128
NCHUNK = S // CHUNK
FF1 = 8192
EPS = 1e-5


class Ser:

    def __init__(self):
        self.ops = []

    def op(self, eng, fn, inc=1, acc=False):
        self.ops.append((eng, fn, inc, acc))

    def t(self, fn, acc=False):
        self.op("tensor", fn, 1, acc)

    def v(self, fn):
        self.op("vector", fn, 1)

    def s(self, fn):
        self.op("scalar", fn, 1)

    def g(self, fn):
        self.op("gpsimd", fn, 1)

    def dma(self, fn, eng="sync"):
        self.op(eng, fn, 16)

    def flush(self, nc, tok, final=False):
        if not self.ops and not final:
            return getattr(self, "base", 0)
        base = getattr(self, "base", 0)
        pre = []
        c = base
        for (_, _, inc, _) in self.ops:
            pre.append(c)
            c += inc
        total = c
        with nc.Block() as block:
            for ename in ["sync", "tensor", "vector", "scalar", "gpsimd"]:
                ops = [(i, fn, inc, acc) for i, (e, fn, inc, acc)
                       in enumerate(self.ops) if e == ename]
                if not ops and not (final and ename == "gpsimd"):
                    continue

                def make(ops, ename):
                    def body(eng):
                        for i, fn, inc, acc in ops:
                            if pre[i] > 0 and not acc:
                                eng.wait_ge(tok, pre[i])
                            inst = fn(eng)
                            inst.then_inc(tok, inc)
                        if final and ename == "gpsimd":
                            eng.wait_ge(tok, total)
                    return body

                getattr(block, ename)(make(ops, ename))
        self.base = total
        self.ops = []
        return total

    def build(self, nc, tok):
        return self.flush(nc, tok, final=True)


def dap(h, off, dims):
    return bass.AP(h, off, [list(d) for d in dims])


def bf(a):
    return np.ascontiguousarray(np.asarray(a).astype(BF))


def f32(a):
    return np.ascontiguousarray(np.asarray(a).astype(np.float32))



def emit_layernorm(ser, x_f32, xn_bf, scr_f32, m_col, v_col):
    W = 2048
    ser.v(lambda e, a=x_f32, o=m_col: e.reduce_sum(o[:, :], a[:, :], AX.X))
    ser.v(lambda e, o=m_col: e.tensor_scalar_mul(o[:, :], o[:, :], -1.0 / W))
    ser.v(lambda e, a=x_f32, o=scr_f32, m=m_col:
          e.tensor_scalar_add(o[:, :], a[:, :], m[:, :]))
    ser.s(lambda e, a=scr_f32, o=x_f32, acc=v_col:
          e.activation(o[:, :], a[:, :], AF.Square, accum_out=acc[:, :]))
    ser.v(lambda e, o=v_col: e.tensor_scalar(o[:, :], o[:, :], 1.0 / W, EPS,
                                             ALU.mult, ALU.add))
    ser.v(lambda e, o=v_col: e.reciprocal(o[:, :], o[:, :]))
    ser.s(lambda e, o=v_col: e.activation(o[:, :], o[:, :], AF.Sqrt))
    ser.v(lambda e, a=scr_f32, o=xn_bf, r=v_col:
          e.tensor_scalar_mul(o[:, :], a[:, :], r[:, :]))



def build_A():
    nc = bass.Bass("TRN2", target_bir_lowering=False)
    x0 = nc.dram_tensor("x0", [SC, DM], F32, kind="ExternalInput")
    wqkT = nc.dram_tensor("wqkT", [DM, 3072], BF16, kind="ExternalInput")
    wvT = nc.dram_tensor("wvT", [DM, 1024], BF16, kind="ExternalInput")
    bqk = nc.dram_tensor("bqk", [3072], F32, kind="ExternalInput")
    bv = nc.dram_tensor("bv", [1024], F32, kind="ExternalInput")
    identb = nc.dram_tensor("identb", [128, 128], BF16, kind="ExternalInput")
    o_xnT = nc.dram_tensor("xnT", [DM, SC], BF16, kind="ExternalOutput")
    o_qkT = nc.dram_tensor("qkT", [3072, SC], BF16, kind="ExternalOutput")
    o_v = nc.dram_tensor("v", [SC, 1024], BF16, kind="ExternalOutput")

    ser = Ser()
    NT = SC // 128
    with ExitStack() as st:
        def sb(name, shape, dtype):
            return st.enter_context(nc.sbuf_tensor(name, shape, dtype))

        def ps(name, shape, dtype):
            return st.enter_context(nc.psum_tensor(name, shape, dtype))

        xt = sb("xt", [128, DM], F32)
        scr = sb("scr", [128, DM], F32)
        xn_bf = sb("xn_bf", [128, NT * DM], BF16)
        xnT = sb("xnT_sb", [128, 16 * SC], BF16)
        wv_sb = sb("wv_sb", [128, 16 * 1024], BF16)
        wch = sb("wch", [128, 2048], BF16)
        bcol = sb("bcol", [128, 1], F32)
        mcol = sb("mcol", [128, 1], F32)
        vcol = sb("vcol", [128, 1], F32)
        ident = sb("ident", [128, 128], BF16)
        ones_row = sb("ones_row", [1, 128], F32)
        bvrow = sb("bvrow", [1, 1024], F32)
        bvb = sb("bvb", [128, 1024], F32)
        qk_out = sb("qk_out", [128, 1024], BF16)
        v_out = sb("v_out", [128, 1024], BF16)
        pmm = ps("pmm", [128, 512], F32)
        ptr = ps("ptr", [128, 128], BF16)
        pbc = ps("pbc", [128, 512], F32)
        tok = st.enter_context(nc.semaphore("tok"))

        ser.dma(lambda e: e.dma_start(ident[:, :], identb.ap()))
        ser.v(lambda e: e.memset(ones_row[:, :], 1.0))
        ser.dma(lambda e: e.dma_start(bvrow[:, :], dap(bv, 0, [[1024, 1], [1, 1024]])))
        for half in range(2):
            ser.t(lambda e, h=half: e.matmul(pbc[:, :], ones_row[:, :],
                                             bvrow[:, h * 512:(h + 1) * 512],
                                             start=True, stop=True))
            ser.v(lambda e, h=half: e.tensor_copy(bvb[:, h * 512:(h + 1) * 512], pbc[:, :]))

        for i in range(NT):
            ser.dma(lambda e, i=i: e.dma_start(
                xt[:, :], dap(x0, i * 128 * DM, [[DM, 128], [1, DM]])))
            xnb_i = (lambda i: lambda: xn_bf[:, i * DM:(i + 1) * DM])(i)
            emit_layernorm(ser, xt, _View(xn_bf, i * DM, DM), scr, mcol, vcol)
            for f in range(16):
                ser.t(lambda e, i=i, f=f: e.transpose(
                    ptr[:, :], xn_bf[:, i * DM + f * 128: i * DM + (f + 1) * 128],
                    ident[:, :]))
                ser.v(lambda e, i=i, f=f: e.tensor_copy(
                    xnT[:, f * SC + i * 128: f * SC + i * 128 + 128], ptr[:, :]))

        for f in range(16):
            ser.dma(lambda e, f=f: e.dma_start(
                dap(o_xnT, f * 128 * SC, [[SC, 128], [1, SC]]),
                xnT[:, f * SC:(f + 1) * SC]), eng="gpsimd")

        for fo in range(24):
            ser.dma(lambda e, fo=fo: e.dma_start(
                wch[:, :], dap(wqkT, fo * 128,
                               [[3072, 128], [128 * 3072, 16], [1, 128]])))
            ser.dma(lambda e, fo=fo: e.dma_start(
                bcol[:, :], dap(bqk, fo * 128, [[1, 128], [1, 1]])))
            for sh in range(2):
                for fi in range(16):
                    ser.t(lambda e, fi=fi, sh=sh: e.matmul(
                        pmm[:, :], wch[:, fi * 128:(fi + 1) * 128],
                        xnT[:, fi * SC + sh * 512: fi * SC + (sh + 1) * 512],
                        start=(fi == 0), stop=(fi == 15)), acc=(fi > 0))
                if fo < 16:
                    ser.v(lambda e, sh=sh: e.tensor_scalar(
                        qk_out[:, sh * 512:(sh + 1) * 512], pmm[:, :],
                        bcol[:, :], SCALE, ALU.add, ALU.mult))
                else:
                    ser.v(lambda e, sh=sh: e.tensor_scalar_add(
                        qk_out[:, sh * 512:(sh + 1) * 512], pmm[:, :], bcol[:, :]))
            ser.dma(lambda e, fo=fo: e.dma_start(
                dap(o_qkT, fo * 128 * SC, [[SC, 128], [1, 1024]]),
                qk_out[:, :]), eng="gpsimd")

        ser.dma(lambda e: e.dma_start(
            wv_sb[:, :], dap(wvT, 0, [[1024, 128], [128 * 1024, 16], [1, 1024]])))
        for i in range(NT):
            for vh in range(2):
                for fi in range(16):
                    ser.t(lambda e, i=i, vh=vh, fi=fi: e.matmul(
                        pmm[:, :], xnT[:, fi * SC + i * 128: fi * SC + (i + 1) * 128],
                        wv_sb[:, fi * 1024 + vh * 512: fi * 1024 + (vh + 1) * 512],
                        start=(fi == 0), stop=(fi == 15)), acc=(fi > 0))
                ser.v(lambda e, vh=vh: e.tensor_add(
                    v_out[:, vh * 512:(vh + 1) * 512], pmm[:, :],
                    bvb[:, vh * 512:(vh + 1) * 512]))
            ser.dma(lambda e, i=i: e.dma_start(
                dap(o_v, i * 128 * 1024, [[1024, 128], [1, 1024]]),
                v_out[:, :]), eng="gpsimd")

        ser.build(nc, tok)
    return nc


class _View:

    def __init__(self, t, off, width):
        self.t = t
        self.off = off
        self.width = width

    def __getitem__(self, key):
        p, f = key
        if isinstance(f, slice):
            start = f.start or 0
            stop = f.stop if f.stop is not None else self.width
        else:
            raise TypeError(f)
        return self.t[p, self.off + start: self.off + stop]



def build_B():
    nc = bass.Bass("TRN2", target_bir_lowering=False)
    q1T = nc.dram_tensor("q1T", [128, S], BF16, kind="ExternalInput")
    q2T = nc.dram_tensor("q2T", [128, S], BF16, kind="ExternalInput")
    k1T = nc.dram_tensor("k1T", [128, S], BF16, kind="ExternalInput")
    k2T = nc.dram_tensor("k2T", [128, S], BF16, kind="ExternalInput")
    vg = nc.dram_tensor("vg", [S, 256], BF16, kind="ExternalInput")
    lq1 = nc.dram_tensor("lq1", [128], F32, kind="ExternalInput")
    lk1 = nc.dram_tensor("lk1", [128], F32, kind="ExternalInput")
    lq2 = nc.dram_tensor("lq2", [128], F32, kind="ExternalInput")
    lk2 = nc.dram_tensor("lk2", [128], F32, kind="ExternalInput")
    subln = nc.dram_tensor("subln", [2, 128], F32, kind="ExternalInput")
    masksB = nc.dram_tensor("masksB", [128, 4 * 512], BF16, kind="ExternalInput")
    o_oT = nc.dram_tensor("oT", [256, S], BF16, kind="ExternalOutput")

    ser = Ser()
    with ExitStack() as st:
        def sb(name, shape, dtype):
            return st.enter_context(nc.sbuf_tensor(name, shape, dtype))

        def ps(name, shape, dtype):
            return st.enter_context(nc.psum_tensor(name, shape, dtype))

        qT = [sb("q1T_sb", [128, S], BF16), sb("q2T_sb", [128, S], BF16)]
        kT = [sb("k1T_sb", [128, S], BF16), sb("k2T_sb", [128, S], BF16)]
        v_sb = sb("v_sb", [128, 64 * 256], BF16)
        masks = sb("masks", [128, 4 * 512], BF16)
        ones_colb = sb("ones_colb", [128, 1], BF16)
        ones_colf = sb("ones_colf", [128, 1], F32)
        ones_rowf = sb("ones_rowf", [1, 128], F32)
        lrow = [sb(f"lrow{i}", [1, 128], F32) for i in range(2)]
        ee = [sb(f"ee{i}", [1, 1], F32) for i in range(2)]
        lamneg = sb("lamneg", [1, 1], F32)
        sw = [sb(f"sw{i}", [128, 1], F32) for i in range(2)]
        pT = sb("pT", [128, 512], BF16)
        pTm = sb("pTm", [128, 512], BF16)
        linv = [sb(f"linv{i}", [1, 512], F32) for i in range(2)]
        bsc = [sb(f"bsc{i}", [128, 512], F32) for i in range(2)]
        oc = [sb(f"oc{i}", [128, 512], F32) for i in range(2)]
        tmp = sb("tmpB", [128, 512], F32)
        sq = [sb(f"sq{i}", [128, 512], F32) for i in range(2)]
        rrow = sb("rrow", [1, 512], F32)
        rb = sb("rb", [128, 512], F32)
        ost = sb("ost", [128, 512], BF16)
        pS = ps("pS", [128, 512], F32)
        pO = [[ps(f"pO{m}{h}", [128, 512], F32) for h in range(2)] for m in range(2)]
        pL = [ps(f"pL{m}", [1, 512], F32) for m in range(2)]
        pB = ps("pB", [128, 512], F32)
        tok = st.enter_context(nc.semaphore("tok"))

        qkin = [q1T, q2T, k1T, k2T]
        for i, dst in enumerate(qT + kT):
            ser.dma(lambda e, d=dst, src=qkin[i]: e.dma_start(d[:, :], src.ap()))
        ser.dma(lambda e: e.dma_start(
            v_sb[:, :], dap(vg, 0, [[256, 128], [128 * 256, 64], [1, 256]])))
        ser.dma(lambda e: e.dma_start(masks[:, :], masksB.ap()))
        ser.v(lambda e: e.memset(ones_colb[:, :], 1.0))
        ser.v(lambda e: e.memset(ones_colf[:, :], 1.0))
        ser.v(lambda e: e.memset(ones_rowf[:, :], 1.0))
        lsrc = [(lq1, lk1), (lq2, lk2)]
        for m in range(2):
            a, b = lsrc[m]
            ser.dma(lambda e, a=a, m=m: e.dma_start(
                lrow[m][:, :], dap(a, 0, [[128, 1], [1, 128]])))
            ser.dma(lambda e, b=b, m=m: e.dma_start(
                pT[:1, :128] if False else tmp[:1, :128], dap(b, 0, [[128, 1], [1, 128]])))
            ser.v(lambda e, m=m: e.tensor_mul(lrow[m][:, :], lrow[m][:, :], tmp[:1, :128]))
            ser.v(lambda e, m=m: e.reduce_sum(ee[m][:, :], lrow[m][:, :], AX.X))
            ser.s(lambda e, m=m: e.activation(ee[m][:, :], ee[m][:, :], AF.Exp))
        ser.v(lambda e: e.tensor_sub(lamneg[:, :], ee[1][:, :], ee[0][:, :]))
        ser.v(lambda e: e.tensor_scalar_add(lamneg[:, :], lamneg[:, :],
                                            float(-LAMBDA_INIT)))
        for h in range(2):
            ser.dma(lambda e, h=h: e.dma_start(
                sw[h][:, :], dap(subln, h * 128, [[1, 128], [1, 1]])))
            ser.v(lambda e, h=h: e.tensor_scalar_mul(sw[h][:, :], sw[h][:, :],
                                                     float(1.0 - LAMBDA_INIT)))

        for qb in range(16):
            q0 = qb * 512
            nk = (q0 + 512) // 128
            for m in range(2):
                for kt in range(nk):
                    last = (kt == nk - 1)
                    ser.t(lambda e, m=m, kt=kt, q0=q0: e.matmul(
                        pS[:, :], kT[m][:, kt * 128:(kt + 1) * 128],
                        qT[m][:, q0:q0 + 512], start=True, stop=True))
                    straddle = kt * 128 >= q0
                    if straddle:
                        off = (kt * 128 - q0) // 128
                        ser.s(lambda e: e.activation(pTm[:, :], pS[:, :], AF.Exp))
                        ser.v(lambda e, off=off: e.tensor_mul(
                            pT[:, :], pTm[:, :], masks[:, off * 512:(off + 1) * 512]))
                    else:
                        ser.s(lambda e: e.activation(pT[:, :], pS[:, :], AF.Exp))
                    ser.t(lambda e, m=m, kt=kt, last=last: e.matmul(
                        pL[m][:, :], ones_colb[:, :], pT[:, :],
                        start=(kt == 0), stop=last))
                    for h in range(2):
                        ser.t(lambda e, m=m, h=h, kt=kt, last=last: e.matmul(
                            pO[m][h][:, :],
                            v_sb[:, kt * 256 + h * 128: kt * 256 + (h + 1) * 128],
                            pT[:, :], start=(kt == 0), stop=last))
            for m in range(2):
                ser.v(lambda e, m=m: e.reciprocal(linv[m][:, :], pL[m][:, :]))
            ser.v(lambda e: e.tensor_scalar_mul(linv[1][:, :], linv[1][:, :],
                                                lamneg[:1, :1]))
            for m in range(2):
                ser.t(lambda e, m=m: e.matmul(pB[:, :], ones_rowf[:, :],
                                              linv[m][:, :], start=True, stop=True))
                ser.v(lambda e, m=m: e.tensor_copy(bsc[m][:, :], pB[:, :]))
            for h in range(2):
                ser.v(lambda e, h=h: e.tensor_mul(oc[h][:, :], pO[0][h][:, :],
                                                  bsc[0][:, :]))
                ser.v(lambda e, h=h: e.tensor_mul(tmp[:, :], pO[1][h][:, :],
                                                  bsc[1][:, :]))
                ser.v(lambda e, h=h: e.tensor_add(oc[h][:, :], oc[h][:, :],
                                                  tmp[:, :]))
                ser.s(lambda e, h=h: e.activation(sq[h][:, :], oc[h][:, :],
                                                  AF.Square))
                ser.t(lambda e, h=h: e.matmul(pL[0][:, :], ones_colf[:, :],
                                              sq[h][:, :], start=(h == 0),
                                              stop=(h == 1)))
            ser.v(lambda e: e.tensor_scalar(rrow[:, :], pL[0][:, :], 1.0 / 256,
                                            EPS, ALU.mult, ALU.add))
            ser.v(lambda e: e.reciprocal(rrow[:, :], rrow[:, :]))
            ser.s(lambda e: e.activation(rrow[:, :], rrow[:, :], AF.Sqrt))
            ser.t(lambda e: e.matmul(pB[:, :], ones_rowf[:, :], rrow[:, :],
                                     start=True, stop=True))
            ser.v(lambda e: e.tensor_copy(rb[:, :], pB[:, :]))
            for h in range(2):
                ser.v(lambda e, h=h: e.scalar_tensor_tensor(
                    ost[:, :], oc[h][:, :], sw[h][:, :], rb[:, :],
                    ALU.mult, ALU.mult))
                ser.dma(lambda e, h=h, q0=q0: e.dma_start(
                    dap(o_oT, h * 128 * S + q0, [[S, 128], [1, 512]]),
                    ost[:, :]), eng="gpsimd")

        ser.build(nc, tok)
    return nc



def build_CD():
    nc = bass.Bass("TRN2", target_bir_lowering=False)
    oTs = nc.dram_tensor("oTs", [DM, SC], BF16, kind="ExternalInput")
    woT = nc.dram_tensor("woT", [DM, DM], BF16, kind="ExternalInput")
    outb = nc.dram_tensor("outb", [DM], F32, kind="ExternalInput")
    xn0Ts = nc.dram_tensor("xn0Ts", [DM, SC], BF16, kind="ExternalInput")
    gwT = nc.dram_tensor("gwT", [DM, INTER0], BF16, kind="ExternalInput")
    uwT = nc.dram_tensor("uwT", [DM, INTER0], BF16, kind="ExternalInput")
    gs = nc.dram_tensor("gs", [INTER0], F32, kind="ExternalInput")
    us = nc.dram_tensor("us", [INTER0], F32, kind="ExternalInput")
    dwT = nc.dram_tensor("dwT", [INTER0, DM], BF16, kind="ExternalInput")
    ds = nc.dram_tensor("ds", [DM], F32, kind="ExternalInput")
    x0 = nc.dram_tensor("x0", [SC, DM], F32, kind="ExternalInput")
    winT = nc.dram_tensor("winT", [DM, PROJ], BF16, kind="ExternalInput")
    identb = nc.dram_tensor("identb", [128, 128], BF16, kind="ExternalInput")
    attnT_h = nc.dram_tensor("attnT_h", [DM, SC], BF16)
    o_x1 = nc.dram_tensor("x1", [SC, DM], F32, kind="ExternalOutput")
    o_xn1T = nc.dram_tensor("xn1T", [DM, SC], BF16, kind="ExternalOutput")
    o_gate = nc.dram_tensor("gate", [SC, M_INTER], BF16, kind="ExternalOutput")
    o_hbct = nc.dram_tensor("hbct", [CONV_DIM, SC], BF16, kind="ExternalOutput")
    o_dtt = nc.dram_tensor("dtt", [MH, SC], F32, kind="ExternalOutput")

    ser = Ser()
    NT = SC // 128
    with ExitStack() as st:
        def sb(name, shape, dtype):
            return st.enter_context(nc.sbuf_tensor(name, shape, dtype))

        def ps(name, shape, dtype):
            return st.enter_context(nc.psum_tensor(name, shape, dtype))

        big_a = sb("big_a", [128, 16 * SC], BF16)
        big_b = sb("big_b", [128, 16 * SC], BF16)
        hbig = sb("hbig", [128, 44 * SC], BF16)
        wch = sb("wch", [128, 2048], BF16)
        wch2 = sb("wch2", [128, 2048], BF16)
        wch44 = sb("wch44", [128, 44 * 128], BF16)
        scol = sb("scol", [128, 1], F32)
        scol2 = sb("scol2", [128, 1], F32)
        mcol = sb("mcol", [128, 1], F32)
        vcol = sb("vcol", [128, 1], F32)
        ident = sb("ident", [128, 128], BF16)
        tmpf = sb("tmpf", [128, 512], F32)
        tmpf2 = sb("tmpf2", [128, 512], F32)
        tmpf3 = sb("tmpf3", [128, 512], F32)
        st_ch = sb("st_ch", [128, 1024], BF16)
        dt_st = sb("dt_st", [32, 1024], F32)
        piece = sb("piece", [128, 128], BF16)
        x1t = sb("x1t", [128, DM], F32)
        scr = sb("scr", [128, DM], F32)
        xnbf = sb("xnbf", [128, DM], BF16)
        pg = ps("pg", [128, 512], F32)
        pu = ps("pu", [128, 512], F32)
        ptr = ps("ptr", [128, 128], BF16)
        tok = st.enter_context(nc.semaphore("tok"))

        ser.dma(lambda e: e.dma_start(ident[:, :], identb.ap()))

        ser.dma(lambda e: e.dma_start(
            big_a[:, :], dap(oTs, 0, [[SC, 128], [128 * SC, 16], [1, SC]])))
        for fo in range(16):
            ser.dma(lambda e, fo=fo: e.dma_start(
                wch[:, :], dap(woT, fo * 128, [[DM, 128], [128 * DM, 16], [1, 128]])))
            ser.dma(lambda e, fo=fo: e.dma_start(
                scol[:, :], dap(outb, fo * 128, [[1, 128], [1, 1]])))
            for sh in range(2):
                for fi in range(16):
                    ser.t(lambda e, fi=fi, sh=sh: e.matmul(
                        pg[:, :], wch[:, fi * 128:(fi + 1) * 128],
                        big_a[:, fi * SC + sh * 512: fi * SC + (sh + 1) * 512],
                        start=(fi == 0), stop=(fi == 15)), acc=(fi > 0))
                ser.v(lambda e, sh=sh: e.tensor_scalar_add(
                    st_ch[:, sh * 512:(sh + 1) * 512], pg[:, :], scol[:, :]))
            ser.dma(lambda e, fo=fo: e.dma_start(
                dap(attnT_h, fo * 128 * SC, [[SC, 128], [1, SC]]),
                st_ch[:, :]), eng="gpsimd")

        ser.dma(lambda e: e.dma_start(
            big_a[:, :], dap(xn0Ts, 0, [[SC, 128], [128 * SC, 16], [1, SC]])))
        for fo in range(44):
            ser.dma(lambda e, fo=fo: e.dma_start(
                wch[:, :], dap(gwT, fo * 128,
                               [[INTER0, 128], [128 * INTER0, 16], [1, 128]])))
            ser.dma(lambda e, fo=fo: e.dma_start(
                wch2[:, :], dap(uwT, fo * 128,
                                [[INTER0, 128], [128 * INTER0, 16], [1, 128]])))
            ser.dma(lambda e, fo=fo: e.dma_start(
                scol[:, :], dap(gs, fo * 128, [[1, 128], [1, 1]])))
            ser.dma(lambda e, fo=fo: e.dma_start(
                scol2[:, :], dap(us, fo * 128, [[1, 128], [1, 1]])))
            for sh in range(2):
                for fi in range(16):
                    ser.t(lambda e, fi=fi, sh=sh: e.matmul(
                        pg[:, :], wch[:, fi * 128:(fi + 1) * 128],
                        big_a[:, fi * SC + sh * 512: fi * SC + (sh + 1) * 512],
                        start=(fi == 0), stop=(fi == 15)), acc=(fi > 0))
                for fi in range(16):
                    ser.t(lambda e, fi=fi, sh=sh: e.matmul(
                        pu[:, :], wch2[:, fi * 128:(fi + 1) * 128],
                        big_a[:, fi * SC + sh * 512: fi * SC + (sh + 1) * 512],
                        start=(fi == 0), stop=(fi == 15)), acc=(fi > 0))
                ser.v(lambda e: e.tensor_scalar_mul(tmpf[:, :], pg[:, :], scol[:, :]))
                ser.s(lambda e: e.activation(tmpf2[:, :], tmpf[:, :], AF.Silu))
                ser.v(lambda e: e.tensor_scalar_mul(tmpf3[:, :], pu[:, :], scol2[:, :]))
                ser.v(lambda e, fo=fo, sh=sh: e.tensor_mul(
                    hbig[:, fo * SC + sh * 512: fo * SC + (sh + 1) * 512],
                    tmpf2[:, :], tmpf3[:, :]))
        for do in range(16):
            ser.dma(lambda e, do=do: e.dma_start(
                wch44[:, :], dap(dwT, do * 128, [[DM, 128], [128 * DM, 44], [1, 128]])))
            ser.dma(lambda e, do=do: e.dma_start(
                scol[:, :], dap(ds, do * 128, [[1, 128], [1, 1]])))
            for sh in range(2):
                for fi in range(44):
                    ser.t(lambda e, fi=fi, sh=sh: e.matmul(
                        pg[:, :], wch44[:, fi * 128:(fi + 1) * 128],
                        hbig[:, fi * SC + sh * 512: fi * SC + (sh + 1) * 512],
                        start=(fi == 0), stop=(fi == 43)), acc=(fi > 0))
                ser.v(lambda e, do=do, sh=sh: e.tensor_scalar_mul(
                    big_b[:, do * SC + sh * 512: do * SC + (sh + 1) * 512],
                    pg[:, :], scol[:, :]))

        for i in range(NT):
            ser.dma(lambda e, i=i: e.dma_start(
                x1t[:, :], dap(x0, i * 128 * DM, [[DM, 128], [1, DM]])))
            for f in range(16):
                ser.dma(lambda e, i=i, f=f: e.dma_start(
                    piece[:, :], dap(attnT_h, f * 128 * SC + i * 128,
                                     [[SC, 128], [1, 128]])))
                ser.t(lambda e: e.transpose(ptr[:, :], piece[:, :], ident[:, :]))
                ser.v(lambda e, f=f: e.tensor_add(
                    x1t[:, f * 128:(f + 1) * 128], x1t[:, f * 128:(f + 1) * 128],
                    ptr[:, :]))
                ser.t(lambda e, i=i, f=f: e.transpose(
                    ptr[:, :], big_b[:, f * SC + i * 128: f * SC + (i + 1) * 128],
                    ident[:, :]))
                ser.v(lambda e, f=f: e.tensor_add(
                    x1t[:, f * 128:(f + 1) * 128], x1t[:, f * 128:(f + 1) * 128],
                    ptr[:, :]))
            ser.dma(lambda e, i=i: e.dma_start(
                dap(o_x1, i * 128 * DM, [[DM, 128], [1, DM]]), x1t[:, :]),
                eng="gpsimd")
            emit_layernorm(ser, x1t, xnbf, scr, mcol, vcol)
            for f in range(16):
                ser.t(lambda e, f=f: e.transpose(
                    ptr[:, :], xnbf[:, f * 128:(f + 1) * 128], ident[:, :]))
                ser.v(lambda e: e.tensor_copy(piece[:, :], ptr[:, :]))
                ser.dma(lambda e, i=i, f=f: e.dma_start(
                    dap(o_xn1T, f * 128 * SC + i * 128, [[SC, 128], [1, 128]]),
                    piece[:, :]), eng="gpsimd")

        ser.dma(lambda e: e.dma_start(
            big_a[:, :], dap(o_xn1T, 0, [[SC, 128], [128 * SC, 16], [1, SC]])))
        for fo in range(35):
            w = 128 if fo < 34 else 32
            ser.dma(lambda e, fo=fo, w=w: e.dma_start(
                wch[:, :16 * w], dap(winT, fo * 128,
                                     [[PROJ, 128], [128 * PROJ, 16], [1, w]])))
            for sh in range(2):
                for fi in range(16):
                    ser.t(lambda e, fi=fi, sh=sh, w=w: e.matmul(
                        pg[:w, :], wch[:, fi * w:(fi + 1) * w],
                        big_a[:, fi * SC + sh * 512: fi * SC + (sh + 1) * 512],
                        start=(fi == 0), stop=(fi == 15)), acc=(fi > 0))
                if fo < 16:
                    ser.v(lambda e, sh=sh: e.tensor_copy(
                        st_ch[:, sh * 512:(sh + 1) * 512], pg[:, :]))
                elif fo < 34:
                    ser.v(lambda e, sh=sh: e.tensor_copy(
                        st_ch[:, sh * 512:(sh + 1) * 512], pg[:, :]))
                else:
                    ser.v(lambda e, sh=sh: e.tensor_copy(
                        dt_st[:, sh * 512:(sh + 1) * 512], pg[:32, :]))
            if fo < 16:
                for i in range(NT):
                    ser.t(lambda e, i=i: e.transpose(
                        ptr[:, :], st_ch[:, i * 128:(i + 1) * 128], ident[:, :]))
                    ser.v(lambda e: e.tensor_copy(piece[:, :], ptr[:, :]))
                    ser.dma(lambda e, i=i, fo=fo: e.dma_start(
                        dap(o_gate, i * 128 * M_INTER + fo * 128,
                            [[M_INTER, 128], [1, 128]]), piece[:, :]),
                        eng="gpsimd")
            elif fo < 34:
                ser.dma(lambda e, fo=fo: e.dma_start(
                    dap(o_hbct, (fo - 16) * 128 * SC, [[SC, 128], [1, SC]]),
                    st_ch[:, :]), eng="gpsimd")
            else:
                ser.dma(lambda e: e.dma_start(
                    dap(o_dtt, 0, [[SC, 32], [1, SC]]), dt_st[:, :]),
                    eng="gpsimd")

        ser.build(nc, tok)
    return nc



def build_E():
    nc = bass.Bass("TRN2", target_bir_lowering=False)
    xs_in = [nc.dram_tensor(f"xs{j}", [128, S], BF16, kind="ExternalInput")
             for j in range(2)]
    bt_in = nc.dram_tensor("bt_in", [128, S], BF16, kind="ExternalInput")
    ct_in = nc.dram_tensor("ct_in", [128, S], BF16, kind="ExternalInput")
    cw = [nc.dram_tensor(f"cw{j}", [128, MK], F32, kind="ExternalInput")
          for j in range(4)]
    cb = [nc.dram_tensor(f"cb{j}", [128], F32, kind="ExternalInput")
          for j in range(4)]
    dtr = nc.dram_tensor("dtr", [4, S], F32, kind="ExternalInput")
    dtb = nc.dram_tensor("dtb", [4], F32, kind="ExternalInput")
    alog = nc.dram_tensor("alog", [4], F32, kind="ExternalInput")
    dpar = nc.dram_tensor("dpar", [4], F32, kind="ExternalInput")
    ut01 = nc.dram_tensor("ut01", [128, 128], F32, kind="ExternalInput")
    eye4d = nc.dram_tensor("eye4d", [4, 4], F32, kind="ExternalInput")
    identb = nc.dram_tensor("identb", [128, 128], BF16, kind="ExternalInput")
    idb2d = nc.dram_tensor("idb2d", [128, 64], BF16, kind="ExternalInput")
    o_y = nc.dram_tensor("ynat", [S, 256], BF16, kind="ExternalOutput")

    ser = Ser()
    with ExitStack() as st:
        def sb(name, shape, dtype):
            return st.enter_context(nc.sbuf_tensor(name, shape, dtype))

        def ps(name, shape, dtype):
            return st.enter_context(nc.psum_tensor(name, shape, dtype))

        xsc = [sb(f"xsc{j}", [128, S], BF16) for j in range(2)]
        btc = sb("btc", [128, S], BF16)
        ctc = sb("ctc", [128, S], BF16)
        u_t = sb("u_t", [128, S + 3], BF16)
        acc = sb("acc", [128, 2048], F32)
        dt4 = sb("dt4", [4, S], F32)
        scr4 = sb("scr4", [4, S], F32)
        ag4 = sb("ag4", [4, S], F32)
        wcol = sb("wcol", [128, MK], F32)
        bcolE = sb("bcolE", [128, 1], F32)
        identB = sb("identB", [128, 128], BF16)
        idb2 = sb("idb2", [128, 64], BF16)
        ut = sb("ut", [128, 128], F32)
        eye4 = sb("eye4", [4, 4], F32)
        ones_rowf = sb("ones_rowfE", [1, 128], F32)
        prow = sb("prow", [1, 4], F32)
        arow = sb("arow", [1, 4], F32)
        drow = sb("drow", [1, 4], F32)
        acol = sb("acol", [4, 1], F32)
        bcol4 = sb("bcol4", [4, 1], F32)
        dcol = [sb(f"dcol{h}", [128, 1], F32) for h in range(4)]
        H = [[sb(f"H{h}_{pp}", [128, MP], F32) for pp in range(2)] for h in range(4)]
        Hbf = sb("Hbf", [128, MP], BF16)
        al = sb("al", [1, 128], F32)
        prevs = sb("prevs", [1, 1], F32)
        negalT = sb("negalT", [128, 1], F32)
        eac = sb("eac", [128, 1], F32)
        dtn = sb("dtn", [128, 1], F32)
        eeh = sb("eeh", [1, 1], F32)
        ebc = sb("ebc", [128, 1], F32)
        BCm = sb("BCm", [128, 128], F32)
        Bnat = sb("Bnat", [128, 128], BF16)
        dec_sb = sb("dec_sb", [128, 128], F32)
        dec2 = sb("dec2", [128, 128], F32)
        Mt = sb("Mt", [128, 128], BF16)
        xnat = sb("xnat", [128, MP], BF16)
        yv = sb("yv", [128, MP], F32)
        y2 = sb("y2", [128, MP], F32)
        dxt = sb("dxt", [128, MP], F32)
        Ssb = sb("Ssb", [128, MP], F32)
        yst = sb("yst", [128, 256], BF16)
        psA = ps("psA", [128, 128], F32)
        psB = ps("psB", [128, 128], F32)
        ptrb = ps("ptrb", [128, 128], BF16)
        ptf = ps("ptf", [128, 128], F32)
        psY = ps("psY", [128, MP], F32)
        psYi = ps("psYi", [128, MP], F32)
        psS = ps("psS", [128, MP], F32)
        psE = ps("psE", [128, 1], F32)
        tok = st.enter_context(nc.semaphore("tok"))

        ser.dma(lambda e: e.dma_start(identB[:, :], identb.ap()))
        ser.dma(lambda e: e.dma_start(idb2[:, :], idb2d.ap()))
        ser.dma(lambda e: e.dma_start(ut[:, :], ut01.ap()))
        ser.dma(lambda e: e.dma_start(eye4[:, :], eye4d.ap()))
        ser.v(lambda e: e.memset(ones_rowf[:, :], 1.0))

        conv_src = [xs_in[0], xs_in[1], bt_in, ct_in]
        conv_dst = [xsc[0], xsc[1], btc, ctc]
        for j in range(4):
            ser.v(lambda e: e.memset(u_t[:, :3], 0.0))
            ser.dma(lambda e, j=j: e.dma_start(u_t[:, 3:], conv_src[j].ap()))
            ser.dma(lambda e, j=j: e.dma_start(wcol[:, :], cw[j].ap()))
            ser.dma(lambda e, j=j: e.dma_start(
                bcolE[:, :], dap(cb[j], 0, [[1, 128], [1, 1]])))
            for q in range(4):
                q0 = q * 2048
                ser.v(lambda e, q0=q0: e.tensor_scalar_mul(
                    acc[:, :], u_t[:, q0:q0 + 2048], wcol[:, 0:1]))
                for kk in range(1, MK):
                    ser.v(lambda e, q0=q0, kk=kk: e.scalar_tensor_tensor(
                        acc[:, :], u_t[:, q0 + kk:q0 + kk + 2048],
                        wcol[:, kk:kk + 1], acc[:, :], ALU.mult, ALU.add))
                ser.s(lambda e, j=j, q0=q0: e.activation(
                    conv_dst[j][:, q0:q0 + 2048], acc[:, :], AF.Silu,
                    bias=bcolE[:, :]))

        ser.dma(lambda e: e.dma_start(prow[:, :], dap(dtb, 0, [[4, 1], [1, 4]])))
        ser.dma(lambda e: e.dma_start(arow[:, :], dap(alog, 0, [[4, 1], [1, 4]])))
        ser.dma(lambda e: e.dma_start(drow[:, :], dap(dpar, 0, [[4, 1], [1, 4]])))
        ser.dma(lambda e: e.dma_start(dt4[:, :], dtr.ap()))
        ser.s(lambda e: e.activation(arow[:, :], arow[:, :], AF.Exp))
        ser.v(lambda e: e.tensor_scalar_mul(arow[:, :], arow[:, :], -1.0))
        ser.t(lambda e: e.matmul(psE[:4, :1], arow[:, :], ones_rowf[:1, :1],
                                 start=True, stop=True))
        ser.v(lambda e: e.tensor_copy(acol[:, :], psE[:4, :1]))
        ser.t(lambda e: e.matmul(psE[:4, :1], prow[:, :], ones_rowf[:1, :1],
                                 start=True, stop=True))
        ser.v(lambda e: e.tensor_copy(bcol4[:, :], psE[:4, :1]))
        ser.v(lambda e: e.tensor_scalar_add(dt4[:, :], dt4[:, :], bcol4[:, :]))
        ser.s(lambda e: e.activation(scr4[:, :], dt4[:, :], AF.Abs))
        ser.s(lambda e: e.activation(scr4[:, :], scr4[:, :], AF.Exp, scale=-1.0))
        ser.s(lambda e: e.activation(scr4[:, :], scr4[:, :], AF.Ln, bias=1.0))
        ser.v(lambda e: e.tensor_scalar_max(dt4[:, :], dt4[:, :], 0.0))
        ser.v(lambda e: e.tensor_add(dt4[:, :], dt4[:, :], scr4[:, :]))
        ser.v(lambda e: e.tensor_scalar_mul(scr4[:, :], dt4[:, :], acol[:, :]))
        ser.v(lambda e: e.tensor_tensor_scan(ag4[:, :], scr4[:, :], scr4[:, :],
                                             0.0, ALU.add, ALU.bypass))
        for h in range(4):
            ser.t(lambda e, h=h: e.matmul(psE[:, :], ones_rowf[:, :],
                                          drow[:1, h:h + 1], start=True, stop=True))
            ser.v(lambda e, h=h: e.tensor_copy(dcol[h][:, :], psE[:, :]))
            ser.v(lambda e, h=h: e.memset(H[h][0][:, :], 0.0))

        for t in range(NCHUNK):
            t0 = t * 128
            cur, nxt = t % 2, (t + 1) % 2
            ser.t(lambda e, t0=t0: e.matmul(psA[:, :], btc[:, t0:t0 + 128],
                                            ctc[:, t0:t0 + 128], start=True, stop=True))
            ser.v(lambda e: e.tensor_mul(BCm[:, :], psA[:, :], ut[:, :]))
            ser.t(lambda e, t0=t0: e.transpose(ptrb[:, :], btc[:, t0:t0 + 128],
                                               identB[:, :]))
            ser.v(lambda e: e.tensor_copy(Bnat[:, :], ptrb[:, :]))
            for h in range(4):
                c0 = h // 2
                p0 = (h % 2) * 64
                if t > 0:
                    ser.t(lambda e, h=h, t0=t0: e.matmul(
                        ptf[:1, :1], eye4[:, h:h + 1], ag4[:4, t0 - 1:t0],
                        start=True, stop=True))
                    ser.v(lambda e: e.tensor_copy(prevs[:, :], ptf[:1, :1]))
                ser.t(lambda e, h=h, t0=t0: e.matmul(
                    ptf[:1, :128], eye4[:, h:h + 1], ag4[:4, t0:t0 + 128],
                    start=True, stop=True))
                if t > 0:
                    ser.v(lambda e: e.tensor_scalar_sub(al[:, :], ptf[:1, :128],
                                                        prevs[:1, :1]))
                else:
                    ser.v(lambda e: e.tensor_copy(al[:, :], ptf[:1, :128]))
                ser.t(lambda e: e.matmul(ptf[:, :1], al[:, :], ones_rowf[:1, :1],
                                         start=True, stop=True))
                ser.s(lambda e: e.activation(eac[:, :], ptf[:, :1], AF.Exp))
                ser.v(lambda e: e.tensor_scalar_mul(negalT[:, :], ptf[:, :1], -1.0))
                ser.t(lambda e, h=h, t0=t0: e.matmul(
                    ptf[:, :1], dt4[:4, t0:t0 + 128], eye4[:, h:h + 1],
                    start=True, stop=True))
                ser.v(lambda e: e.tensor_copy(dtn[:, :], ptf[:, :1]))
                ser.t(lambda e: e.matmul(psB[:, :], ones_rowf[:, :], al[:, :],
                                         start=True, stop=True))
                ser.v(lambda e: e.tensor_scalar(dec_sb[:, :], psB[:, :],
                                                negalT[:, :], 0.0,
                                                ALU.add, ALU.min))
                ser.s(lambda e: e.activation(dec_sb[:, :], dec_sb[:, :], AF.Exp))
                ser.v(lambda e: e.tensor_scalar_mul(dec2[:, :], dec_sb[:, :],
                                                    dtn[:, :]))
                ser.v(lambda e: e.tensor_mul(Mt[:, :], dec2[:, :], BCm[:, :]))
                idap = (lambda e: identB[:64, :64]) if p0 == 0 else (
                    lambda e: idb2[64:128, :64])
                ser.t(lambda e, c0=c0, p0=p0, t0=t0, idap=idap: e.transpose(
                    ptrb[:, :MP], xsc[c0][p0:p0 + 64, t0:t0 + 128], idap(e)))
                ser.v(lambda e: e.tensor_copy(xnat[:, :], ptrb[:, :MP]))
                ser.t(lambda e: e.matmul(psY[:, :], Mt[:, :], xnat[:, :],
                                         start=True, stop=True))
                ser.v(lambda e, h=h, cur=cur: e.tensor_copy(Hbf[:, :],
                                                            H[h][cur][:, :]))
                ser.t(lambda e, t0=t0: e.matmul(
                    psYi[:, :], ctc[:, t0:t0 + 128], Hbf[:, :],
                    start=True, stop=True))
                ser.v(lambda e: e.tensor_scalar_mul(yv[:, :], psYi[:, :], eac[:, :]))
                ser.v(lambda e: e.tensor_add(y2[:, :], psY[:, :], yv[:, :]))
                ser.v(lambda e, h=h: e.tensor_scalar_mul(dxt[:, :], xnat[:, :],
                                                         dcol[h][:, :]))
                ser.v(lambda e, h=h: e.tensor_add(
                    yst[:, h * MP:(h + 1) * MP], dxt[:, :], y2[:, :]))
                ser.v(lambda e: e.tensor_scalar_mul(Mt[:, :], Bnat[:, :],
                                                    dec2[:, 127:128]))
                ser.t(lambda e: e.matmul(psS[:, :], Mt[:, :], xnat[:, :],
                                         start=True, stop=True))
                ser.v(lambda e: e.tensor_copy(Ssb[:, :], psS[:, :]))
                ser.s(lambda e: e.activation(eeh[:, :], al[:1, 127:128], AF.Exp))
                ser.t(lambda e: e.matmul(psE[:, :], ones_rowf[:, :], eeh[:, :],
                                         start=True, stop=True))
                ser.v(lambda e: e.tensor_copy(ebc[:, :], psE[:, :]))
                ser.v(lambda e, h=h, cur=cur, nxt=nxt: e.scalar_tensor_tensor(
                    H[h][nxt][:, :], H[h][cur][:, :], ebc[:, :], Ssb[:, :],
                    ALU.mult, ALU.add))
            ser.dma(lambda e, t0=t0: e.dma_start(
                dap(o_y, t0 * 256, [[256, 128], [1, 256]]), yst[:, :]),
                eng="gpsimd")

        ser.build(nc, tok)
    return nc



def build_F():
    nc = bass.Bass("TRN2", target_bir_lowering=False)
    ynat = nc.dram_tensor("ynat", [SC, M_INTER], BF16, kind="ExternalInput")
    gate = nc.dram_tensor("gate", [SC, M_INTER], BF16, kind="ExternalInput")
    nrm = nc.dram_tensor("nrm", [DM], F32, kind="ExternalInput")
    woutT = nc.dram_tensor("woutT", [DM, DM], BF16, kind="ExternalInput")
    xn1Ts = nc.dram_tensor("xn1Ts", [DM, SC], BF16, kind="ExternalInput")
    wg0 = nc.dram_tensor("wg0", [DM, FF1], BF16, kind="ExternalInput")
    wg1 = nc.dram_tensor("wg1", [DM, FF1], BF16, kind="ExternalInput")
    wlin = nc.dram_tensor("wlin", [FF1, DM], BF16, kind="ExternalInput")
    x1 = nc.dram_tensor("x1", [SC, DM], F32, kind="ExternalInput")
    identb = nc.dram_tensor("identb", [128, 128], BF16, kind="ExternalInput")
    moT_h = nc.dram_tensor("moT_h", [DM, SC], BF16)
    o_out = nc.dram_tensor("out", [SC, DM], F32, kind="ExternalOutput")

    ser = Ser()
    NT = SC // 128
    GC = float(np.sqrt(2.0 / np.pi))
    with ExitStack() as st:
        def sb(name, shape, dtype):
            return st.enter_context(nc.sbuf_tensor(name, shape, dtype))

        def ps(name, shape, dtype):
            return st.enter_context(nc.psum_tensor(name, shape, dtype))

        gnT = sb("gnT", [128, 16 * SC], BF16)
        xn1T = sb("xn1T_sbF", [128, 16 * SC], BF16)
        hid = sb("hid", [128, 64 * 256], BF16)
        fT = sb("fT", [128, 16 * 256], BF16)
        wch = sb("wchF", [128, 2048], BF16)
        wch2 = sb("wch2F", [128, 2048], BF16)
        wch64 = sb("wch64", [128, 64 * 128], BF16)
        nwb = sb("nwb", [128, 2048], F32)
        nrow = sb("nrow", [1, 2048], F32)
        ones_rowf = sb("ones_rowfF", [1, 128], F32)
        ybf = sb("ybf", [128, 2048], BF16)
        gbf = sb("gbf", [128, 2048], BF16)
        sgate = sb("sgate", [128, 2048], BF16)
        g_f = sb("g_f", [128, 2048], F32)
        scr = sb("scrF", [128, 2048], F32)
        gnw = sb("gnw", [128, 2048], BF16)
        ident = sb("identF_", [128, 128], BF16)
        sscol = sb("sscol", [128, 1], F32)
        x2s = sb("x2s", [128, 256], F32)
        x3s = sb("x3s", [128, 256], F32)
        inn = sb("inn", [128, 256], F32)
        ths = sb("ths", [128, 256], F32)
        t3s = sb("t3s", [128, 256], F32)
        ot = sb("ot", [128, 2048], F32)
        piece = sb("pieceF", [128, 128], BF16)
        pg0 = ps("pg0", [128, 512], F32)
        pg1 = ps("pg1", [128, 512], F32)
        ptr = ps("ptrF", [128, 128], BF16)
        pbc = ps("pbcF", [128, 512], F32)
        tok = st.enter_context(nc.semaphore("tok"))

        ser.dma(lambda e: e.dma_start(ident[:, :], identb.ap()))
        ser.v(lambda e: e.memset(ones_rowf[:, :], 1.0))
        ser.dma(lambda e: e.dma_start(nrow[:, :], dap(nrm, 0, [[DM, 1], [1, DM]])))
        for q in range(4):
            ser.t(lambda e, q=q: e.matmul(pbc[:, :], ones_rowf[:, :],
                                          nrow[:1, q * 512:(q + 1) * 512],
                                          start=True, stop=True))
            ser.v(lambda e, q=q: e.tensor_copy(nwb[:, q * 512:(q + 1) * 512],
                                               pbc[:, :]))
        ser.dma(lambda e: e.dma_start(
            xn1T[:, :], dap(xn1Ts, 0, [[SC, 128], [128 * SC, 16], [1, SC]])))

        for i in range(NT):
            ser.dma(lambda e, i=i: e.dma_start(
                ybf[:, :], dap(ynat, i * 128 * M_INTER, [[M_INTER, 128], [1, M_INTER]])))
            ser.dma(lambda e, i=i: e.dma_start(
                gbf[:, :], dap(gate, i * 128 * M_INTER, [[M_INTER, 128], [1, M_INTER]])))
            ser.s(lambda e: e.activation(sgate[:, :], gbf[:, :], AF.Silu))
            ser.v(lambda e: e.tensor_mul(g_f[:, :], ybf[:, :], sgate[:, :]))
            ser.s(lambda e: e.activation(scr[:, :], g_f[:, :], AF.Square,
                                         accum_out=sscol[:, :]))
            ser.v(lambda e: e.tensor_scalar(sscol[:, :], sscol[:, :], 1.0 / DM,
                                            EPS, ALU.mult, ALU.add))
            ser.v(lambda e: e.reciprocal(sscol[:, :], sscol[:, :]))
            ser.s(lambda e: e.activation(sscol[:, :], sscol[:, :], AF.Sqrt))
            ser.v(lambda e: e.tensor_scalar_mul(g_f[:, :], g_f[:, :], sscol[:, :]))
            ser.v(lambda e: e.tensor_mul(gnw[:, :], g_f[:, :], nwb[:, :]))
            for f in range(16):
                ser.t(lambda e, f=f: e.transpose(
                    ptr[:, :], gnw[:, f * 128:(f + 1) * 128], ident[:, :]))
                ser.v(lambda e, f=f, i=i: e.tensor_copy(
                    gnT[:, f * SC + i * 128: f * SC + (i + 1) * 128], ptr[:, :]))
        for fo in range(16):
            ser.dma(lambda e, fo=fo: e.dma_start(
                wch[:, :], dap(woutT, fo * 128, [[DM, 128], [128 * DM, 16], [1, 128]])))
            for sh in range(2):
                for fi in range(16):
                    ser.t(lambda e, fi=fi, sh=sh: e.matmul(
                        pg0[:, :], wch[:, fi * 128:(fi + 1) * 128],
                        gnT[:, fi * SC + sh * 512: fi * SC + (sh + 1) * 512],
                        start=(fi == 0), stop=(fi == 15)), acc=(fi > 0))
                ser.v(lambda e, sh=sh: e.tensor_copy(
                    ybf[:, sh * 512:(sh + 1) * 512], pg0[:, :]))
            ser.dma(lambda e, fo=fo: e.dma_start(
                dap(moT_h, fo * 128 * SC, [[SC, 128], [1, SC]]),
                ybf[:, :1024]), eng="gpsimd")

        for qf in range(4):
            s0 = qf * 256
            for fo in range(64):
                ser.dma(lambda e, fo=fo: e.dma_start(
                    wch[:, :], dap(wg0, fo * 128,
                                   [[FF1, 128], [128 * FF1, 16], [1, 128]])))
                ser.dma(lambda e, fo=fo: e.dma_start(
                    wch2[:, :], dap(wg1, fo * 128,
                                    [[FF1, 128], [128 * FF1, 16], [1, 128]])))
                for fi in range(16):
                    ser.t(lambda e, fi=fi, s0=s0: e.matmul(
                        pg0[:, :256], wch[:, fi * 128:(fi + 1) * 128],
                        xn1T[:, fi * SC + s0: fi * SC + s0 + 256],
                        start=(fi == 0), stop=(fi == 15)), acc=(fi > 0))
                for fi in range(16):
                    ser.t(lambda e, fi=fi, s0=s0: e.matmul(
                        pg1[:, :256], wch2[:, fi * 128:(fi + 1) * 128],
                        xn1T[:, fi * SC + s0: fi * SC + s0 + 256],
                        start=(fi == 0), stop=(fi == 15)), acc=(fi > 0))
                ser.s(lambda e: e.activation(x2s[:, :], pg0[:, :256], AF.Square))
                ser.v(lambda e: e.tensor_mul(x3s[:, :], x2s[:, :], pg0[:, :256]))
                ser.v(lambda e: e.scalar_tensor_tensor(
                    inn[:, :], x3s[:, :], 0.044715, pg0[:, :256], ALU.mult, ALU.add))
                ser.s(lambda e: e.activation(ths[:, :], inn[:, :], AF.Tanh, scale=GC))
                ser.v(lambda e: e.tensor_scalar(ths[:, :], ths[:, :], 1.0, 0.5,
                                                ALU.add, ALU.mult))
                ser.v(lambda e: e.tensor_mul(t3s[:, :], ths[:, :], pg0[:, :256]))
                ser.v(lambda e, fo=fo: e.tensor_mul(
                    hid[:, fo * 256:(fo + 1) * 256], t3s[:, :], pg1[:, :256]))
            for do in range(16):
                ser.dma(lambda e, do=do: e.dma_start(
                    wch64[:, :], dap(wlin, do * 128,
                                     [[DM, 128], [128 * DM, 64], [1, 128]])))
                for fi in range(64):
                    ser.t(lambda e, fi=fi: e.matmul(
                        pg0[:, :256], wch64[:, fi * 128:(fi + 1) * 128],
                        hid[:, fi * 256:(fi + 1) * 256],
                        start=(fi == 0), stop=(fi == 63)), acc=(fi > 0))
                ser.v(lambda e, do=do: e.tensor_copy(
                    fT[:, do * 256:(do + 1) * 256], pg0[:, :256]))
            for ii in range(2):
                ig = qf * 2 + ii
                ser.dma(lambda e, ig=ig: e.dma_start(
                    ot[:, :], dap(x1, ig * 128 * DM, [[DM, 128], [1, DM]])))
                for f in range(16):
                    ser.dma(lambda e, ig=ig, f=f: e.dma_start(
                        piece[:, :], dap(moT_h, f * 128 * SC + ig * 128,
                                         [[SC, 128], [1, 128]])))
                    ser.t(lambda e: e.transpose(ptr[:, :], piece[:, :], ident[:, :]))
                    ser.v(lambda e, f=f: e.tensor_add(
                        ot[:, f * 128:(f + 1) * 128], ot[:, f * 128:(f + 1) * 128],
                        ptr[:, :]))
                    ser.t(lambda e, f=f, ii=ii: e.transpose(
                        ptr[:, :], fT[:, f * 256 + ii * 128: f * 256 + (ii + 1) * 128],
                        ident[:, :]))
                    ser.v(lambda e, f=f: e.tensor_add(
                        ot[:, f * 128:(f + 1) * 128], ot[:, f * 128:(f + 1) * 128],
                        ptr[:, :]))
                ser.dma(lambda e, ig=ig: e.dma_start(
                    dap(o_out, ig * 128 * DM, [[DM, 128], [1, DM]]), ot[:, :]),
                    eng="gpsimd")

        ser.build(nc, tok)
    return nc



def kernel_5launch(**inp):
    g = {k: np.asarray(v) for k, v in inp.items()}
    x0 = f32(g["L0_attn_hidden_states"].reshape(S, DM))
    wqkv = g["L0_attn_Wqkv_w"]
    b_all = f32(g["L0_attn_Wqkv_b"])
    identb = bf(np.eye(128))
    cores = list(range(NC))

    ncA = build_A()
    im = [{
        "x0": x0[c * SC:(c + 1) * SC],
        "wqkT": bf(wqkv[:3072].T), "wvT": bf(wqkv[3072:].T),
        "bqk": f32(b_all[:3072]), "bv": f32(b_all[3072:]),
        "identb": identb,
    } for c in range(NC)]
    rA = run_bass_kernel_spmd(ncA, im, core_ids=cores).results
    xnT = np.concatenate([rA[c]["xnT"] for c in range(NC)], axis=1)
    qkT = np.concatenate([rA[c]["qkT"] for c in range(NC)], axis=1)
    v = np.concatenate([rA[c]["v"] for c in range(NC)], axis=0)

    masksB = np.zeros((128, 4 * 512), np.float32)
    for off in range(4):
        for kj in range(128):
            masksB[kj, off * 512 + off * 128 + kj:(off + 1) * 512] = 1.0
    ncB = build_B()
    im = [{
        "q1T": np.ascontiguousarray(qkT[h * 256:h * 256 + 128]),
        "q2T": np.ascontiguousarray(qkT[h * 256 + 128:h * 256 + 256]),
        "k1T": np.ascontiguousarray(qkT[2048 + (h // 2) * 256:2048 + (h // 2) * 256 + 128]),
        "k2T": np.ascontiguousarray(qkT[2048 + (h // 2) * 256 + 128:2048 + (h // 2) * 256 + 256]),
        "vg": np.ascontiguousarray(v[:, (h // 2) * 256:(h // 2 + 1) * 256]),
        "lq1": f32(g["L0_attn_lambda_q1"]), "lk1": f32(g["L0_attn_lambda_k1"]),
        "lq2": f32(g["L0_attn_lambda_q2"]), "lk2": f32(g["L0_attn_lambda_k2"]),
        "subln": f32(g["L0_attn_subln_w"]).reshape(2, 128),
        "masksB": bf(masksB),
    } for h in range(NC)]
    rB = run_bass_kernel_spmd(ncB, im, core_ids=cores).results
    oT = np.concatenate([rB[h]["oT"] for h in range(NC)], axis=0)

    ncCD = build_CD()
    im = [{
        "oTs": np.ascontiguousarray(oT[:, c * SC:(c + 1) * SC]),
        "woT": bf(g["L0_attn_out_w"].T),
        "outb": f32(g["L0_attn_out_b"]),
        "xn0Ts": np.ascontiguousarray(xnT[:, c * SC:(c + 1) * SC]),
        "gwT": bf(g["L0_ffn_gate_w"].T),
        "uwT": bf(g["L0_ffn_up_w"].T),
        "gs": f32(g["L0_ffn_gate_s"]), "us": f32(g["L0_ffn_up_s"]),
        "dwT": bf(g["L0_ffn_down_w"].T),
        "ds": f32(g["L0_ffn_down_s"]),
        "x0": x0[c * SC:(c + 1) * SC],
        "winT": bf(g["L1_attn_W_in"].T),
        "identb": identb,
    } for c in range(NC)]
    rCD = run_bass_kernel_spmd(ncCD, im, core_ids=cores).results
    x1 = np.concatenate([rCD[c]["x1"] for c in range(NC)], axis=0)
    xn1T = np.concatenate([rCD[c]["xn1T"] for c in range(NC)], axis=1)
    gate = np.concatenate([rCD[c]["gate"] for c in range(NC)], axis=0)
    hbct = np.concatenate([rCD[c]["hbct"] for c in range(NC)], axis=1)
    dtt = np.concatenate([rCD[c]["dtt"] for c in range(NC)], axis=1)

    cwr = f32(g["L1_attn_conv_w"]).reshape(18, 128, MK)
    cbr = f32(g["L1_attn_conv_b"]).reshape(18, 128)
    ut01 = np.ascontiguousarray(np.triu(np.ones((128, 128), np.float32)))
    idb2 = np.zeros((128, 64), np.float32)
    for i in range(64):
        idb2[64 + i, i] = 1.0
    ncE = build_E()
    im = [{
        "xs0": np.ascontiguousarray(hbct[c * 256:c * 256 + 128]),
        "xs1": np.ascontiguousarray(hbct[c * 256 + 128:c * 256 + 256]),
        "bt_in": np.ascontiguousarray(hbct[2048:2176]),
        "ct_in": np.ascontiguousarray(hbct[2176:2304]),
        "cw0": cwr[2 * c], "cw1": cwr[2 * c + 1], "cw2": cwr[16], "cw3": cwr[17],
        "cb0": cbr[2 * c], "cb1": cbr[2 * c + 1], "cb2": cbr[16], "cb3": cbr[17],
        "dtr": np.ascontiguousarray(dtt[4 * c:4 * c + 4]),
        "dtb": f32(g["L1_attn_dt_bias"][4 * c:4 * c + 4]),
        "alog": f32(g["L1_attn_A_log"][4 * c:4 * c + 4]),
        "dpar": f32(g["L1_attn_D_param"][4 * c:4 * c + 4]),
        "ut01": ut01, "eye4d": np.eye(4, dtype=np.float32),
        "identb": identb, "idb2d": bf(idb2),
    } for c in range(NC)]
    rE = run_bass_kernel_spmd(ncE, im, core_ids=cores).results
    y = np.concatenate([rE[c]["ynat"] for c in range(NC)], axis=1)

    ncF = build_F()
    im = [{
        "ynat": np.ascontiguousarray(y[c * SC:(c + 1) * SC]),
        "gate": np.ascontiguousarray(gate[c * SC:(c + 1) * SC]),
        "nrm": f32(g["L1_attn_norm_w"]),
        "woutT": bf(g["L1_attn_W_out"].T),
        "xn1Ts": np.ascontiguousarray(xn1T[:, c * SC:(c + 1) * SC]),
        "wg0": bf(g["L1_ffn_w_gating"][0]),
        "wg1": bf(g["L1_ffn_w_gating"][1]),
        "wlin": bf(g["L1_ffn_w_linear"]),
        "x1": x1[c * SC:(c + 1) * SC],
        "identb": identb,
    } for c in range(NC)]
    rF = run_bass_kernel_spmd(ncF, im, core_ids=cores).results
    out = np.concatenate([rF[c]["out"] for c in range(NC)], axis=0)
    return out.reshape(1, S, DM).astype(np.float32)



def build_FUSED():
    nc = bass.Bass("TRN2", target_bir_lowering=False)
    x0f = nc.dram_tensor("x0f", [S, DM], F32, kind="ExternalInput")
    x0s = nc.dram_tensor("x0s", [SC, DM], F32, kind="ExternalInput")
    offs = nc.dram_tensor("offs", [1, 4], mybir.dt.int32, kind="ExternalInput")
    wqk_sl = nc.dram_tensor("wqk_sl", [DM, 512], BF16, kind="ExternalInput")
    bqk_sl = nc.dram_tensor("bqk_sl", [512], F32, kind="ExternalInput")
    wv_sl = nc.dram_tensor("wv_sl", [DM, 256], BF16, kind="ExternalInput")
    bv_sl = nc.dram_tensor("bv_sl", [256], F32, kind="ExternalInput")
    lq1 = nc.dram_tensor("lq1", [128], F32, kind="ExternalInput")
    lk1 = nc.dram_tensor("lk1", [128], F32, kind="ExternalInput")
    lq2 = nc.dram_tensor("lq2", [128], F32, kind="ExternalInput")
    lk2 = nc.dram_tensor("lk2", [128], F32, kind="ExternalInput")
    subln = nc.dram_tensor("subln", [2, 128], F32, kind="ExternalInput")
    masksB = nc.dram_tensor("masksB", [128, 4 * 512], BF16, kind="ExternalInput")
    woT = nc.dram_tensor("woT", [DM, DM], BF16, kind="ExternalInput")
    outb = nc.dram_tensor("outb", [DM], F32, kind="ExternalInput")
    gwT = nc.dram_tensor("gwT", [DM, INTER0], BF16, kind="ExternalInput")
    uwT = nc.dram_tensor("uwT", [DM, INTER0], BF16, kind="ExternalInput")
    gs = nc.dram_tensor("gs", [INTER0], F32, kind="ExternalInput")
    us = nc.dram_tensor("us", [INTER0], F32, kind="ExternalInput")
    dwT = nc.dram_tensor("dwT", [INTER0, DM], BF16, kind="ExternalInput")
    ds = nc.dram_tensor("ds", [DM], F32, kind="ExternalInput")
    winT = nc.dram_tensor("winT", [DM, PROJ], BF16, kind="ExternalInput")
    win_sl = nc.dram_tensor("win_sl", [DM, 516], BF16, kind="ExternalInput")
    cw = [nc.dram_tensor(f"cw{j}", [128, MK], F32, kind="ExternalInput")
          for j in range(4)]
    cb = [nc.dram_tensor(f"cb{j}", [128], F32, kind="ExternalInput")
          for j in range(4)]
    dtb = nc.dram_tensor("dtb", [4], F32, kind="ExternalInput")
    alog = nc.dram_tensor("alog", [4], F32, kind="ExternalInput")
    dpar = nc.dram_tensor("dpar", [4], F32, kind="ExternalInput")
    ut01 = nc.dram_tensor("ut01", [128, 128], F32, kind="ExternalInput")
    eye4d = nc.dram_tensor("eye4d", [4, 4], F32, kind="ExternalInput")
    identb = nc.dram_tensor("identb", [128, 128], BF16, kind="ExternalInput")
    idb2d = nc.dram_tensor("idb2d", [128, 64], BF16, kind="ExternalInput")
    nrm = nc.dram_tensor("nrm", [DM], F32, kind="ExternalInput")
    woutT = nc.dram_tensor("woutT", [DM, DM], BF16, kind="ExternalInput")
    wg0 = nc.dram_tensor("wg0", [DM, FF1], BF16, kind="ExternalInput")
    wg1 = nc.dram_tensor("wg1", [DM, FF1], BF16, kind="ExternalInput")
    wlin = nc.dram_tensor("wlin", [FF1, DM], BF16, kind="ExternalInput")
    xnT_d = nc.dram_tensor("xnT_d", [DM, S], BF16)
    cin_oT = nc.dram_tensor("cin_oT", [256, S], BF16)
    oT_all = nc.dram_tensor("oT_all", [DM, S], BF16)
    attnT_h = nc.dram_tensor("attnT_h", [DM, SC], BF16)
    x1_d = nc.dram_tensor("x1_d", [SC, DM], F32)
    gate_d = nc.dram_tensor("gate_d", [SC, M_INTER], BF16)
    cin_xn1T = nc.dram_tensor("cin_xn1T", [DM, SC], BF16)
    xn1T_all = nc.dram_tensor("xn1T_all", [NC * DM, SC], BF16)
    xsB_d = [nc.dram_tensor(f"xsB_d{j}", [128, S], BF16) for j in range(4)]
    cin_y = nc.dram_tensor("cin_y", [S, 256], BF16)
    y_all = nc.dram_tensor("y_all", [NC * S, 256], BF16)
    y_own = nc.dram_tensor("y_own", [SC, M_INTER], BF16)
    moT_h = nc.dram_tensor("moT_h", [DM, SC], BF16)
    o_out = nc.dram_tensor("out", [SC, DM], F32, kind="ExternalOutput")
    import os as _os
    dbg = _os.environ.get("FUSED_DEBUG") == "1"
    if dbg:
        d_xnT = nc.dram_tensor("d_xnT", [DM, S], BF16, kind="ExternalOutput")
        d_biga = nc.dram_tensor("d_biga", [DM, SC], BF16, kind="ExternalOutput")
        d_oT = nc.dram_tensor("d_oT", [256, S], BF16, kind="ExternalOutput")
        d_oTall = nc.dram_tensor("d_oTall", [DM, 512], BF16, kind="ExternalOutput")
        d_x1 = nc.dram_tensor("d_x1", [SC, DM], F32, kind="ExternalOutput")
        d_xn1 = nc.dram_tensor("d_xn1", [DM, SC], BF16, kind="ExternalOutput")
        d_y = nc.dram_tensor("d_y", [S, 256], BF16, kind="ExternalOutput")
        d_yown = nc.dram_tensor("d_yown", [SC, M_INTER], BF16, kind="ExternalOutput")
        d_gate = nc.dram_tensor("d_gate", [SC, M_INTER], BF16, kind="ExternalOutput")
        d_xsB = nc.dram_tensor("d_xsB", [128, S], BF16, kind="ExternalOutput")
        d_attnT = nc.dram_tensor("d_attnT", [DM, SC], BF16, kind="ExternalOutput")
        d_h = nc.dram_tensor("d_h", [INTER0, SC], BF16, kind="ExternalOutput")
        d_f0 = nc.dram_tensor("d_f0", [DM, SC], BF16, kind="ExternalOutput")

    ser = Ser()
    NT = SC // 128
    GC = float(np.sqrt(2.0 / np.pi))
    with ExitStack() as st:
        tok = st.enter_context(nc.semaphore("tok"))
        grb = st.enter_context(nc.gpsimd.register("grb"))
        gry = st.enter_context(nc.gpsimd.register("gry"))
        grt = st.enter_context(nc.gpsimd.register("grt"))

        def sbT(name, shape, dtype):
            return nc.sbuf_tensor(name, shape, dtype)

        ident = st.enter_context(sbT("identP", [128, 128], BF16))
        ones_rowf = st.enter_context(sbT("ones_rowfP", [1, 128], F32))
        offs_sb = st.enter_context(sbT("offs_sb", [1, 4], mybir.dt.int32))
        ser.dma(lambda e: e.dma_start(ident[:, :], identb.ap()))
        ser.dma(lambda e: e.dma_start(offs_sb[:, :], offs.ap()))
        ser.v(lambda e: e.memset(ones_rowf[:, :], 1.0))
        ser.op("gpsimd", lambda e: e.reg_load(grb, offs_sb[:1, :1]), 0)
        ser.op("gpsimd", lambda e: e.reg_load(gry, offs_sb[:1, 1:2]), 0)

        with sbT("xtP", [128, DM], F32) as xt, \
             sbT("scrP", [128, DM], F32) as scr, \
             sbT("xnbfP", [128, DM], BF16) as xnbf, \
             sbT("slabP", [128, 16 * 128], BF16) as slab, \
             sbT("mcolP", [128, 1], F32) as mcol, \
             sbT("vcolP", [128, 1], F32) as vcol, \
             nc.psum_tensor("ptrP", [128, 128], BF16) as ptr:
            for i in range(S // 128):
                ser.dma(lambda e, i=i: e.dma_start(
                    xt[:, :], dap(x0f, i * 128 * DM, [[DM, 128], [1, DM]])))
                emit_layernorm(ser, xt, xnbf, scr, mcol, vcol)
                for f in range(16):
                    ser.t(lambda e, f=f: e.transpose(
                        ptr[:, :], xnbf[:, f * 128:(f + 1) * 128], ident[:, :]))
                    ser.v(lambda e, f=f: e.tensor_copy(
                        slab[:, f * 128:(f + 1) * 128], ptr[:, :]))
                ser.dma(lambda e, i=i: e.dma_start(
                    dap(xnT_d, i * 128, [[S, 128], [128 * S, 16], [1, 128]]),
                    slab[:, :]), eng="gpsimd")
            ser.flush(nc, tok)

        st23 = ExitStack()
        qT = [st23.enter_context(sbT(f"qTf{m}", [128, S], BF16)) for m in range(2)]
        kT = [st23.enter_context(sbT(f"kTf{m}", [128, S], BF16)) for m in range(2)]
        v_sb = st23.enter_context(sbT("v_sbF", [128, 64 * 256], BF16))
        qkT4 = qT + kT
        with sbT("xnblk", [128, 16 * 2048], BF16) as xnblk, \
             sbT("wchP2", [128, 2048], BF16) as wch, \
             sbT("wvP2", [128, 16 * 256], BF16) as wv_sb2, \
             sbT("bcolP2", [128, 1], F32) as bcol, \
             sbT("bvrow2", [1, 256], F32) as bvrow, \
             sbT("bvb2", [128, 256], F32) as bvb, \
             sbT("qko2", [128, 512], BF16) as qko, \
             nc.psum_tensor("pmmP2", [128, 512], F32) as pmm, \
             nc.psum_tensor("pbcP2", [128, 256], F32) as pbc:
            ser.dma(lambda e: e.dma_start(
                wv_sb2[:, :], dap(wv_sl, 0, [[256, 128], [128 * 256, 16], [1, 256]])))
            ser.dma(lambda e: e.dma_start(
                bvrow[:, :], dap(bv_sl, 0, [[256, 1], [1, 256]])))
            ser.t(lambda e: e.matmul(pbc[:, :], ones_rowf[:, :], bvrow[:, :],
                                     start=True, stop=True))
            ser.v(lambda e: e.tensor_copy(bvb[:, :], pbc[:, :]))
            for blk in range(4):
                b0 = blk * 2048
                for fi in range(16):
                    ser.dma(lambda e, fi=fi, b0=b0: e.dma_start(
                        xnblk[:, fi * 2048:(fi + 1) * 2048],
                        dap(xnT_d, fi * 128 * S + b0, [[S, 128], [1, 2048]])))
                for fo in range(4):
                    ser.dma(lambda e, fo=fo: e.dma_start(
                        wch[:, :], dap(wqk_sl, fo * 128,
                                       [[512, 128], [128 * 512, 16], [1, 128]])))
                    ser.dma(lambda e, fo=fo: e.dma_start(
                        bcol[:, :], dap(bqk_sl, fo * 128, [[1, 128], [1, 1]])))
                    for sh in range(4):
                        for fi in range(16):
                            ser.t(lambda e, fi=fi, sh=sh: e.matmul(
                                pmm[:, :], wch[:, fi * 128:(fi + 1) * 128],
                                xnblk[:, fi * 2048 + sh * 512: fi * 2048 + (sh + 1) * 512],
                                start=(fi == 0), stop=(fi == 15)), acc=(fi > 0))
                        if fo < 2:
                            ser.v(lambda e: e.tensor_scalar(
                                qko[:, :], pmm[:, :], bcol[:, :], SCALE,
                                ALU.add, ALU.mult))
                        else:
                            ser.v(lambda e: e.tensor_scalar_add(
                                qko[:, :], pmm[:, :], bcol[:, :]))
                        ser.v(lambda e, fo=fo, b0=b0, sh=sh: e.tensor_copy(
                            qkT4[fo][:, b0 + sh * 512: b0 + (sh + 1) * 512],
                            qko[:, :]))
                for i in range(16):
                    for fi in range(16):
                        ser.t(lambda e, i=i, fi=fi: e.matmul(
                            pbc[:, :], xnblk[:, fi * 2048 + i * 128: fi * 2048 + (i + 1) * 128],
                            wv_sb2[:, fi * 256:(fi + 1) * 256],
                            start=(fi == 0), stop=(fi == 15)), acc=(fi > 0))
                    ser.v(lambda e, blk=blk, i=i: e.tensor_add(
                        v_sb[:, (blk * 16 + i) * 256:(blk * 16 + i + 1) * 256],
                        pbc[:, :], bvb[:, :]))
            ser.flush(nc, tok)

        with ExitStack() as s3:
            def e3(name, shape, dtype):
                return s3.enter_context(sbT(name, shape, dtype))

            masks = e3("masksF", [128, 4 * 512], BF16)
            ones_colb = e3("ones_colbF", [128, 1], BF16)
            ones_colf = e3("ones_colfF", [128, 1], F32)
            lrow0 = e3("lrow0", [1, 128], F32)
            lrow1 = e3("lrow1", [1, 128], F32)
            ee0 = e3("ee0", [1, 1], F32)
            ee1 = e3("ee1", [1, 1], F32)
            lamneg = e3("lamnegF", [1, 1], F32)
            sw0 = e3("sw0", [128, 1], F32)
            sw1 = e3("sw1", [128, 1], F32)
            pT = e3("pTF", [128, 512], BF16)
            pTm = e3("pTmF", [128, 512], BF16)
            linv0 = e3("linv0", [1, 512], F32)
            linv1 = e3("linv1", [1, 512], F32)
            bsc0 = e3("bsc0", [128, 512], F32)
            bsc1 = e3("bsc1", [128, 512], F32)
            oc0 = e3("oc0", [128, 512], F32)
            oc1 = e3("oc1", [128, 512], F32)
            tmp = e3("tmpBF", [128, 512], F32)
            sq0 = e3("sq0", [128, 512], F32)
            sq1 = e3("sq1", [128, 512], F32)
            rrow = e3("rrowF", [1, 512], F32)
            rb = e3("rbF", [128, 512], F32)
            ost = e3("ostF", [128, 512], BF16)
            pS = s3.enter_context(nc.psum_tensor("pSF", [128, 512], F32))
            pO00 = s3.enter_context(nc.psum_tensor("pO00", [128, 512], F32))
            pO01 = s3.enter_context(nc.psum_tensor("pO01", [128, 512], F32))
            pO10 = s3.enter_context(nc.psum_tensor("pO10", [128, 512], F32))
            pO11 = s3.enter_context(nc.psum_tensor("pO11", [128, 512], F32))
            pL0 = s3.enter_context(nc.psum_tensor("pL0", [1, 512], F32))
            pL1 = s3.enter_context(nc.psum_tensor("pL1", [1, 512], F32))
            pB = s3.enter_context(nc.psum_tensor("pBF", [128, 512], F32))
            pO = [[pO00, pO01], [pO10, pO11]]
            pL = [pL0, pL1]
            linv = [linv0, linv1]
            bsc = [bsc0, bsc1]
            oc = [oc0, oc1]
            sq = [sq0, sq1]
            sw = [sw0, sw1]
            lrow = [lrow0, lrow1]
            ee = [ee0, ee1]
            ser.dma(lambda e: e.dma_start(masks[:, :], masksB.ap()))
            ser.v(lambda e: e.memset(ones_colb[:, :], 1.0))
            ser.v(lambda e: e.memset(ones_colf[:, :], 1.0))
            lsrc = [(lq1, lk1), (lq2, lk2)]
            for m in range(2):
                a, b = lsrc[m]
                ser.dma(lambda e, a=a, m=m: e.dma_start(
                    lrow[m][:, :], dap(a, 0, [[128, 1], [1, 128]])))
                ser.dma(lambda e, b=b: e.dma_start(
                    tmp[:1, :128], dap(b, 0, [[128, 1], [1, 128]])))
                ser.v(lambda e, m=m: e.tensor_mul(lrow[m][:, :], lrow[m][:, :],
                                                  tmp[:1, :128]))
                ser.v(lambda e, m=m: e.reduce_sum(ee[m][:, :], lrow[m][:, :], AX.X))
                ser.s(lambda e, m=m: e.activation(ee[m][:, :], ee[m][:, :], AF.Exp))
            ser.v(lambda e: e.tensor_sub(lamneg[:, :], ee[1][:, :], ee[0][:, :]))
            ser.v(lambda e: e.tensor_scalar_add(lamneg[:, :], lamneg[:, :],
                                                float(-LAMBDA_INIT)))
            for h in range(2):
                ser.dma(lambda e, h=h: e.dma_start(
                    sw[h][:, :], dap(subln, h * 128, [[1, 128], [1, 1]])))
                ser.v(lambda e, h=h: e.tensor_scalar_mul(
                    sw[h][:, :], sw[h][:, :], float(1.0 - LAMBDA_INIT)))
            for qb in range(16):
                q0 = qb * 512
                nk = (q0 + 512) // 128
                for m in range(2):
                    for kt in range(nk):
                        last = (kt == nk - 1)
                        ser.t(lambda e, m=m, kt=kt, q0=q0: e.matmul(
                            pS[:, :], kT[m][:, kt * 128:(kt + 1) * 128],
                            qT[m][:, q0:q0 + 512], start=True, stop=True),
                            acc=(kt > 0))
                        if kt * 128 >= q0:
                            off = (kt * 128 - q0) // 128
                            ser.s(lambda e: e.activation(pTm[:, :], pS[:, :], AF.Exp))
                            ser.v(lambda e, off=off: e.tensor_mul(
                                pT[:, :], pTm[:, :],
                                masks[:, off * 512:(off + 1) * 512]))
                        else:
                            ser.s(lambda e: e.activation(pT[:, :], pS[:, :], AF.Exp))
                        ser.t(lambda e, m=m, kt=kt, last=last: e.matmul(
                            pL[m][:, :], ones_colb[:, :], pT[:, :],
                            start=(kt == 0), stop=last))
                        for h in range(2):
                            ser.t(lambda e, m=m, h=h, kt=kt, last=last: e.matmul(
                                pO[m][h][:, :],
                                v_sb[:, kt * 256 + h * 128: kt * 256 + (h + 1) * 128],
                                pT[:, :], start=(kt == 0), stop=last),
                                acc=True)
                for m in range(2):
                    ser.v(lambda e, m=m: e.reciprocal(linv[m][:, :], pL[m][:, :]))
                ser.v(lambda e: e.tensor_scalar_mul(linv[1][:, :], linv[1][:, :],
                                                    lamneg[:1, :1]))
                for m in range(2):
                    ser.t(lambda e, m=m: e.matmul(pB[:, :], ones_rowf[:, :],
                                                  linv[m][:, :], start=True, stop=True))
                    ser.v(lambda e, m=m: e.tensor_copy(bsc[m][:, :], pB[:, :]))
                for h in range(2):
                    ser.v(lambda e, h=h: e.tensor_mul(oc[h][:, :], pO[0][h][:, :],
                                                      bsc[0][:, :]))
                    ser.v(lambda e, h=h: e.tensor_mul(tmp[:, :], pO[1][h][:, :],
                                                      bsc[1][:, :]))
                    ser.v(lambda e, h=h: e.tensor_add(oc[h][:, :], oc[h][:, :],
                                                      tmp[:, :]))
                    ser.s(lambda e, h=h: e.activation(sq[h][:, :], oc[h][:, :],
                                                      AF.Square))
                    ser.t(lambda e, h=h: e.matmul(pL[0][:, :], ones_colf[:, :],
                                                  sq[h][:, :], start=(h == 0),
                                                  stop=(h == 1)))
                ser.v(lambda e: e.tensor_scalar(rrow[:, :], pL[0][:, :], 1.0 / 256,
                                                EPS, ALU.mult, ALU.add))
                ser.v(lambda e: e.reciprocal(rrow[:, :], rrow[:, :]))
                ser.s(lambda e: e.activation(rrow[:, :], rrow[:, :], AF.Sqrt))
                ser.t(lambda e: e.matmul(pB[:, :], ones_rowf[:, :], rrow[:, :],
                                         start=True, stop=True))
                ser.v(lambda e: e.tensor_copy(rb[:, :], pB[:, :]))
                for h in range(2):
                    ser.v(lambda e, h=h: e.scalar_tensor_tensor(
                        ost[:, :], oc[h][:, :], sw[h][:, :], rb[:, :],
                        ALU.mult, ALU.mult))
                    ser.dma(lambda e, h=h, q0=q0: e.dma_start(
                        dap(cin_oT, h * 128 * S + q0, [[S, 128], [1, 512]]),
                        ost[:, :]), eng="gpsimd")
            ser.flush(nc, tok)
        st23.close()

        ser.op("gpsimd", lambda e: e.collective_compute(
            "AllGather", ALU.bypass, ins=[cin_oT.ap().opt()],
            outs=[oT_all.ap().opt()], replica_groups=[list(range(NC))]), 1)

        emit_fused_tail(nc, ser, dict(
            oT_all=oT_all, attnT_h=attnT_h, x1_d=x1_d, gate_d=gate_d,
            cin_xn1T=cin_xn1T, xn1T_all=xn1T_all, xsB_d=xsB_d, cin_y=cin_y,
            identb=identb, cin_oT_h=cin_oT, tok=tok,
            **({"d_xnT": d_xnT, "d_biga": d_biga, "d_oT": d_oT, "d_oTall": d_oTall,
                "d_x1": d_x1, "d_xn1": d_xn1, "d_y": d_y, "d_yown": d_yown,
                "d_gate": d_gate, "d_xsB": d_xsB, "d_attnT": d_attnT,
                "d_f0": d_f0, "d_h": d_h} if dbg else {}),
            y_all=y_all, y_own=y_own, moT_h=moT_h, xnT_d=xnT_d, o_out=o_out,
            woT=woT, outb=outb, gwT=gwT, uwT=uwT, gs=gs, us=us, dwT=dwT,
            ds=ds, winT=winT, win_sl=win_sl, cw=cw, cb=cb, dtb=dtb,
            alog=alog, dpar=dpar, ut01=ut01, eye4d=eye4d, idb2d=idb2d,
            nrm=nrm, woutT=woutT, wg0=wg0, wg1=wg1, wlin=wlin, x0s=x0s,
            ident=ident, ones_rowf=ones_rowf, grb=grb, gry=gry, grt=grt))
        ser.build(nc, tok)
    return nc


def emit_fused_tail(nc, ser, H):
    g = type("G", (), H)
    NT = SC // 128
    GC = float(np.sqrt(2.0 / np.pi))
    ident = H["ident"]
    ones_rowf = H["ones_rowf"]
    grb, gry, grt = H["grb"], H["gry"], H["grt"]

    def sbT(name, shape, dtype):
        return nc.sbuf_tensor(name, shape, dtype)

    with ExitStack() as s4:
        big_a = s4.enter_context(sbT("big_a4", [128, 16 * SC], BF16))
        big_b = s4.enter_context(sbT("big_b4", [128, 16 * SC], BF16))
        hbig = s4.enter_context(sbT("hbig4", [128, 44 * SC], BF16))
        wch = s4.enter_context(sbT("wch4", [128, 2048], BF16))
        wch2 = s4.enter_context(sbT("wch24", [128, 2048], BF16))
        wch44 = s4.enter_context(sbT("wch444", [128, 44 * 128], BF16))
        scol = s4.enter_context(sbT("scol4", [128, 1], F32))
        scol2 = s4.enter_context(sbT("scol24", [128, 1], F32))
        mcol = s4.enter_context(sbT("mcol4", [128, 1], F32))
        vcol = s4.enter_context(sbT("vcol4", [128, 1], F32))
        tmpf = s4.enter_context(sbT("tmpf4", [128, 512], F32))
        tmpf2 = s4.enter_context(sbT("tmpf24", [128, 512], F32))
        tmpf3 = s4.enter_context(sbT("tmpf34", [128, 512], F32))
        st_ch = s4.enter_context(sbT("st_ch4", [128, 1024], BF16))
        piece = s4.enter_context(sbT("piece4", [128, 128], BF16))
        x1t = s4.enter_context(sbT("x1t4", [128, DM], F32))
        scr = s4.enter_context(sbT("scr4_", [128, DM], F32))
        xnbf = s4.enter_context(sbT("xnbf4", [128, DM], BF16))
        pg = s4.enter_context(nc.psum_tensor("pg4", [128, 512], F32))
        pu = s4.enter_context(nc.psum_tensor("pu4", [128, 512], F32))
        ptr = s4.enter_context(nc.psum_tensor("ptr4", [128, 128], BF16))

        ser.dma(lambda e: e.dma_start(
            big_a[:, :], bass.AP(H["oT_all"], grb,
                                 [[S, 128], [128 * S, 16], [1, SC]])),
            eng="gpsimd")
        for fo in range(16):
            ser.dma(lambda e, fo=fo: e.dma_start(
                wch[:, :], dap(H["woT"], fo * 128,
                               [[DM, 128], [128 * DM, 16], [1, 128]])))
            ser.dma(lambda e, fo=fo: e.dma_start(
                scol[:, :], dap(H["outb"], fo * 128, [[1, 128], [1, 1]])))
            for sh in range(2):
                for fi in range(16):
                    ser.t(lambda e, fi=fi, sh=sh: e.matmul(
                        pg[:, :], wch[:, fi * 128:(fi + 1) * 128],
                        big_a[:, fi * SC + sh * 512: fi * SC + (sh + 1) * 512],
                        start=(fi == 0), stop=(fi == 15)), acc=(fi > 0))
                ser.v(lambda e, sh=sh: e.tensor_scalar_add(
                    st_ch[:, sh * 512:(sh + 1) * 512], pg[:, :], scol[:, :]))
            ser.dma(lambda e, fo=fo: e.dma_start(
                dap(H["attnT_h"], fo * 128 * SC, [[SC, 128], [1, SC]]),
                st_ch[:, :]), eng="gpsimd")
        ser.dma(lambda e: e.dma_start(
            big_a[:, :], bass.AP(H["xnT_d"], grb,
                                 [[S, 128], [128 * S, 16], [1, SC]])),
            eng="gpsimd")
        if "d_biga" in H:
            for f_ in range(16):
                ser.dma(lambda e, f_=f_: e.dma_start(
                    dap(H["d_biga"], f_ * 128 * SC, [[SC, 128], [1, SC]]),
                    big_a[:, f_ * SC:(f_ + 1) * SC]), eng="gpsimd")
        for fo in range(44):
            ser.dma(lambda e, fo=fo: e.dma_start(
                wch[:, :], dap(H["gwT"], fo * 128,
                               [[INTER0, 128], [128 * INTER0, 16], [1, 128]])))
            ser.dma(lambda e, fo=fo: e.dma_start(
                wch2[:, :], dap(H["uwT"], fo * 128,
                                [[INTER0, 128], [128 * INTER0, 16], [1, 128]])))
            ser.dma(lambda e, fo=fo: e.dma_start(
                scol[:, :], dap(H["gs"], fo * 128, [[1, 128], [1, 1]])))
            ser.dma(lambda e, fo=fo: e.dma_start(
                scol2[:, :], dap(H["us"], fo * 128, [[1, 128], [1, 1]])))
            for sh in range(2):
                for fi in range(16):
                    ser.t(lambda e, fi=fi, sh=sh: e.matmul(
                        pg[:, :], wch[:, fi * 128:(fi + 1) * 128],
                        big_a[:, fi * SC + sh * 512: fi * SC + (sh + 1) * 512],
                        start=(fi == 0), stop=(fi == 15)), acc=(fi > 0))
                for fi in range(16):
                    ser.t(lambda e, fi=fi, sh=sh: e.matmul(
                        pu[:, :], wch2[:, fi * 128:(fi + 1) * 128],
                        big_a[:, fi * SC + sh * 512: fi * SC + (sh + 1) * 512],
                        start=(fi == 0), stop=(fi == 15)), acc=(fi > 0))
                ser.v(lambda e: e.tensor_scalar_mul(tmpf[:, :], pg[:, :], scol[:, :]))
                ser.s(lambda e: e.activation(tmpf2[:, :], tmpf[:, :], AF.Silu))
                ser.v(lambda e: e.tensor_scalar_mul(tmpf3[:, :], pu[:, :], scol2[:, :]))
                ser.v(lambda e, fo=fo, sh=sh: e.tensor_mul(
                    hbig[:, fo * SC + sh * 512: fo * SC + (sh + 1) * 512],
                    tmpf2[:, :], tmpf3[:, :]))
        if "d_h" in H:
            for f_ in range(44):
                ser.dma(lambda e, f_=f_: e.dma_start(
                    dap(H["d_h"], f_ * 128 * SC, [[SC, 128], [1, SC]]),
                    hbig[:, f_ * SC:(f_ + 1) * SC]), eng="gpsimd")
        for do in range(16):
            ser.dma(lambda e, do=do: e.dma_start(
                wch44[:, :], dap(H["dwT"], do * 128,
                                 [[DM, 128], [128 * DM, 44], [1, 128]])))
            ser.dma(lambda e, do=do: e.dma_start(
                scol[:, :], dap(H["ds"], do * 128, [[1, 128], [1, 1]])))
            for sh in range(2):
                for fi in range(44):
                    ser.t(lambda e, fi=fi, sh=sh: e.matmul(
                        pg[:, :], wch44[:, fi * 128:(fi + 1) * 128],
                        hbig[:, fi * SC + sh * 512: fi * SC + (sh + 1) * 512],
                        start=(fi == 0), stop=(fi == 43)), acc=(fi > 0))
                ser.v(lambda e, do=do, sh=sh: e.tensor_scalar_mul(
                    big_b[:, do * SC + sh * 512: do * SC + (sh + 1) * 512],
                    pg[:, :], scol[:, :]))
        if "d_f0" in H:
            for f_ in range(16):
                ser.dma(lambda e, f_=f_: e.dma_start(
                    dap(H["d_f0"], f_ * 128 * SC, [[SC, 128], [1, SC]]),
                    big_b[:, f_ * SC:(f_ + 1) * SC]), eng="gpsimd")
        for i in range(NT):
            ser.dma(lambda e, i=i: e.dma_start(
                x1t[:, :], dap(H["x0s"], i * 128 * DM, [[DM, 128], [1, DM]])))
            for f in range(16):
                ser.dma(lambda e, i=i, f=f: e.dma_start(
                    piece[:, :], dap(H["attnT_h"], f * 128 * SC + i * 128,
                                     [[SC, 128], [1, 128]])))
                ser.t(lambda e: e.transpose(ptr[:, :], piece[:, :], ident[:, :]))
                ser.v(lambda e, f=f: e.tensor_add(
                    x1t[:, f * 128:(f + 1) * 128], x1t[:, f * 128:(f + 1) * 128],
                    ptr[:, :]))
                ser.t(lambda e, i=i, f=f: e.transpose(
                    ptr[:, :], big_b[:, f * SC + i * 128: f * SC + (i + 1) * 128],
                    ident[:, :]))
                ser.v(lambda e, f=f: e.tensor_add(
                    x1t[:, f * 128:(f + 1) * 128], x1t[:, f * 128:(f + 1) * 128],
                    ptr[:, :]))
            ser.dma(lambda e, i=i: e.dma_start(
                dap(H["x1_d"], i * 128 * DM, [[DM, 128], [1, DM]]), x1t[:, :]),
                eng="gpsimd")
            emit_layernorm(ser, x1t, xnbf, scr, mcol, vcol)
            for f in range(16):
                ser.t(lambda e, f=f: e.transpose(
                    ptr[:, :], xnbf[:, f * 128:(f + 1) * 128], ident[:, :]))
                ser.v(lambda e: e.tensor_copy(piece[:, :], ptr[:, :]))
                ser.dma(lambda e, i=i, f=f: e.dma_start(
                    dap(H["cin_xn1T"], f * 128 * SC + i * 128,
                        [[SC, 128], [1, 128]]), piece[:, :]), eng="gpsimd")
        ser.dma(lambda e: e.dma_start(
            big_a[:, :], dap(H["cin_xn1T"], 0,
                             [[SC, 128], [128 * SC, 16], [1, SC]])))
        for fo in range(16):
            ser.dma(lambda e, fo=fo: e.dma_start(
                wch[:, :], dap(H["winT"], fo * 128,
                               [[PROJ, 128], [128 * PROJ, 16], [1, 128]])))
            for sh in range(2):
                for fi in range(16):
                    ser.t(lambda e, fi=fi, sh=sh: e.matmul(
                        pg[:, :], wch[:, fi * 128:(fi + 1) * 128],
                        big_a[:, fi * SC + sh * 512: fi * SC + (sh + 1) * 512],
                        start=(fi == 0), stop=(fi == 15)), acc=(fi > 0))
                ser.v(lambda e, sh=sh: e.tensor_copy(
                    st_ch[:, sh * 512:(sh + 1) * 512], pg[:, :]))
            for i in range(NT):
                ser.t(lambda e, i=i: e.transpose(
                    ptr[:, :], st_ch[:, i * 128:(i + 1) * 128], ident[:, :]))
                ser.v(lambda e: e.tensor_copy(piece[:, :], ptr[:, :]))
                ser.dma(lambda e, i=i, fo=fo: e.dma_start(
                    dap(H["gate_d"], i * 128 * M_INTER + fo * 128,
                        [[M_INTER, 128], [1, 128]]), piece[:, :]),
                    eng="gpsimd")
        ser.flush(nc, H["tok"])

    ser.op("gpsimd", lambda e: e.collective_compute(
        "AllGather", ALU.bypass, ins=[H["cin_xn1T"].ap().opt()],
        outs=[H["xn1T_all"].ap().opt()], replica_groups=[list(range(NC))]), 1)

    with ExitStack() as s5o:
        dt4 = s5o.enter_context(sbT("dt4F", [4, S], F32))
        with ExitStack() as s5:
            xnblk = s5.enter_context(sbT("xnblk5", [128, 16 * SC], BF16))
            wch = s5.enter_context(sbT("wch5", [128, 2048], BF16))
            stg = s5.enter_context(sbT("stg5", [128, 1024], BF16))
            pg = s5.enter_context(nc.psum_tensor("pg5", [128, 512], F32))
            for r in range(NC):
                for fi in range(16):
                    ser.dma(lambda e, r=r, fi=fi: e.dma_start(
                        xnblk[:, fi * SC:(fi + 1) * SC],
                        dap(H["xn1T_all"], (r * DM + fi * 128) * SC,
                            [[SC, 128], [1, SC]])))
                for fo in range(5):
                    w = 128 if fo < 4 else 4
                    ser.dma(lambda e, fo=fo, w=w: e.dma_start(
                        wch[:, :16 * w], dap(H["win_sl"], fo * 128,
                                             [[516, 128], [128 * 516, 16], [1, w]])))
                    for sh in range(2):
                        for fi in range(16):
                            ser.t(lambda e, fi=fi, sh=sh, w=w: e.matmul(
                                pg[:w, :], wch[:, fi * w:(fi + 1) * w],
                                xnblk[:, fi * SC + sh * 512: fi * SC + (sh + 1) * 512],
                                start=(fi == 0), stop=(fi == 15)), acc=(fi > 0))
                        if fo < 4:
                            ser.v(lambda e, sh=sh: e.tensor_copy(
                                stg[:, sh * 512:(sh + 1) * 512], pg[:, :]))
                        else:
                            ser.v(lambda e, r=r, sh=sh: e.tensor_copy(
                                dt4[:, r * SC + sh * 512: r * SC + (sh + 1) * 512],
                                pg[:4, :]))
                    if fo < 4:
                        ser.dma(lambda e, fo=fo, r=r: e.dma_start(
                            dap(H["xsB_d"][fo], r * SC, [[S, 128], [1, SC]]),
                            stg[:, :]), eng="gpsimd")
            ser.flush(nc, H["tok"])

        emit_scan(nc, ser, H, dt4, ident, ones_rowf)

    ser.op("gpsimd", lambda e: e.collective_compute(
        "AllGather", ALU.bypass, ins=[H["cin_y"].ap().opt()],
        outs=[H["y_all"].ap().opt()], replica_groups=[list(range(NC))]), 1)
    for eb in range(NC):
        ser.op("gpsimd", lambda e, eb=eb: e.reg_mov(grt, eb * S * 256), 0)
        ser.op("gpsimd", lambda e: e.reg_add(grt, grt, gry), 0)
        ser.dma(lambda e, eb=eb: e.dma_start(
            dap(H["y_own"], eb * 256, [[M_INTER, 128], [128 * M_INTER, 8], [1, 256]]),
            bass.AP(H["y_all"], grt, [[256, 128], [128 * 256, 8], [1, 256]])),
            eng="gpsimd")

    emit_final(nc, ser, H, ident, ones_rowf)
    if "d_xnT" in H:
        def cp(dst, srct, rows, cols, srccols):
            ser.dma(lambda e, dst=dst, srct=srct, rows=rows, cols=cols, srccols=srccols:
                    e.dma_start(
                        dap(dst, 0, [[cols, 128], [128 * cols, rows // 128], [1, cols]]),
                        dap(srct, 0, [[srccols, 128], [128 * srccols, rows // 128], [1, cols]])),
                    eng="gpsimd")
        cp(H["d_xnT"], H["xnT_d"], DM, S, S)
        cp(H["d_oT"], H["cin_oT_h"], 256, S, S)
        cp(H["d_oTall"], H["oT_all"], DM, 512, S)
        cp(H["d_x1"], H["x1_d"], SC, DM, DM)
        cp(H["d_xn1"], H["cin_xn1T"], DM, SC, SC)
        cp(H["d_y"], H["cin_y"], S, 256, 256)
        cp(H["d_yown"], H["y_own"], SC, M_INTER, M_INTER)
        cp(H["d_gate"], H["gate_d"], SC, M_INTER, M_INTER)
        cp(H["d_xsB"], H["xsB_d"][0], 128, S, S)
        cp(H["d_attnT"], H["attnT_h"], DM, SC, SC)


def emit_scan(nc, ser, HD, dt4, identP, ones_rowfP):
    tok = HD["tok"]
    with ExitStack() as st:
        def sb(name, shape, dtype):
            return st.enter_context(nc.sbuf_tensor(name, shape, dtype))

        def ps(name, shape, dtype):
            return st.enter_context(nc.psum_tensor(name, shape, dtype))

        xsc = [sb(f"xsc{j}", [128, S], BF16) for j in range(2)]
        btc = sb("btc", [128, S], BF16)
        ctc = sb("ctc", [128, S], BF16)
        u_t = sb("u_t", [128, S + 3], BF16)
        acc = sb("acc", [128, 2048], F32)
        scr4 = sb("scr4", [4, S], F32)
        ag4 = sb("ag4", [4, S], F32)
        wcol = sb("wcol", [128, MK], F32)
        bcolE = sb("bcolE", [128, 1], F32)
        identB = sb("identB", [128, 128], BF16)
        idb2 = sb("idb2", [128, 64], BF16)
        ut = sb("ut", [128, 128], F32)
        eye4 = sb("eye4", [4, 4], F32)
        ones_rowf = sb("ones_rowfE", [1, 128], F32)
        prow = sb("prow", [1, 4], F32)
        arow = sb("arow", [1, 4], F32)
        drow = sb("drow", [1, 4], F32)
        acol = sb("acol", [4, 1], F32)
        bcol4 = sb("bcol4", [4, 1], F32)
        dcol = [sb(f"dcol{h}", [128, 1], F32) for h in range(4)]
        H = [[sb(f"H{h}_{pp}", [128, MP], F32) for pp in range(2)] for h in range(4)]
        Hbf = sb("Hbf", [128, MP], BF16)
        al = sb("al", [1, 128], F32)
        prevs = sb("prevs", [1, 1], F32)
        negalT = sb("negalT", [128, 1], F32)
        eac = sb("eac", [128, 1], F32)
        dtn = sb("dtn", [128, 1], F32)
        eeh = sb("eeh", [1, 1], F32)
        ebc = sb("ebc", [128, 1], F32)
        BCm = sb("BCm", [128, 128], F32)
        Bnat = sb("Bnat", [128, 128], BF16)
        dec_sb = sb("dec_sb", [128, 128], F32)
        dec2 = sb("dec2", [128, 128], F32)
        Mt = sb("Mt", [128, 128], BF16)
        xnat = sb("xnat", [128, MP], BF16)
        yv = sb("yv", [128, MP], F32)
        y2 = sb("y2", [128, MP], F32)
        dxt = sb("dxt", [128, MP], F32)
        Ssb = sb("Ssb", [128, MP], F32)
        yst = sb("yst", [128, 256], BF16)
        psA = ps("psA", [128, 128], F32)
        psB = ps("psB", [128, 128], F32)
        ptrb = ps("ptrb", [128, 128], BF16)
        ptf = ps("ptf", [128, 128], F32)
        psY = ps("psY", [128, MP], F32)
        psYi = ps("psYi", [128, MP], F32)
        psS = ps("psS", [128, MP], F32)
        psE = ps("psE", [128, 1], F32)

        ser.dma(lambda e: e.dma_start(identB[:, :], HD["identb"].ap()))
        ser.dma(lambda e: e.dma_start(idb2[:, :], HD["idb2d"].ap()))
        ser.dma(lambda e: e.dma_start(ut[:, :], HD["ut01"].ap()))
        ser.dma(lambda e: e.dma_start(eye4[:, :], HD["eye4d"].ap()))
        ser.v(lambda e: e.memset(ones_rowf[:, :], 1.0))

        conv_src = HD["xsB_d"]
        conv_dst = [xsc[0], xsc[1], btc, ctc]
        for j in range(4):
            ser.v(lambda e: e.memset(u_t[:, :3], 0.0))
            ser.dma(lambda e, j=j: e.dma_start(u_t[:, 3:], conv_src[j].ap()))
            ser.dma(lambda e, j=j: e.dma_start(wcol[:, :], HD["cw"][j].ap()))
            ser.dma(lambda e, j=j: e.dma_start(
                bcolE[:, :], dap(HD["cb"][j], 0, [[1, 128], [1, 1]])))
            for q in range(4):
                q0 = q * 2048
                ser.v(lambda e, q0=q0: e.tensor_scalar_mul(
                    acc[:, :], u_t[:, q0:q0 + 2048], wcol[:, 0:1]))
                for kk in range(1, MK):
                    ser.v(lambda e, q0=q0, kk=kk: e.scalar_tensor_tensor(
                        acc[:, :], u_t[:, q0 + kk:q0 + kk + 2048],
                        wcol[:, kk:kk + 1], acc[:, :], ALU.mult, ALU.add))
                ser.s(lambda e, j=j, q0=q0: e.activation(
                    conv_dst[j][:, q0:q0 + 2048], acc[:, :], AF.Silu,
                    bias=bcolE[:, :]))

        ser.dma(lambda e: e.dma_start(prow[:, :], dap(HD["dtb"], 0, [[4, 1], [1, 4]])))
        ser.dma(lambda e: e.dma_start(arow[:, :], dap(HD["alog"], 0, [[4, 1], [1, 4]])))
        ser.dma(lambda e: e.dma_start(drow[:, :], dap(HD["dpar"], 0, [[4, 1], [1, 4]])))
        ser.s(lambda e: e.activation(arow[:, :], arow[:, :], AF.Exp))
        ser.v(lambda e: e.tensor_scalar_mul(arow[:, :], arow[:, :], -1.0))
        ser.t(lambda e: e.matmul(psE[:4, :1], arow[:, :], ones_rowf[:1, :1],
                                 start=True, stop=True))
        ser.v(lambda e: e.tensor_copy(acol[:, :], psE[:4, :1]))
        ser.t(lambda e: e.matmul(psE[:4, :1], prow[:, :], ones_rowf[:1, :1],
                                 start=True, stop=True))
        ser.v(lambda e: e.tensor_copy(bcol4[:, :], psE[:4, :1]))
        ser.v(lambda e: e.tensor_scalar_add(dt4[:, :], dt4[:, :], bcol4[:, :]))
        ser.s(lambda e: e.activation(scr4[:, :], dt4[:, :], AF.Abs))
        ser.s(lambda e: e.activation(scr4[:, :], scr4[:, :], AF.Exp, scale=-1.0))
        ser.s(lambda e: e.activation(scr4[:, :], scr4[:, :], AF.Ln, bias=1.0))
        ser.v(lambda e: e.tensor_scalar_max(dt4[:, :], dt4[:, :], 0.0))
        ser.v(lambda e: e.tensor_add(dt4[:, :], dt4[:, :], scr4[:, :]))
        ser.v(lambda e: e.tensor_scalar_mul(scr4[:, :], dt4[:, :], acol[:, :]))
        ser.v(lambda e: e.tensor_tensor_scan(ag4[:, :], scr4[:, :], scr4[:, :],
                                             0.0, ALU.add, ALU.bypass))
        for h in range(4):
            ser.t(lambda e, h=h: e.matmul(psE[:, :], ones_rowf[:, :],
                                          drow[:1, h:h + 1], start=True, stop=True))
            ser.v(lambda e, h=h: e.tensor_copy(dcol[h][:, :], psE[:, :]))
            ser.v(lambda e, h=h: e.memset(H[h][0][:, :], 0.0))

        for t in range(NCHUNK):
            t0 = t * 128
            cur, nxt = t % 2, (t + 1) % 2
            ser.t(lambda e, t0=t0: e.matmul(psA[:, :], btc[:, t0:t0 + 128],
                                            ctc[:, t0:t0 + 128], start=True, stop=True))
            ser.v(lambda e: e.tensor_mul(BCm[:, :], psA[:, :], ut[:, :]))
            ser.t(lambda e, t0=t0: e.transpose(ptrb[:, :], btc[:, t0:t0 + 128],
                                               identB[:, :]))
            ser.v(lambda e: e.tensor_copy(Bnat[:, :], ptrb[:, :]))
            for h in range(4):
                c0 = h // 2
                p0 = (h % 2) * 64
                if t > 0:
                    ser.t(lambda e, h=h, t0=t0: e.matmul(
                        ptf[:1, :1], eye4[:, h:h + 1], ag4[:4, t0 - 1:t0],
                        start=True, stop=True))
                    ser.v(lambda e: e.tensor_copy(prevs[:, :], ptf[:1, :1]))
                ser.t(lambda e, h=h, t0=t0: e.matmul(
                    ptf[:1, :128], eye4[:, h:h + 1], ag4[:4, t0:t0 + 128],
                    start=True, stop=True))
                if t > 0:
                    ser.v(lambda e: e.tensor_scalar_sub(al[:, :], ptf[:1, :128],
                                                        prevs[:1, :1]))
                else:
                    ser.v(lambda e: e.tensor_copy(al[:, :], ptf[:1, :128]))
                ser.t(lambda e: e.matmul(ptf[:, :1], al[:, :], ones_rowf[:1, :1],
                                         start=True, stop=True))
                ser.s(lambda e: e.activation(eac[:, :], ptf[:, :1], AF.Exp))
                ser.v(lambda e: e.tensor_scalar_mul(negalT[:, :], ptf[:, :1], -1.0))
                ser.t(lambda e, h=h, t0=t0: e.matmul(
                    ptf[:, :1], dt4[:4, t0:t0 + 128], eye4[:, h:h + 1],
                    start=True, stop=True))
                ser.v(lambda e: e.tensor_copy(dtn[:, :], ptf[:, :1]))
                ser.t(lambda e: e.matmul(psB[:, :], ones_rowf[:, :], al[:, :],
                                         start=True, stop=True))
                ser.v(lambda e: e.tensor_scalar(dec_sb[:, :], psB[:, :],
                                                negalT[:, :], 0.0,
                                                ALU.add, ALU.min))
                ser.s(lambda e: e.activation(dec_sb[:, :], dec_sb[:, :], AF.Exp))
                ser.v(lambda e: e.tensor_scalar_mul(dec2[:, :], dec_sb[:, :],
                                                    dtn[:, :]))
                ser.v(lambda e: e.tensor_mul(Mt[:, :], dec2[:, :], BCm[:, :]))
                idap = (lambda e: identB[:64, :64]) if p0 == 0 else (
                    lambda e: idb2[64:128, :64])
                ser.t(lambda e, c0=c0, p0=p0, t0=t0, idap=idap: e.transpose(
                    ptrb[:, :MP], xsc[c0][p0:p0 + 64, t0:t0 + 128], idap(e)))
                ser.v(lambda e: e.tensor_copy(xnat[:, :], ptrb[:, :MP]))
                ser.t(lambda e: e.matmul(psY[:, :], Mt[:, :], xnat[:, :],
                                         start=True, stop=True))
                ser.v(lambda e, h=h, cur=cur: e.tensor_copy(Hbf[:, :],
                                                            H[h][cur][:, :]))
                ser.t(lambda e, t0=t0: e.matmul(
                    psYi[:, :], ctc[:, t0:t0 + 128], Hbf[:, :],
                    start=True, stop=True))
                ser.v(lambda e: e.tensor_scalar_mul(yv[:, :], psYi[:, :], eac[:, :]))
                ser.v(lambda e: e.tensor_add(y2[:, :], psY[:, :], yv[:, :]))
                ser.v(lambda e, h=h: e.tensor_scalar_mul(dxt[:, :], xnat[:, :],
                                                         dcol[h][:, :]))
                ser.v(lambda e, h=h: e.tensor_add(
                    yst[:, h * MP:(h + 1) * MP], dxt[:, :], y2[:, :]))
                ser.v(lambda e: e.tensor_scalar_mul(Mt[:, :], Bnat[:, :],
                                                    dec2[:, 127:128]))
                ser.t(lambda e: e.matmul(psS[:, :], Mt[:, :], xnat[:, :],
                                         start=True, stop=True))
                ser.v(lambda e: e.tensor_copy(Ssb[:, :], psS[:, :]))
                ser.s(lambda e: e.activation(eeh[:, :], al[:1, 127:128], AF.Exp))
                ser.t(lambda e: e.matmul(psE[:, :], ones_rowf[:, :], eeh[:, :],
                                         start=True, stop=True))
                ser.v(lambda e: e.tensor_copy(ebc[:, :], psE[:, :]))
                ser.v(lambda e, h=h, cur=cur, nxt=nxt: e.scalar_tensor_tensor(
                    H[h][nxt][:, :], H[h][cur][:, :], ebc[:, :], Ssb[:, :],
                    ALU.mult, ALU.add))
            ser.dma(lambda e, t0=t0: e.dma_start(
                dap(HD["cin_y"], t0 * 256, [[256, 128], [1, 256]]), yst[:, :]),
                eng="gpsimd")
        ser.flush(nc, tok)




def emit_final(nc, ser, H, identP, ones_rowfP):
    tok = H["tok"]
    NT = SC // 128
    GC = float(np.sqrt(2.0 / np.pi))
    with ExitStack() as st:
        def sb(name, shape, dtype):
            return st.enter_context(nc.sbuf_tensor(name, shape, dtype))

        def ps(name, shape, dtype):
            return st.enter_context(nc.psum_tensor(name, shape, dtype))

        gnT = sb("gnT", [128, 16 * SC], BF16)
        xn1T = sb("xn1T_sbF", [128, 16 * SC], BF16)
        hid = sb("hid", [128, 64 * 256], BF16)
        fT = sb("fT", [128, 16 * 256], BF16)
        wch = sb("wchF", [128, 2048], BF16)
        wch2 = sb("wch2F", [128, 2048], BF16)
        wch64 = sb("wch64", [128, 64 * 128], BF16)
        nwb = sb("nwb", [128, 2048], F32)
        nrow = sb("nrow", [1, 2048], F32)
        ones_rowf = sb("ones_rowfF", [1, 128], F32)
        ybf = sb("ybf", [128, 2048], BF16)
        gbf = sb("gbf", [128, 2048], BF16)
        sgate = sb("sgate", [128, 2048], BF16)
        g_f = sb("g_f", [128, 2048], F32)
        scr = sb("scrF", [128, 2048], F32)
        gnw = sb("gnw", [128, 2048], BF16)
        ident = sb("identF_", [128, 128], BF16)
        sscol = sb("sscol", [128, 1], F32)
        x2s = sb("x2s", [128, 256], F32)
        x3s = sb("x3s", [128, 256], F32)
        inn = sb("inn", [128, 256], F32)
        ths = sb("ths", [128, 256], F32)
        t3s = sb("t3s", [128, 256], F32)
        ot = sb("ot", [128, 2048], F32)
        piece = sb("pieceF", [128, 128], BF16)
        pg0 = ps("pg0", [128, 512], F32)
        pg1 = ps("pg1", [128, 512], F32)
        ptr = ps("ptrF", [128, 128], BF16)
        pbc = ps("pbcF", [128, 512], F32)

        ser.dma(lambda e: e.dma_start(ident[:, :], H["identb"].ap()))
        ser.v(lambda e: e.memset(ones_rowf[:, :], 1.0))
        ser.dma(lambda e: e.dma_start(nrow[:, :], dap(H["nrm"], 0, [[DM, 1], [1, DM]])))
        for q in range(4):
            ser.t(lambda e, q=q: e.matmul(pbc[:, :], ones_rowf[:, :],
                                          nrow[:1, q * 512:(q + 1) * 512],
                                          start=True, stop=True))
            ser.v(lambda e, q=q: e.tensor_copy(nwb[:, q * 512:(q + 1) * 512],
                                               pbc[:, :]))
        ser.dma(lambda e: e.dma_start(
            xn1T[:, :], dap(H["cin_xn1T"], 0, [[SC, 128], [128 * SC, 16], [1, SC]])))

        for i in range(NT):
            ser.dma(lambda e, i=i: e.dma_start(
                ybf[:, :], dap(H["y_own"], i * 128 * M_INTER, [[M_INTER, 128], [1, M_INTER]])))
            ser.dma(lambda e, i=i: e.dma_start(
                gbf[:, :], dap(H["gate_d"], i * 128 * M_INTER, [[M_INTER, 128], [1, M_INTER]])))
            ser.s(lambda e: e.activation(sgate[:, :], gbf[:, :], AF.Silu))
            ser.v(lambda e: e.tensor_mul(g_f[:, :], ybf[:, :], sgate[:, :]))
            ser.s(lambda e: e.activation(scr[:, :], g_f[:, :], AF.Square,
                                         accum_out=sscol[:, :]))
            ser.v(lambda e: e.tensor_scalar(sscol[:, :], sscol[:, :], 1.0 / DM,
                                            EPS, ALU.mult, ALU.add))
            ser.v(lambda e: e.reciprocal(sscol[:, :], sscol[:, :]))
            ser.s(lambda e: e.activation(sscol[:, :], sscol[:, :], AF.Sqrt))
            ser.v(lambda e: e.tensor_scalar_mul(g_f[:, :], g_f[:, :], sscol[:, :]))
            ser.v(lambda e: e.tensor_mul(gnw[:, :], g_f[:, :], nwb[:, :]))
            for f in range(16):
                ser.t(lambda e, f=f: e.transpose(
                    ptr[:, :], gnw[:, f * 128:(f + 1) * 128], ident[:, :]))
                ser.v(lambda e, f=f, i=i: e.tensor_copy(
                    gnT[:, f * SC + i * 128: f * SC + (i + 1) * 128], ptr[:, :]))
        for fo in range(16):
            ser.dma(lambda e, fo=fo: e.dma_start(
                wch[:, :], dap(H["woutT"], fo * 128, [[DM, 128], [128 * DM, 16], [1, 128]])))
            for sh in range(2):
                for fi in range(16):
                    ser.t(lambda e, fi=fi, sh=sh: e.matmul(
                        pg0[:, :], wch[:, fi * 128:(fi + 1) * 128],
                        gnT[:, fi * SC + sh * 512: fi * SC + (sh + 1) * 512],
                        start=(fi == 0), stop=(fi == 15)), acc=(fi > 0))
                ser.v(lambda e, sh=sh: e.tensor_copy(
                    ybf[:, sh * 512:(sh + 1) * 512], pg0[:, :]))
            ser.dma(lambda e, fo=fo: e.dma_start(
                dap(H["moT_h"], fo * 128 * SC, [[SC, 128], [1, SC]]),
                ybf[:, :1024]), eng="gpsimd")

        for qf in range(4):
            s0 = qf * 256
            for fo in range(64):
                ser.dma(lambda e, fo=fo: e.dma_start(
                    wch[:, :], dap(H["wg0"], fo * 128,
                                   [[FF1, 128], [128 * FF1, 16], [1, 128]])))
                ser.dma(lambda e, fo=fo: e.dma_start(
                    wch2[:, :], dap(H["wg1"], fo * 128,
                                    [[FF1, 128], [128 * FF1, 16], [1, 128]])))
                for fi in range(16):
                    ser.t(lambda e, fi=fi, s0=s0: e.matmul(
                        pg0[:, :256], wch[:, fi * 128:(fi + 1) * 128],
                        xn1T[:, fi * SC + s0: fi * SC + s0 + 256],
                        start=(fi == 0), stop=(fi == 15)), acc=(fi > 0))
                for fi in range(16):
                    ser.t(lambda e, fi=fi, s0=s0: e.matmul(
                        pg1[:, :256], wch2[:, fi * 128:(fi + 1) * 128],
                        xn1T[:, fi * SC + s0: fi * SC + s0 + 256],
                        start=(fi == 0), stop=(fi == 15)), acc=(fi > 0))
                ser.s(lambda e: e.activation(x2s[:, :], pg0[:, :256], AF.Square))
                ser.v(lambda e: e.tensor_mul(x3s[:, :], x2s[:, :], pg0[:, :256]))
                ser.v(lambda e: e.scalar_tensor_tensor(
                    inn[:, :], x3s[:, :], 0.044715, pg0[:, :256], ALU.mult, ALU.add))
                ser.s(lambda e: e.activation(ths[:, :], inn[:, :], AF.Tanh, scale=GC))
                ser.v(lambda e: e.tensor_scalar(ths[:, :], ths[:, :], 1.0, 0.5,
                                                ALU.add, ALU.mult))
                ser.v(lambda e: e.tensor_mul(t3s[:, :], ths[:, :], pg0[:, :256]))
                ser.v(lambda e, fo=fo: e.tensor_mul(
                    hid[:, fo * 256:(fo + 1) * 256], t3s[:, :], pg1[:, :256]))
            for do in range(16):
                ser.dma(lambda e, do=do: e.dma_start(
                    wch64[:, :], dap(H["wlin"], do * 128,
                                     [[DM, 128], [128 * DM, 64], [1, 128]])))
                for fi in range(64):
                    ser.t(lambda e, fi=fi: e.matmul(
                        pg0[:, :256], wch64[:, fi * 128:(fi + 1) * 128],
                        hid[:, fi * 256:(fi + 1) * 256],
                        start=(fi == 0), stop=(fi == 63)), acc=(fi > 0))
                ser.v(lambda e, do=do: e.tensor_copy(
                    fT[:, do * 256:(do + 1) * 256], pg0[:, :256]))
            for ii in range(2):
                ig = qf * 2 + ii
                ser.dma(lambda e, ig=ig: e.dma_start(
                    ot[:, :], dap(H["x1_d"], ig * 128 * DM, [[DM, 128], [1, DM]])))
                for f in range(16):
                    ser.dma(lambda e, ig=ig, f=f: e.dma_start(
                        piece[:, :], dap(H["moT_h"], f * 128 * SC + ig * 128,
                                         [[SC, 128], [1, 128]])))
                    ser.t(lambda e: e.transpose(ptr[:, :], piece[:, :], ident[:, :]))
                    ser.v(lambda e, f=f: e.tensor_add(
                        ot[:, f * 128:(f + 1) * 128], ot[:, f * 128:(f + 1) * 128],
                        ptr[:, :]))
                    ser.t(lambda e, f=f, ii=ii: e.transpose(
                        ptr[:, :], fT[:, f * 256 + ii * 128: f * 256 + (ii + 1) * 128],
                        ident[:, :]))
                    ser.v(lambda e, f=f: e.tensor_add(
                        ot[:, f * 128:(f + 1) * 128], ot[:, f * 128:(f + 1) * 128],
                        ptr[:, :]))
                ser.dma(lambda e, ig=ig: e.dma_start(
                    dap(H["o_out"], ig * 128 * DM, [[DM, 128], [1, DM]]), ot[:, :]),
                    eng="gpsimd")
        ser.flush(nc, tok)




def kernel_fused(**inp):
    g = {k: np.asarray(v) for k, v in inp.items()}
    x0 = f32(g["L0_attn_hidden_states"].reshape(S, DM))
    wqkv = g["L0_attn_Wqkv_w"]
    b_all = f32(g["L0_attn_Wqkv_b"])
    identb_np = bf(np.eye(128))
    masksB = np.zeros((128, 4 * 512), np.float32)
    for off in range(4):
        for kj in range(128):
            masksB[kj, off * 512 + off * 128 + kj:(off + 1) * 512] = 1.0
    cwr = f32(g["L1_attn_conv_w"]).reshape(18, 128, MK)
    cbr = f32(g["L1_attn_conv_b"]).reshape(18, 128)
    ut01 = np.ascontiguousarray(np.triu(np.ones((128, 128), np.float32)))
    idb2 = np.zeros((128, 64), np.float32)
    for i in range(64):
        idb2[64 + i, i] = 1.0
    win = np.asarray(g["L1_attn_W_in"])

    ncF = build_FUSED()
    im = []
    for c in range(NC):
        gq = c // 2
        wq1 = wqkv[c * 256:c * 256 + 128]
        wq2 = wqkv[c * 256 + 128:c * 256 + 256]
        wk1 = wqkv[2048 + gq * 256:2048 + gq * 256 + 128]
        wk2 = wqkv[2048 + gq * 256 + 128:2048 + gq * 256 + 256]
        wqk_sl = bf(np.concatenate([wq1, wq2, wk1, wk2], axis=0).T)
        bqk_sl = f32(np.concatenate([
            b_all[c * 256:c * 256 + 256],
            b_all[2048 + gq * 256:2048 + gq * 256 + 256]]))
        wv_sl = bf(wqkv[3072 + gq * 256:3072 + (gq + 1) * 256].T)
        bv_sl = f32(b_all[3072 + gq * 256:3072 + (gq + 1) * 256])
        win_rows = np.concatenate([
            win[M_INTER + c * 256: M_INTER + (c + 1) * 256],
            win[M_INTER + 2048:M_INTER + 2048 + 128],
            win[M_INTER + 2048 + 128:M_INTER + 2048 + 256],
            win[M_INTER + CONV_DIM + 4 * c: M_INTER + CONV_DIM + 4 * c + 4]],
            axis=0)
        win_sl = bf(win_rows.T)
        offs_np = np.array([[c * SC, c * SC * 256, 0, 0]], np.int32)
        im.append({
            "x0f": x0, "x0s": x0[c * SC:(c + 1) * SC], "offs": offs_np,
            "wqk_sl": wqk_sl, "bqk_sl": bqk_sl, "wv_sl": wv_sl, "bv_sl": bv_sl,
            "lq1": f32(g["L0_attn_lambda_q1"]), "lk1": f32(g["L0_attn_lambda_k1"]),
            "lq2": f32(g["L0_attn_lambda_q2"]), "lk2": f32(g["L0_attn_lambda_k2"]),
            "subln": f32(g["L0_attn_subln_w"]).reshape(2, 128),
            "masksB": bf(masksB),
            "woT": bf(g["L0_attn_out_w"].T), "outb": f32(g["L0_attn_out_b"]),
            "gwT": bf(g["L0_ffn_gate_w"].T), "uwT": bf(g["L0_ffn_up_w"].T),
            "gs": f32(g["L0_ffn_gate_s"]), "us": f32(g["L0_ffn_up_s"]),
            "dwT": bf(g["L0_ffn_down_w"].T), "ds": f32(g["L0_ffn_down_s"]),
            "winT": bf(win.T), "win_sl": win_sl,
            "cw0": cwr[2 * c], "cw1": cwr[2 * c + 1], "cw2": cwr[16], "cw3": cwr[17],
            "cb0": cbr[2 * c], "cb1": cbr[2 * c + 1], "cb2": cbr[16], "cb3": cbr[17],
            "dtb": f32(g["L1_attn_dt_bias"][4 * c:4 * c + 4]),
            "alog": f32(g["L1_attn_A_log"][4 * c:4 * c + 4]),
            "dpar": f32(g["L1_attn_D_param"][4 * c:4 * c + 4]),
            "ut01": ut01, "eye4d": np.eye(4, dtype=np.float32),
            "identb": identb_np, "idb2d": bf(idb2),
            "nrm": f32(g["L1_attn_norm_w"]),
            "woutT": bf(g["L1_attn_W_out"].T),
            "wg0": bf(g["L1_ffn_w_gating"][0]), "wg1": bf(g["L1_ffn_w_gating"][1]),
            "wlin": bf(g["L1_ffn_w_linear"]),
        })
    r = run_bass_kernel_spmd(ncF, im, core_ids=list(range(NC))).results
    out = np.concatenate([r[c]["out"] for c in range(NC)], axis=0)
    return out.reshape(1, S, DM).astype(np.float32)


kernel = kernel_fused
